# version 08
# baseline (speedup 1.0000x reference)
import numpy as np
import concourse.bass as bass
import concourse.mybir as mybir
from concourse.bass_utils import run_bass_kernel_spmd
from concourse.aot_env import allow_ici_all_to_all

F32 = mybir.dt.float32
BF16 = mybir.dt.bfloat16
F16 = mybir.dt.float16
AF = mybir.ActivationFunctionType
ALU = mybir.AluOpType
AX = mybir.AxisListType

FULL_CFG = dict(D=2048, H=8192, L=8192, N=16, R=128, C=8)


class Ctr:
    def __init__(self, handle):
        self.h = handle
        self.n = 0


def build_nc(cfg):
    stop_after = cfg.get("stop_after", "full")
    D, H, L, N, R, C = cfg["D"], cfg["H"], cfg["L"], cfg["N"], cfg["R"], cfg["C"]
    T = L // C
    ds = D // C
    nds = ds // 128
    ndc = D // 128
    NB = min(512, T)
    ntb = T // NB
    HRS = min(256, H)
    nhr = H // HRS
    nht_r = HRS // 128
    nh = H // 128
    OQ = min(512, D)
    noq = D // OQ
    no_t = OQ // 128
    TC = min(2048, L)
    nch = L // TC
    nblk_c = max(1, TC // T)
    ntt = T // 128
    assert TC % T == 0 and R == 128 and ds % 128 == 0

    nc = bass.Bass("TRN2", target_bir_lowering=False, detect_race_conditions=False)

    x_tok = nc.dram_tensor("x_tok", [T, D], F32, kind="ExternalInput").ap()
    x_ownT = nc.dram_tensor("x_ownT", [D, T], F32, kind="ExternalInput").ap()
    x_dT = nc.dram_tensor("x_dT", [ds, L], F32, kind="ExternalInput").ap()
    xpwT = nc.dram_tensor("xpwT", [D, 160], F32, kind="ExternalInput").ap()
    dtpwT = nc.dram_tensor("dtpwT", [R, ds], F32, kind="ExternalInput").ap()
    dtb = nc.dram_tensor("dtb", [128, nds], F32, kind="ExternalInput").ap()
    alog = nc.dram_tensor("alog", [128, nds * N], F32, kind="ExternalInput").ap()
    dp = nc.dram_tensor("dp", [128, D // 128], F32, kind="ExternalInput").ap()
    wgT = nc.dram_tensor("wgT", [D, H], F32, kind="ExternalInput").ap()
    wuT = nc.dram_tensor("wuT", [D, H], F32, kind="ExternalInput").ap()
    wdT = nc.dram_tensor("wdT", [H, D], F32, kind="ExternalInput").ap()
    out_tok = nc.dram_tensor("out_tok", [T, D], F32, kind="ExternalOutput").ap()

    scr_tv_t = nc.dram_tensor("scr_tv", [8, T], F32, kind="Internal")
    aga_in_t = nc.dram_tensor("aga_in", [160, T], BF16, kind="Internal")
    aga_out_t = nc.dram_tensor("aga_out", [160 * C, T], BF16, kind="Internal")
    agb_out_t = nc.dram_tensor("agb_out", [2 * C, T], F32, kind="Internal")
    scr_dtT_t = nc.dram_tensor("scr_dtT", [ds, L], F32, kind="Internal")
    p_scr_t = nc.dram_tensor("p_scr", [H, T], BF16, kind="Internal")
    ffn_scr_t = nc.dram_tensor("ffn_scr", [D, T], BF16, kind="Internal")
    y_in_t = nc.dram_tensor("y_a2a_in", [C * ds, T], BF16, kind="Internal")
    y_out_t = nc.dram_tensor("y_a2a_out", [C * ds, T], BF16, kind="Internal")
    scr_tv = scr_tv_t.ap()
    aga_in = aga_in_t.ap()
    aga_out = aga_out_t.ap()
    agb_out = agb_out_t.ap()
    scr_dtT = scr_dtT_t.ap()
    p_scr = p_scr_t.ap()
    ffn_scr = ffn_scr_t.ap()
    y_in = y_in_t.ap()
    y_out = y_out_t.ap()

    A = nc.alloc_sbuf_tensor
    AAT = nc.alloc_sbuf_tensor_at

    def addr_of(h):
        return nc.lookup_mloc(h).addr

    sb_xtok = A("sb_xtok", [128, D], F32)
    sb_sq = A("sb_sq", [128, D], F32)
    sb_sq2 = A("sb_sq2", [128, D], F32)
    sb_p0a = A("sb_p0a", [128, 3072], F32)
    sb_tv = A("sb_tv", [128, 8], F32)
    sb_red = A("sb_red", [128, 12], F32)
    sb_xod = A("sb_xod", [128, T], F32)
    sb_xnTf = A("sb_xnTf", [128, T], F32)
    sb_xnb = [A(f"sb_xnb{i}", [128, T], BF16) for i in range(2)]
    sb_kT = A("sb_kT", [128, ndc * T], BF16)
    sb_mb = A("sb_mb", [128, T], F32)
    sb_rb = A("sb_rb", [128, T], F32)
    sb_m2b = A("sb_m2b", [128, T], F32)
    sb_r2b = A("sb_r2b", [128, T], F32)
    sb_ssb = A("sb_ssb", [128, T], F32)
    sb_iv2 = A("sb_iv2", [128, T], BF16)
    sb_xpw = A("sb_xpw", [128, ndc * 160], BF16)
    sb_dtw = A("sb_dtw", [128, ds], BF16)
    sb_ag2 = A("sb_ag2", [32, T], BF16)
    sb_dtbc = A("sb_dtbc", [128, nds], F32)
    sb_acol = A("sb_acol", [128, nds * N], F32)
    sb_wf = [A(f"sb_wf{i}", [128, HRS], F32) for i in range(2)]
    sb_sgn_g = [A(f"sb_sgn_g{i}", [128, ndc * HRS], BF16) for i in range(2)]
    sb_sgn_u = [A(f"sb_sgn_u{i}", [128, ndc * HRS], BF16) for i in range(2)]
    sb_q = [A(f"sb_q{i}", [128, NB], BF16) for i in range(2)]
    sb_evg = A("sb_evg", [128, NB], BF16)
    sb_evu = A("sb_evu", [128, NB], BF16)
    sb_ev1 = A("sb_ev1", [128, NB], BF16)
    sb_ev2 = A("sb_ev2", [128, NB], BF16)
    sb_ev3 = A("sb_ev3", [128, NB], BF16)
    sb_ev4 = A("sb_ev4", [128, NB], BF16)
    sb_dtT = A("sb_dtT", [128, TC], F32)
    sb_xdc = A("sb_xdc", [128, TC], F32)
    sb_h = A("sb_h", [128, TC], BF16)
    sb_bcb = [A(f"sb_bcb{i}", [128, TC], BF16) for i in range(2)]
    sb_bcc = [A(f"sb_bcc{i}", [128, TC], BF16) for i in range(2)]
    sb_hst = A("sb_hst", [128, max(1, nds * N)], F32)
    sb_ytmp = A("sb_ytmp", [128, TC], BF16)
    sb_yacc = A("sb_yacc", [128, TC], BF16)
    sb_fs = [A(f"sb_fs{i}", [128, NB], BF16) for i in range(2)]
    sb_dpb = A("sb_dpb", [128, ndc], F32)
    sb_cst = A("sb_cst", [128, 4], F32)
    a_xtok = addr_of(sb_xtok)
    sb_pt = [AAT(f"sb_pt{i}", [128, 2 * T], BF16, offset=a_xtok + i * (4 * T)) for i in range(2)]
    a_sq = addr_of(sb_sq)
    sb_wdf = [AAT(f"sb_wdf{i}", [128, 2 * OQ], F32, offset=a_sq + i * (8 * OQ)) for i in range(2)]
    sb_wds = None
    a_sq2 = addr_of(sb_sq2)
    sb_mab = AAT("sb_mab", [128, TC], BF16, offset=a_sq2)
    sb_rab = AAT("sb_rab", [128, TC], BF16, offset=a_sq2 + 2 * TC)
    a_p0a = addr_of(sb_p0a)
    sb_sp1 = AAT("sb_sp1", [128, NB], F32, offset=a_p0a)
    sb_sp2 = AAT("sb_sp2", [128, NB], F32, offset=a_p0a + 4 * NB)
    sb_sp3 = AAT("sb_sp3", [128, NB], F32, offset=a_p0a + 8 * NB)
    sb_f16 = AAT("sb_f16", [128, T], F16, offset=a_p0a + 12 * NB)
    sb_dlow = AAT("sb_dlow", [128, T], BF16, offset=a_p0a + 12 * NB + 2 * T)
    sb_ag1 = AAT("sb_ag1", [128, T], BF16, offset=a_p0a + 12 * NB + 4 * T)
    sb_dtx = AAT("sb_dtx", [128, TC], BF16, offset=a_p0a)
    sb_ab3 = None
    sb_bx = AAT("sb_bx", [128, TC], BF16, offset=a_p0a + 4 * TC)
    a_dtT = addr_of(sb_dtT)
    sb_acc = AAT("sb_acc", [128, T], F32, offset=a_dtT)
    sb_ycol = AAT("sb_ycol", [128, T], BF16, offset=a_dtT + 4 * T)
    sb_fload = AAT("sb_fload", [128, T], BF16, offset=a_dtT + 6 * T)
    a_xdc = addr_of(sb_xdc)
    sb_ot = [AAT(f"sb_ot{i}", [128, 128], F32, offset=a_xdc + i * 512) for i in range(2)]
    sb_ident = AAT("sb_ident", [128, 128], F32, offset=a_xdc + 1024)
    def _fit(name, shape, dtype, host, host_bytes):
        nbytes = shape[1] * (2 if dtype in (BF16, F16) else 4)
        if nbytes <= host_bytes:
            return AAT(name, shape, dtype, offset=addr_of(host))
        return A(name, shape, dtype)
    sb_ab3 = [AAT("sb_ab0", [128, TC], BF16, offset=a_p0a + 2 * TC),
              _fit("sb_ab1", [128, TC], BF16, sb_xod, 4 * T),
              _fit("sb_ab2", [128, TC], BF16, sb_xnTf, 4 * T)]
    sb_wds = [_fit(f"sb_wds{i}", [128, 2 * OQ], BF16, sb_xnb[i], 2 * T) for i in range(2)]
    ps = [nc.alloc_psum_tensor(f"ps{i}", [128, 512], F32) for i in range(8)]

    prog = {e: [] for e in ["sync", "gpsimd", "tensor", "vector", "scalar"]}
    ctrs = {}
    sem_keep = []

    def S(name):
        c = ctrs.get(name)
        if c is None:
            g = nc.semaphore(name)
            h = g.__enter__()
            sem_keep.append(g)
            c = Ctr(h)
            ctrs[name] = c
        return c

    self_ctr = {}

    def op(e, fn, waits=(), inc=None, k=1):
        own = self_ctr.get(e)
        wl = [(c.h, v) for (c, v) in waits if v > 0 and c is not own]
        dd = {}
        for h, v in wl:
            dd[h] = max(dd.get(h, 0), v)
        wl = list(dd.items())
        if inc is not None:
            inc.n += k

        def run(eng, fn=fn, wl=wl, inc=inc, k=k):
            for h, v in wl:
                eng.wait_ge(h, v)
            ins = fn(eng)
            if inc is not None:
                ins.then_inc(inc.h, k)
        prog[e].append(run)

    sv = S("sv")
    ss = S("ss")
    stt = S("stt")
    sgp = S("sgp")
    scc_b = S("scc_b")
    scc_a = S("scc_a")
    scc_y = S("scc_y")
    s_in = S("s_in")
    s_x3 = S("s_x3")
    s_tv = S("s_tv")
    s_scr = S("s_scr")
    s_bco = S("s_bco")
    s_ag = S("s_ag")
    s_dts = S("s_dts")
    s_dtl = S("s_dtl")
    s_bcs = [S("s_bc0"), S("s_bc1")]
    s_mrb = S("s_mrb")
    s_wf = [S("s_wf0"), S("s_wf1")]
    s_p = [S("s_p0"), S("s_p1")]
    s_pl = [S("s_pl0"), S("s_pl1")]
    s_wd = [S("s_wd0"), S("s_wd1")]
    s_y = S("s_y")
    s_fsp = [S("s_fsp0"), S("s_fsp1")]
    fs_slot_ms = [0, 0]
    s_out = S("s_out")

    self_ctr.update({"tensor": stt})

    def VOP(fn, waits=()):
        op("vector", fn, waits=list(waits) + [(sv, sv.n)], inc=sv)

    def SOP(fn, waits=()):
        op("scalar", fn, waits=list(waits) + [(ss, ss.n)], inc=ss)

    def GOP(fn, waits=()):
        op("gpsimd", fn, waits=list(waits) + [(sgp, sgp.n)], inc=sgp)

    def DMA(e, out_ap, in_ap, ctr, waits=()):
        op(e, lambda eng, o=out_ap, i=in_ap: eng.dma_start(out=o, in_=i),
           waits=waits, inc=ctr, k=16)

    DMA("sync", sb_dtbc[:, :], dtb, s_in)
    DMA("sync", sb_acol[:, :], alog, s_in)
    DMA("sync", sb_dpb[:, :], dp, s_in)
    DMA("sync", sb_sq[0:R, 0:ds], dtpwT, s_in)
    in0 = s_in.n
    VOP(lambda e: e.tensor_copy(sb_dtw[0:R, :], sb_sq[0:R, 0:ds]), waits=[(s_in, in0)])
    v_dtw = sv.n
    SOP(lambda e: e.activation(sb_acol[:, :], sb_acol[:, :], AF.Exp, bias=0.0, scale=1.0),
        waits=[(s_in, in0)])
    VOP(lambda e: e.tensor_scalar_mul(sb_acol[:, :], sb_acol[:, :], -1.0), waits=[(ss, ss.n)])
    v_acol = sv.n
    GOP(lambda e: e.memset(sb_cst[:, 0:1], 1e-5))
    GOP(lambda e: e.memset(sb_cst[:, 1:2], 1e-8))
    GOP(lambda e: e.memset(sb_cst[:, 2:3], 1.0))
    gp_cst = sgp.n
    GOP(lambda e: e.memset(sb_hst[:, :], 0.0))
    gp_hst0 = sgp.n

    if stop_after == "params":
        _finish_early(prog, ctrs, out_tok, sb_sq, S, op, T, D, ndc)
        return _instantiate(nc, prog)

    for tt in range(ntt):
        DMA("sync", sb_xtok[:, :], x_tok[tt * 128:(tt + 1) * 128, :], s_tv,
            waits=[(sv, sv.n)] if tt > 0 else [])
        VOP(lambda e: e.reduce_sum(sb_red[:, 0:1], sb_xtok[:, :], axis=AX.X),
            waits=[(s_tv, s_tv.n), (s_scr, s_scr.n)])
        VOP(lambda e: e.tensor_scalar_mul(sb_tv[:, 0:1], sb_red[:, 0:1], 1.0 / D))
        VOP(lambda e: e.tensor_scalar(sb_sq[:, 0:D], sb_xtok[:, :], sb_tv[:, 0:1], None, ALU.subtract))
        VOP(lambda e: e.tensor_mul(sb_sq2[:, 0:D], sb_sq[:, 0:D], sb_sq[:, 0:D]))
        VOP(lambda e: e.reduce_sum(sb_red[:, 1:2], sb_sq2[:, 0:D], axis=AX.X))
        VOP(lambda e: e.tensor_scalar_mul(sb_red[:, 2:3], sb_red[:, 1:2], 1.0 / D))
        SOP(lambda e: e.activation(sb_red[:, 3:4], sb_red[:, 2:3], AF.Ln, bias=sb_cst[:, 0:1], scale=1.0),
            waits=[(sv, sv.n), (sgp, gp_cst)])
        SOP(lambda e: e.activation(sb_tv[:, 1:2], sb_red[:, 3:4], AF.Exp, bias=0.0, scale=-0.5))
        VOP(lambda e: e.tensor_scalar(sb_sq[:, 0:D], sb_xtok[:, :], sb_tv[:, 0:1], sb_tv[:, 1:2],
                                      ALU.subtract, ALU.mult), waits=[(ss, ss.n)])
        VOP(lambda e: e.reduce_sum(sb_red[:, 4:5], sb_sq[:, 0:D], axis=AX.X))
        VOP(lambda e: e.tensor_scalar_mul(sb_tv[:, 2:3], sb_red[:, 4:5], 1.0 / D))
        VOP(lambda e: e.tensor_scalar(sb_sq2[:, 0:D], sb_sq[:, 0:D], sb_tv[:, 2:3], None, ALU.subtract))
        VOP(lambda e: e.tensor_mul(sb_sq2[:, 0:D], sb_sq2[:, 0:D], sb_sq2[:, 0:D]))
        VOP(lambda e: e.reduce_sum(sb_red[:, 5:6], sb_sq2[:, 0:D], axis=AX.X))
        VOP(lambda e: e.tensor_scalar_mul(sb_red[:, 6:7], sb_red[:, 5:6], 1.0 / D))
        SOP(lambda e: e.activation(sb_red[:, 7:8], sb_red[:, 6:7], AF.Ln, bias=sb_cst[:, 1:2], scale=1.0),
            waits=[(sv, sv.n), (sgp, gp_cst)])
        SOP(lambda e: e.activation(sb_tv[:, 3:4], sb_red[:, 7:8], AF.Exp, bias=0.0, scale=-0.5))
        VOP(lambda e: e.tensor_scalar(sb_sq2[:, 0:D], sb_sq[:, 0:D], sb_tv[:, 2:3], sb_tv[:, 3:4],
                                      ALU.subtract, ALU.mult), waits=[(ss, ss.n)])
        VOP(lambda e: e.tensor_reduce(sb_red[:, 8:9], sb_sq2[:, 0:D], AX.X, ALU.max,
                                      apply_absolute_value=True))
        VOP(lambda e: e.reciprocal(sb_red[:, 9:10], sb_red[:, 8:9]))
        VOP(lambda e: e.tensor_scalar_mul(sb_tv[:, 4:5], sb_red[:, 9:10], 127.0))
        VOP(lambda e: e.tensor_scalar_mul(sb_tv[:, 5:6], sb_red[:, 8:9], 1.0 / 127.0))
        for r in range(6):
            DMA("gpsimd", scr_tv[r:r + 1, tt * 128:(tt + 1) * 128], sb_tv[:, r:r + 1],
                s_scr, waits=[(sv, sv.n)])
    v_stats = sv.n
    scr_stats = s_scr.n

    if stop_after == "stats":
        _finish_early(prog, ctrs, out_tok, sb_sq, S, op, T, D, ndc)
        return _instantiate(nc, prog)

    op("gpsimd",
       lambda e: e.collective_compute("AllGather", ALU.bypass,
                                      replica_groups=[list(range(C))],
                                      ins=[scr_tv[0:2, :].opt()], outs=[agb_out.opt()]),
       waits=[(s_scr, scr_stats)], inc=scc_b, k=1)
    cc_agb = scc_b.n

    for dst, r in [(sb_mb, 0), (sb_rb, 1), (sb_m2b, 2), (sb_r2b, 3), (sb_ssb, 4), (sb_iv2, 5)]:
        DMA("gpsimd", dst[:, :], bass.AP(scr_tv_t, r * T, [[0, 128], [1, T]]),
            s_bco, waits=[(s_scr, scr_stats)])
    bco = s_bco.n

    if stop_after == "agb":
        _finish_early(prog, ctrs, out_tok, sb_sq, S, op, T, D, ndc)
        return _instantiate(nc, prog)

    for dc in range(ndc):
        DMA("sync", sb_sq[:, 0:160], xpwT[dc * 128:(dc + 1) * 128, :], s_x3,
            waits=[(sv, sv.n)])
        VOP(lambda e, dc=dc: e.tensor_copy(sb_xpw[:, dc * 160:(dc + 1) * 160], sb_sq[:, 0:160]),
            waits=[(s_x3, s_x3.n)])
    v_xpw = sv.n

    nxpj = T // NB
    xp_mm_dc = []
    for dc in range(ndc):
        DMA("sync", sb_xod[:, :], x_ownT[dc * 128:(dc + 1) * 128, :], s_x3,
            waits=[(sv, sv.n)] if dc > 0 else [(sv, v_xpw)])
        VOP(lambda e: e.tensor_sub(sb_xnTf[:, :], sb_xod[:, :], sb_mb[:, :]),
            waits=[(s_x3, s_x3.n), (s_bco, bco)])
        VOP(lambda e: e.tensor_mul(sb_xnTf[:, :], sb_xnTf[:, :], sb_rb[:, :]))
        par = dc % 2
        VOP(lambda e, par=par: e.tensor_copy(sb_xnb[par][:, :], sb_xnTf[:, :]),
            waits=[(stt, xp_mm_dc[dc - 2])] if dc >= 2 else [])
        v_xnb = sv.n
        for j in range(nxpj):
            op("tensor",
               lambda e, dc=dc, j=j, par=par: e.matmul(
                   ps[j][:, 0:NB], sb_xpw[:, dc * 160:dc * 160 + 128],
                   sb_xnb[par][:, j * NB:(j + 1) * NB],
                   start=(dc == 0), stop=(dc == ndc - 1)),
               waits=[(sv, v_xnb), (sv, v_xpw)] if j == 0 else ())
        for j in range(nxpj):
            op("tensor",
               lambda e, dc=dc, j=j, par=par: e.matmul(
                   ps[2 + j][0:32, 0:NB], sb_xpw[:, dc * 160 + 128:dc * 160 + 160],
                   sb_xnb[par][:, j * NB:(j + 1) * NB],
                   start=(dc == 0), stop=(dc == ndc - 1)),
               inc=(stt if j == nxpj - 1 else None))
        xp_mm_dc.append(stt.n)
        VOP(lambda e: e.tensor_sub(sb_xod[:, :], sb_xnTf[:, :], sb_m2b[:, :]))
        VOP(lambda e: e.tensor_mul(sb_xod[:, :], sb_xod[:, :], sb_r2b[:, :]))
        VOP(lambda e: e.tensor_mul(sb_xod[:, :], sb_xod[:, :], sb_ssb[:, :]))
        VOP(lambda e: e.tensor_scalar(sb_f16[:, :], sb_xod[:, :], 1536.0, None, ALU.add))
        VOP(lambda e, dc=dc: e.tensor_scalar(sb_kT[:, dc * T:(dc + 1) * T], sb_f16[:, :],
                                             1536.0, None, ALU.subtract))
    v_kT = sv.n
    mm_xp = stt.n
    SOP(lambda e: e.activation(sb_ag1[:, 0:NB], ps[0][:, 0:NB],
                               AF.Identity, bias=0.0, scale=1.0), waits=[(stt, mm_xp), (sv, v_kT)])
    if nxpj > 1:
        SOP(lambda e: e.activation(sb_ag1[:, NB:2 * NB], ps[1][:, 0:NB],
                                   AF.Identity, bias=0.0, scale=1.0))
    SOP(lambda e: e.activation(sb_ag2[0:32, 0:NB], ps[2][0:32, 0:NB],
                               AF.Identity, bias=0.0, scale=1.0))
    if nxpj > 1:
        SOP(lambda e: e.activation(sb_ag2[0:32, NB:2 * NB], ps[3][0:32, 0:NB],
                                   AF.Identity, bias=0.0, scale=1.0))
    ss_aga = ss.n
    DMA("gpsimd", aga_in[0:128, :], sb_ag1[:, :], s_scr, waits=[(ss, ss_aga)])
    DMA("gpsimd", aga_in[128:160, :], sb_ag2[0:32, :], s_scr)
    op("gpsimd",
       lambda e: e.collective_compute("AllGather", ALU.bypass,
                                      replica_groups=[list(range(C))],
                                      ins=[aga_in.opt()], outs=[aga_out.opt()]),
       waits=[(s_scr, s_scr.n)], inc=scc_a, k=1)
    cc_aga = scc_a.n

    if stop_after == "kx":
        _finish_early(prog, ctrs, out_tok, sb_sq, S, op, T, D, ndc)
        return _instantiate(nc, prog)

    for dt in range(nds):
        for blk in range(C):
            DMA("sync", sb_dlow[:, 0:T], aga_out[blk * 160:blk * 160 + 128, :], s_ag,
                waits=[(scc_a, cc_aga)] + ([(stt, stt.n)] if (dt or blk) else []))
            for j in range(T // NB):
                op("tensor",
                   lambda e, dt=dt, j=j: e.matmul(ps[2][:, 0:NB],
                                                  sb_dtw[0:R, dt * 128:(dt + 1) * 128],
                                                  sb_dlow[0:R, j * NB:(j + 1) * NB],
                                                  start=True, stop=True),
                   waits=[(s_ag, s_ag.n), (sv, v_dtw), (ss, ss.n), (sv, sv.n)],
                   inc=stt)
                mm_dt = stt.n
                SOP(lambda e, dt=dt: e.activation(sb_sp1[:, 0:NB], ps[2][:, 0:NB], AF.Abs,
                                                  bias=sb_dtbc[:, dt:dt + 1], scale=1.0),
                    waits=[(stt, mm_dt)])
                SOP(lambda e: e.activation(sb_sp2[:, 0:NB], sb_sp1[:, 0:NB], AF.Exp,
                                           bias=0.0, scale=-1.0))
                SOP(lambda e: e.activation(sb_sp1[:, 0:NB], sb_sp2[:, 0:NB], AF.Ln,
                                           bias=sb_cst[:, 2:3], scale=1.0))
                SOP(lambda e, dt=dt: e.activation(sb_sp3[:, 0:NB], ps[2][:, 0:NB], AF.Relu,
                                                  bias=sb_dtbc[:, dt:dt + 1], scale=1.0),
                    waits=[(s_dts, s_dts.n)])
                VOP(lambda e: e.tensor_add(sb_sp3[:, 0:NB], sb_sp3[:, 0:NB], sb_sp1[:, 0:NB]),
                    waits=[(ss, ss.n), (s_dts, s_dts.n)])
                DMA("gpsimd", scr_dtT[dt * 128:(dt + 1) * 128, blk * T + j * NB:blk * T + (j + 1) * NB],
                    sb_sp3[:, 0:NB], s_dts, waits=[(sv, sv.n)])
    dts_all = s_dts.n

    scan_chunk_ready = {}
    bc_last_use = [0, 0]

    gp_yacc_cur = [0]

    def emit_scan_chunk_prep(dt, ch):
        GOP(lambda e: e.memset(sb_yacc[:, :], 0.0), waits=[(sv, sv.n), (s_y, s_y.n)])
        gp_yacc_cur[0] = sgp.n
        DMA("gpsimd", sb_dtT[:, :], scr_dtT[dt * 128:(dt + 1) * 128, ch * TC:(ch + 1) * TC],
            s_dtl, waits=[(s_dts, dts_all), (sv, sv.n), (ss, ss.n)])
        DMA("gpsimd", sb_xdc[:, :], x_dT[dt * 128:(dt + 1) * 128, ch * TC:(ch + 1) * TC], s_dtl)
        blk0 = (ch * TC) // T
        for b in range(nblk_c):
            DMA("gpsimd", sb_mab[:, b * T:(b + 1) * T],
                bass.AP(agb_out_t, (2 * (blk0 + b)) * T, [[0, 128], [1, T]]), s_mrb,
                waits=[(scc_b, cc_agb), (sv, sv.n)] if b == 0 else [])
            DMA("gpsimd", sb_rab[:, b * T:(b + 1) * T],
                bass.AP(agb_out_t, (2 * (blk0 + b) + 1) * T, [[0, 128], [1, T]]), s_mrb)
        VOP(lambda e: e.tensor_sub(sb_xdc[:, :], sb_xdc[:, :], sb_mab[:, :]),
            waits=[(s_dtl, s_dtl.n), (s_mrb, s_mrb.n)])
        VOP(lambda e: e.tensor_mul(sb_xdc[:, :], sb_xdc[:, :], sb_rab[:, :]))
        VOP(lambda e: e.tensor_mul(sb_dtx[:, :], sb_dtT[:, :], sb_xdc[:, :]))
        scan_chunk_ready[(dt, ch)] = sv.n
        emit_scan_exp(dt, ch, 0)
        emit_scan_exp(dt, ch, 1)

    scan_exp_ms = {}
    scan_done_ms = {}

    def emit_scan_exp(dt, ch, n):
        slot = n % 3
        w = [(sv, scan_chunk_ready[(dt, ch)])]
        prev = (dt, ch, n - 3)
        if prev in scan_done_ms:
            w.append((sv, scan_done_ms[prev]))
        SOP(lambda e, dt=dt, n=n, slot=slot: e.activation(
            sb_ab3[slot][:, :], sb_dtT[:, :], AF.Exp, bias=0.0,
            scale=sb_acol[:, dt * N + n:dt * N + n + 1]), waits=w)
        scan_exp_ms[(dt, ch, n)] = ss.n

    def emit_scan_group(dt, ch, n):
        par = n % 2
        slot = n % 3
        blk0 = (ch * TC) // T
        for b in range(nblk_c):
            row_b = (blk0 + b) * 160 + 128 + n
            row_c = (blk0 + b) * 160 + 144 + n
            w = [(scc_a, cc_aga)] if (dt, ch, n) == (0, 0, 0) else \
                ([(sv, bc_last_use[par])] if b == 0 else [])
            DMA("gpsimd", sb_bcb[par][:, b * T:(b + 1) * T],
                bass.AP(aga_out_t, row_b * T, [[0, 128], [1, T]]), s_bcs[par], waits=w)
            DMA("gpsimd", sb_bcc[par][:, b * T:(b + 1) * T],
                bass.AP(aga_out_t, row_c * T, [[0, 128], [1, T]]), s_bcs[par])
        ahead = n + 2
        if ahead < N and (dt, ch, ahead) not in scan_exp_ms:
            emit_scan_exp(dt, ch, ahead)
        VOP(lambda e, par=par: e.tensor_mul(sb_bx[:, :], sb_dtx[:, :], sb_bcb[par][:, :]),
            waits=[(s_bcs[par], s_bcs[par].n)])
        VOP(lambda e, dt=dt, n=n, slot=slot: e.tensor_tensor_scan(
            sb_h[:, :], sb_ab3[slot][:, :], sb_bx[:, :],
            sb_hst[:, dt * N + n:dt * N + n + 1], ALU.mult, ALU.add),
            waits=[(ss, scan_exp_ms[(dt, ch, n)]), (sgp, gp_hst0)])
        scan_done_ms[(dt, ch, n)] = sv.n
        VOP(lambda e, dt=dt, n=n: e.tensor_copy(sb_hst[:, dt * N + n:dt * N + n + 1],
                                                sb_h[:, TC - 1:TC]))
        VOP(lambda e, par=par: e.tensor_mul(sb_ytmp[:, :], sb_h[:, :], sb_bcc[par][:, :]))
        VOP(lambda e: e.tensor_add(sb_yacc[:, :], sb_yacc[:, :], sb_ytmp[:, :]),
            waits=[(sgp, gp_yacc_cur[0])])
        bc_last_use[par] = sv.n

    def emit_scan_chunk_finish(dt, ch):
        blk0 = (ch * TC) // T
        for b in range(nblk_c):
            DMA("gpsimd", y_in[(blk0 + b) * ds + dt * 128:(blk0 + b) * ds + (dt + 1) * 128, :],
                sb_yacc[:, b * T:(b + 1) * T], s_y, waits=[(sv, sv.n)] if b == 0 else [])

    scan_steps = []
    for dt in range(nds):
        for ch in range(nch):
            scan_steps.append(("prep", dt, ch, 0))
            for n in range(N):
                scan_steps.append(("group", dt, ch, n))
            scan_steps.append(("finish", dt, ch, 0))
    scan_pos = [0]

    def emit_some_scan(k):
        if stop_after == "upnoscan":
            return
        done = 0
        while done < k and scan_pos[0] < len(scan_steps):
            kind, dt, ch, n = scan_steps[scan_pos[0]]
            if kind == "prep":
                emit_scan_chunk_prep(dt, ch)
            elif kind == "group":
                emit_scan_group(dt, ch, n)
            else:
                emit_scan_chunk_finish(dt, ch)
            scan_pos[0] += 1
            done += 1

    total_up_slots = nhr * nht_r * ntb
    scan_per_slot = max(1, -(-len(scan_steps) // total_up_slots))

    if stop_after == "p0":
        _finish_early(prog, ctrs, out_tok, sb_sq, S, op, T, D, ndc)
        return _instantiate(nc, prog)

    up_psum_free = {0: 0, 1: 0}
    up_hr_ms = []
    wf_sign_ms = [0, 0]
    q_dma_ms = [0, 0]
    for hr in range(nhr):
        spar = hr % 2
        for dc in range(ndc):
            for gm, (wsrc, sgl) in enumerate([(wgT, sb_sgn_g), (wuT, sb_sgn_u)]):
                sgt = sgl[spar]
                par = (dc * 2 + gm) % 2
                w = [(ss, wf_sign_ms[par])]
                if hr >= 2 and dc == 0 and gm == 0:
                    w.append((stt, up_hr_ms[hr - 2]))
                DMA("sync", sb_wf[par][:, :], wsrc[dc * 128:(dc + 1) * 128, hr * HRS:(hr + 1) * HRS],
                    s_wf[par], waits=w)
                SOP(lambda e, par=par, dc=dc, sgt=sgt: e.activation(
                    sgt[:, dc * HRS:(dc + 1) * HRS], sb_wf[par][:, :], AF.Sign,
                    bias=0.0, scale=1.0),
                    waits=[(s_wf[par], s_wf[par].n)])
                wf_sign_ms[par] = ss.n
        sgn_done = ss.n
        for ht in range(nht_r):
            for tb in range(ntb):
                hg = hr * nht_r + ht
                slot = (ht * ntb + tb) % 2
                gb, ub = 2 * slot, 2 * slot + 1
                tw = [(ss, sgn_done), (sv, v_kT), (ss, up_psum_free[slot])]
                for dc in range(ndc):
                    op("tensor",
                       lambda e, dc=dc, ht=ht, tb=tb, gb=gb, spar=spar: e.matmul(
                           ps[gb][:, 0:NB], sb_sgn_g[spar][:, dc * HRS + ht * 128:dc * HRS + ht * 128 + 128],
                           sb_kT[:, dc * T + tb * NB:dc * T + (tb + 1) * NB],
                           start=(dc == 0), stop=(dc == ndc - 1)),
                       waits=tw if dc == 0 else ())
                for dc in range(ndc):
                    last = dc == ndc - 1
                    op("tensor",
                       lambda e, dc=dc, ht=ht, tb=tb, ub=ub, last=last, spar=spar: e.matmul(
                           ps[ub][:, 0:NB], sb_sgn_u[spar][:, dc * HRS + ht * 128:dc * HRS + ht * 128 + 128],
                           sb_kT[:, dc * T + tb * NB:dc * T + (tb + 1) * NB],
                           start=(dc == 0), stop=last),
                       inc=(stt if last else None))
                mm_up = stt.n
                SOP(lambda e, gb=gb: e.activation(sb_evg[:, :], ps[gb][:, 0:NB],
                                                  AF.Identity, bias=0.0, scale=1.0),
                    waits=[(stt, mm_up), (sv, sv.n)])
                SOP(lambda e, ub=ub: e.activation(sb_evu[:, :], ps[ub][:, 0:NB],
                                                  AF.Identity, bias=0.0, scale=1.0))
                up_psum_free[slot] = ss.n
                VOP(lambda e, tb=tb: e.tensor_tensor(sb_ev1[:, :], sb_evg[:, :],
                                                     sb_iv2[:, tb * NB:(tb + 1) * NB], ALU.mult),
                    waits=[(ss, ss.n), (s_bco, bco)])
                SOP(lambda e: e.activation(sb_ev2[:, :], sb_ev1[:, :], AF.Tanh, bias=0.0, scale=0.5),
                    waits=[(sv, sv.n)])
                VOP(lambda e: e.tensor_scalar(sb_ev3[:, :], sb_ev2[:, :], 0.5, 0.5, ALU.mult, ALU.add),
                    waits=[(ss, ss.n)])
                VOP(lambda e: e.tensor_mul(sb_ev4[:, :], sb_ev1[:, :], sb_ev3[:, :]))
                qp = slot
                VOP(lambda e, qp=qp: e.tensor_mul(sb_q[qp][:, :], sb_ev4[:, :], sb_evu[:, :]),
                    waits=[(s_p[qp], q_dma_ms[qp])])
                VOP(lambda e, tb=tb, qp=qp: e.tensor_mul(sb_q[qp][:, :], sb_q[qp][:, :],
                                                         sb_iv2[:, tb * NB:(tb + 1) * NB]))
                DMA("sync", p_scr[hg * 128:(hg + 1) * 128, tb * NB:(tb + 1) * NB], sb_q[qp][:, :],
                    s_p[qp], waits=[(sv, sv.n)])
                q_dma_ms[qp] = s_p[qp].n
                emit_some_scan(scan_per_slot)
        up_hr_ms.append(stt.n)
    emit_some_scan(len(scan_steps))
    p_all = (s_p[0].n, s_p[1].n)
    y_all = s_y.n

    if stop_after == "up":
        _finish_early(prog, ctrs, out_tok, sb_sq, S, op, T, D, ndc)
        return _instantiate(nc, prog)
    if stop_after == "upnoscan":
        _finish_early(prog, ctrs, out_tok, sb_sq, S, op, T, D, ndc)
        return _instantiate(nc, prog)

    def cc_y_fn(e):
        with allow_ici_all_to_all():
            return e.collective_compute("AllToAll", ALU.bypass,
                                        replica_groups=[list(range(C))],
                                        ins=[y_in.opt()], outs=[y_out.opt()])
    op("gpsimd", cc_y_fn, waits=[(s_y, y_all)], inc=scc_y, k=1)
    cc_y = scc_y.n

    dn_ms = []
    for oq in range(noq):
        for hh in range(nh // 2):
            par = hh % 2
            gi = oq * (nh // 2) + hh
            w_slot = [(stt, dn_ms[gi - 2])] if gi >= 2 else []
            DMA("sync",
                sb_wdf[par][:, :].rearrange("p (a f) -> p a f", a=2),
                wdT[hh * 256:(hh + 1) * 256, oq * OQ:(oq + 1) * OQ].rearrange("(a p) f -> p a f", p=128),
                s_wd[par], waits=[(sv, sv.n)] + w_slot)
            DMA("sync",
                sb_pt[par][:, :].rearrange("p (a f) -> p a f", a=2),
                p_scr[hh * 256:(hh + 1) * 256, :].rearrange("(a p) f -> p a f", p=128),
                s_pl[par],
                waits=([(s_p[0], p_all[0]), (s_p[1], p_all[1])] if gi == 0 else []) + w_slot)
            VOP(lambda e, par=par: e.tensor_scalar(sb_wds[par][:, :], sb_wdf[par][:, :],
                                                   0.0, 0.5, ALU.is_ge, ALU.subtract),
                waits=[(s_wd[par], s_wd[par].n)] + w_slot)
            sgn_d = sv.n
            for hi in range(2):
                h = hh * 2 + hi
                for ot in range(no_t):
                    for tb in range(ntb):
                        bank = ot * ntb + tb
                        w = []
                        if hi == 0 and ot == 0 and tb == 0:
                            w = [(sv, sgn_d), (s_pl[par], s_pl[par].n)]
                            if h == 0 and oq > 0:
                                w.append((ss, ss.n))
                        op("tensor",
                           lambda e, par=par, hi=hi, ot=ot, tb=tb, h=h, bank=bank: e.matmul(
                               ps[bank][:, 0:NB],
                               sb_wds[par][:, hi * OQ + ot * 128:hi * OQ + (ot + 1) * 128],
                               sb_pt[par][:, hi * T + tb * NB:hi * T + (tb + 1) * NB],
                               start=(h == 0), stop=(h == nh - 1)),
                           waits=w,
                           inc=(stt if (hi == 1 and ot == no_t - 1 and tb == ntb - 1) else None))
            dn_ms.append(stt.n)
        mm_dn = stt.n
        for ot in range(no_t):
            for tb in range(ntb):
                bank = ot * ntb + tb
                oc = oq * no_t + ot
                fpar = bank % 2
                SOP(lambda e, bank=bank, fpar=fpar: e.activation(
                    sb_fs[fpar][:, :], ps[bank][:, 0:NB],
                    AF.Identity, bias=0.0, scale=2.0),
                    waits=[(stt, mm_dn), (s_fsp[fpar], fs_slot_ms[fpar])])
                DMA("gpsimd", ffn_scr[oc * 128:(oc + 1) * 128, tb * NB:(tb + 1) * NB],
                    sb_fs[fpar][:, :], s_fsp[fpar], waits=[(ss, ss.n)])
                fs_slot_ms[fpar] = s_fsp[fpar].n
    ss_dn = ss.n

    if stop_after == "down":
        _finish_early(prog, ctrs, out_tok, sb_sq, S, op, T, D, ndc)
        return _instantiate(nc, prog)

    GOP(lambda e: e.memset(sb_ident[:, :], 1.0), waits=[(sv, sv.n), (ss, ss.n)])
    GOP(lambda e: e.affine_select(sb_ident[:, :], sb_ident[:, :], [[1, 128]],
                                  ALU.is_equal, 0.0, base=0, channel_multiplier=-1),
        waits=[(sgp, sgp.n)])
    gp_ident = sgp.n
    out_slot_ms = [0, 0]
    for dc in range(ndc):
        DMA("sync", sb_xod[:, :], x_ownT[dc * 128:(dc + 1) * 128, :], s_x3,
            waits=[(sv, sv.n), (ss, ss.n)] if dc == 0 else [(sv, sv.n)])
        DMA("sync", sb_ycol[:, :], y_out[dc * 128:(dc + 1) * 128, :], s_x3,
            waits=[(scc_y, cc_y), (ss, ss.n)] if dc == 0 else [])
        DMA("sync", sb_fload[:, :], ffn_scr[dc * 128:(dc + 1) * 128, :], s_x3,
            waits=[(s_fsp[0], s_fsp[0].n), (s_fsp[1], s_fsp[1].n)] if dc == 0 else [])
        x3n = s_x3.n
        VOP(lambda e: e.tensor_sub(sb_xnTf[:, :], sb_xod[:, :], sb_mb[:, :]),
            waits=[(s_x3, x3n), (stt, stt.n)])
        VOP(lambda e: e.tensor_mul(sb_xnTf[:, :], sb_xnTf[:, :], sb_rb[:, :]))
        VOP(lambda e, dc=dc: e.scalar_tensor_tensor(sb_acc[:, :], sb_xnTf[:, :],
                                                    sb_dpb[:, dc:dc + 1], sb_xod[:, :],
                                                    ALU.mult, ALU.add))
        VOP(lambda e: e.tensor_add(sb_acc[:, :], sb_acc[:, :], sb_ycol[:, :]))
        VOP(lambda e: e.tensor_add(sb_acc[:, :], sb_acc[:, :], sb_fload[:, :]),
            waits=[(ss, ss_dn)])
        acc_done = sv.n
        for tt in range(ntt):
            pslot = tt % 2
            op("tensor",
               lambda e, tt=tt, pslot=pslot: e.transpose(ps[pslot][:, 0:128],
                                                         sb_acc[:, tt * 128:(tt + 1) * 128],
                                                         sb_ident[:, :]),
               waits=[(sv, acc_done), (sgp, gp_ident)] +
                     ([(ss, ss.n)] if (dc * ntt + tt) >= 2 else []),
               inc=stt)
            tr_done = stt.n
            SOP(lambda e, pslot=pslot: e.copy(sb_ot[pslot][:, :], ps[pslot][:, 0:128]),
                waits=[(stt, tr_done), (s_out, out_slot_ms[pslot])])
            DMA("sync", out_tok[tt * 128:(tt + 1) * 128, dc * 128:(dc + 1) * 128],
                sb_ot[pslot][:, :], s_out, waits=[(ss, ss.n)])
            out_slot_ms[pslot] = s_out.n
    prog["sync"].append(lambda eng, h=s_out.h, v=s_out.n: eng.wait_ge(h, v))
    prog["gpsimd"].append(lambda eng, h=s_out.h, v=s_out.n: eng.wait_ge(h, v))

    return _instantiate(nc, prog)


def _finish_early(prog, ctrs, out_tok, sb_src, S, op, T, D, ndc):
    s_out = ctrs["s_out"]
    sv = ctrs["sv"]
    ss = ctrs["ss"]
    W = min(D, sb_src.shape[1])
    for tt in range(T // 128):
        for c0 in range(0, D, W):
            op("sync", lambda eng, tt=tt, c0=c0: eng.dma_start(
                out=out_tok[tt * 128:(tt + 1) * 128, c0:c0 + W],
                in_=sb_src[:, 0:W]), waits=[(sv, sv.n), (ss, ss.n)] if (tt == 0 and c0 == 0) else [],
                inc=s_out, k=16)
    prog["sync"].append(lambda eng, h=s_out.h, v=s_out.n: eng.wait_ge(h, v))
    prog["gpsimd"].append(lambda eng, h=s_out.h, v=s_out.n: eng.wait_ge(h, v))


def _instantiate(nc, prog):
    with nc.Block() as block:
        @block.sync
        def _(eng):
            for fn in prog["sync"]:
                fn(eng)

        @block.gpsimd
        def _(eng):
            for fn in prog["gpsimd"]:
                fn(eng)

        @block.tensor
        def _(eng):
            for fn in prog["tensor"]:
                fn(eng)

        @block.vector
        def _(eng):
            for fn in prog["vector"]:
                fn(eng)

        @block.scalar
        def _(eng):
            for fn in prog["scalar"]:
                fn(eng)

    return nc


def make_in_maps(cfg, inputs):
    D, H, L, N, R, C = cfg["D"], cfg["H"], cfg["L"], cfg["N"], cfg["R"], cfg["C"]
    T = L // C
    ds = D // C
    x = np.ascontiguousarray(np.asarray(inputs["L0_attn_x"], dtype=np.float32)[0])
    xpw = np.ascontiguousarray(np.asarray(inputs["L0_attn_x_proj_w"], np.float32).T)
    dtpw = np.ascontiguousarray(np.asarray(inputs["L0_attn_dt_proj_w"], np.float32).T)
    dtbv = np.asarray(inputs["L0_attn_dt_proj_b"], np.float32)
    al = np.asarray(inputs["L0_attn_A_log"], np.float32)
    dpv = np.ascontiguousarray(np.asarray(inputs["L0_attn_D_param"], np.float32).reshape(D // 128, 128).T)
    wg = np.ascontiguousarray(np.asarray(inputs["L0_ffn_W_g"], np.float32).T)
    wu = np.ascontiguousarray(np.asarray(inputs["L0_ffn_W_u"], np.float32).T)
    wd = np.ascontiguousarray(np.asarray(inputs["L0_ffn_W_d"], np.float32).T)
    xT = np.ascontiguousarray(x.T)
    in_maps = []
    for c in range(C):
        in_maps.append({
            "x_tok": np.ascontiguousarray(x[c * T:(c + 1) * T, :]),
            "x_ownT": np.ascontiguousarray(xT[:, c * T:(c + 1) * T]),
            "x_dT": np.ascontiguousarray(xT[c * ds:(c + 1) * ds, :]),
            "xpwT": xpw,
            "dtpwT": np.ascontiguousarray(dtpw[:, c * ds:(c + 1) * ds]),
            "dtb": np.ascontiguousarray(dtbv[c * ds:(c + 1) * ds].reshape(ds // 128, 128).T),
            "alog": np.ascontiguousarray(
                al[c * ds:(c + 1) * ds, :].reshape(ds // 128, 128, N).transpose(1, 0, 2).reshape(128, -1)),
            "dp": dpv,
            "wgT": wg,
            "wuT": wu,
            "wdT": wd,
        })
    return in_maps


def kernel(**inputs):
    cfg = FULL_CFG
    nc = build_nc(cfg)
    in_maps = make_in_maps(cfg, inputs)
    res = run_bass_kernel_spmd(nc, in_maps, core_ids=list(range(cfg["C"])))
    out = np.concatenate([np.asarray(r["out_tok"]) for r in res.results], axis=0)
    return out[None].astype(np.float32)


# version 10
# speedup vs baseline: 1.0194x; 1.0194x over previous
import numpy as np
import concourse.bass as bass
import concourse.mybir as mybir
from concourse.bass_utils import run_bass_kernel_spmd
from concourse.aot_env import allow_ici_all_to_all

F32 = mybir.dt.float32
BF16 = mybir.dt.bfloat16
F16 = mybir.dt.float16
AF = mybir.ActivationFunctionType
ALU = mybir.AluOpType
AX = mybir.AxisListType

FULL_CFG = dict(D=2048, H=8192, L=8192, N=16, R=128, C=8)


class Ctr:
    def __init__(self, handle):
        self.h = handle
        self.n = 0


def build_nc(cfg):
    stop_after = cfg.get("stop_after", "full")
    D, H, L, N, R, C = cfg["D"], cfg["H"], cfg["L"], cfg["N"], cfg["R"], cfg["C"]
    T = L // C
    ds = D // C
    nds = ds // 128
    ndc = D // 128
    NB = min(512, T)
    ntb = T // NB
    HRS = min(256, H)
    nhr = H // HRS
    nht_r = HRS // 128
    nh = H // 128
    OQ = min(512, D)
    noq = D // OQ
    no_t = OQ // 128
    TC = min(2048, L)
    nch = L // TC
    nblk_c = max(1, TC // T)
    ntt = T // 128
    assert TC % T == 0 and R == 128 and ds % 128 == 0

    nc = bass.Bass("TRN2", target_bir_lowering=False, detect_race_conditions=False)

    x_tok = nc.dram_tensor("x_tok", [T, D], F32, kind="ExternalInput").ap()
    x_ownT = nc.dram_tensor("x_ownT", [D, T], F32, kind="ExternalInput").ap()
    x_dT = nc.dram_tensor("x_dT", [ds, L], F32, kind="ExternalInput").ap()
    xpwT = nc.dram_tensor("xpwT", [D, 160], F32, kind="ExternalInput").ap()
    dtpwT = nc.dram_tensor("dtpwT", [R, ds], F32, kind="ExternalInput").ap()
    dtb = nc.dram_tensor("dtb", [128, nds], F32, kind="ExternalInput").ap()
    alog = nc.dram_tensor("alog", [128, nds * N], F32, kind="ExternalInput").ap()
    dp = nc.dram_tensor("dp", [128, D // 128], F32, kind="ExternalInput").ap()
    wgT = nc.dram_tensor("wgT", [D, H], F32, kind="ExternalInput").ap()
    wuT = nc.dram_tensor("wuT", [D, H], F32, kind="ExternalInput").ap()
    wdT = nc.dram_tensor("wdT", [H, D], F32, kind="ExternalInput").ap()
    out_tok = nc.dram_tensor("out_tok", [T, D], F32, kind="ExternalOutput").ap()

    scr_tv_t = nc.dram_tensor("scr_tv", [8, T], F32, kind="Internal")
    aga_in_t = nc.dram_tensor("aga_in", [160, T], BF16, kind="Internal")
    aga_out_t = nc.dram_tensor("aga_out", [160 * C, T], BF16, kind="Internal")
    agb_out_t = nc.dram_tensor("agb_out", [2 * C, T], F32, kind="Internal")
    scr_dtT_t = nc.dram_tensor("scr_dtT", [ds, L], F32, kind="Internal")
    p_scr_t = nc.dram_tensor("p_scr", [H, T], BF16, kind="Internal")
    ffn_scr_t = nc.dram_tensor("ffn_scr", [D, T], BF16, kind="Internal")
    y_in_t = nc.dram_tensor("y_a2a_in", [C * ds, T], BF16, kind="Internal")
    y_out_t = nc.dram_tensor("y_a2a_out", [C * ds, T], BF16, kind="Internal")
    scr_tv = scr_tv_t.ap()
    aga_in = aga_in_t.ap()
    aga_out = aga_out_t.ap()
    agb_out = agb_out_t.ap()
    scr_dtT = scr_dtT_t.ap()
    p_scr = p_scr_t.ap()
    ffn_scr = ffn_scr_t.ap()
    y_in = y_in_t.ap()
    y_out = y_out_t.ap()

    A = nc.alloc_sbuf_tensor
    AAT = nc.alloc_sbuf_tensor_at

    def addr_of(h):
        return nc.lookup_mloc(h).addr

    sb_xtok = A("sb_xtok", [128, D], F32)
    sb_sq = A("sb_sq", [128, D], F32)
    sb_sq2 = A("sb_sq2", [128, D], F32)
    sb_p0a = A("sb_p0a", [128, 3072], F32)
    sb_tv = A("sb_tv", [128, 8], F32)
    sb_red = A("sb_red", [128, 12], F32)
    sb_xod = A("sb_xod", [128, T], F32)
    sb_xnTf = A("sb_xnTf", [128, T], F32)
    sb_xnb = [A(f"sb_xnb{i}", [128, T], BF16) for i in range(2)]
    sb_kT = A("sb_kT", [128, ndc * T], BF16)
    sb_mb = A("sb_mb", [128, T], F32)
    sb_rb = A("sb_rb", [128, T], F32)
    sb_m2b = A("sb_m2b", [128, T], F32)
    sb_r2b = A("sb_r2b", [128, T], F32)
    sb_ssb = A("sb_ssb", [128, T], F32)
    sb_iv2 = A("sb_iv2", [128, T], BF16)
    sb_xpw = A("sb_xpw", [128, ndc * 160], BF16)
    sb_dtw = A("sb_dtw", [128, ds], BF16)
    sb_ag2 = A("sb_ag2", [32, T], BF16)
    sb_dtbc = A("sb_dtbc", [128, nds], F32)
    sb_acol = A("sb_acol", [128, nds * N], F32)
    sb_wf = [A(f"sb_wf{i}", [128, HRS], F32) for i in range(2)]
    sb_sgn_g = [A(f"sb_sgn_g{i}", [128, ndc * HRS], BF16) for i in range(2)]
    sb_sgn_u = [A(f"sb_sgn_u{i}", [128, ndc * HRS], BF16) for i in range(2)]
    sb_q = [A(f"sb_q{i}", [128, NB], BF16) for i in range(2)]
    sb_evg = [A(f"sb_evg{i}", [128, NB], BF16) for i in range(2)]
    sb_evu = [A(f"sb_evu{i}", [128, NB], BF16) for i in range(2)]
    sb_ev1 = A("sb_ev1", [128, NB], BF16)
    sb_ev2 = A("sb_ev2", [128, NB], BF16)
    sb_ev3 = A("sb_ev3", [128, NB], BF16)
    sb_ev4 = A("sb_ev4", [128, NB], BF16)
    sb_dtT = A("sb_dtT", [128, TC], F32)
    sb_xdc = A("sb_xdc", [128, TC], F32)
    sb_h = A("sb_h", [128, TC], BF16)
    sb_bc = [A(f"sb_bc{i}", [128, 2 * TC], BF16) for i in range(3)]
    sb_hst = A("sb_hst", [128, max(1, nds * N)], F32)
    sb_ytmp = A("sb_ytmp", [128, TC], BF16)
    sb_yacc = A("sb_yacc", [128, TC], BF16)

    sb_dpb = A("sb_dpb", [128, ndc], F32)
    sb_cst = A("sb_cst", [128, 4], F32)
    a_xtok = addr_of(sb_xtok)
    sb_pt = [AAT(f"sb_pt{i}", [128, 2 * T], BF16, offset=a_xtok + i * (4 * T)) for i in range(2)]
    a_sq = addr_of(sb_sq)
    sb_wdf = [AAT(f"sb_wdf{i}", [128, 2 * OQ], F32, offset=a_sq + i * (8 * OQ)) for i in range(2)]
    sb_wds = None
    a_sq2 = addr_of(sb_sq2)
    sb_mab = AAT("sb_mab", [128, TC], BF16, offset=a_sq2)
    sb_rab = AAT("sb_rab", [128, TC], BF16, offset=a_sq2 + 2 * TC)
    a_p0a = addr_of(sb_p0a)
    sb_sp1 = AAT("sb_sp1", [128, NB], F32, offset=a_p0a)
    sb_sp2 = AAT("sb_sp2", [128, NB], F32, offset=a_p0a + 4 * NB)
    sb_sp3 = AAT("sb_sp3", [128, NB], F32, offset=a_p0a + 8 * NB)
    sb_f16 = AAT("sb_f16", [128, T], F16, offset=a_p0a + 12 * NB)
    sb_dlow = AAT("sb_dlow", [128, T], BF16, offset=a_p0a + 12 * NB + 2 * T)
    sb_ag1 = AAT("sb_ag1", [128, T], BF16, offset=a_p0a + 12 * NB + 4 * T)
    sb_dtx = AAT("sb_dtx", [128, TC], BF16, offset=a_p0a)
    sb_ab3 = None
    sb_bx = AAT("sb_bx", [128, TC], BF16, offset=a_p0a + 4 * TC)
    a_dtT = addr_of(sb_dtT)
    sb_acc = AAT("sb_acc", [128, T], F32, offset=a_dtT)
    sb_ycol = AAT("sb_ycol", [128, T], BF16, offset=a_dtT + 4 * T)
    sb_fload = AAT("sb_fload", [128, T], BF16, offset=a_dtT + 6 * T)
    a_xdc = addr_of(sb_xdc)
    sb_ot = [AAT(f"sb_ot{i}", [128, 128], F32, offset=a_xdc + i * 512) for i in range(2)]
    sb_ident = AAT("sb_ident", [128, 128], F32, offset=a_xdc + 1024)
    def _fit(name, shape, dtype, host, host_bytes):
        nbytes = shape[1] * (2 if dtype in (BF16, F16) else 4)
        if nbytes <= host_bytes:
            return AAT(name, shape, dtype, offset=addr_of(host))
        return A(name, shape, dtype)
    sb_ab3 = [AAT("sb_ab0", [128, TC], BF16, offset=a_p0a + 2 * TC),
              _fit("sb_ab1", [128, TC], BF16, sb_xod, 4 * T),
              _fit("sb_ab2", [128, TC], BF16, sb_xnTf, 4 * T)]
    sb_wds = [_fit(f"sb_wds{i}", [128, 2 * OQ], BF16, sb_xnb[i], 2 * T) for i in range(2)]
    sb_fs = [AAT(f"sb_fs{i}", [128, NB], BF16, offset=addr_of(sb_q[i])) for i in range(2)]
    ps = [nc.alloc_psum_tensor(f"ps{i}", [128, 512], F32) for i in range(8)]

    prog = {e: [] for e in ["sync", "gpsimd", "tensor", "vector", "scalar"]}
    ctrs = {}
    sem_keep = []

    def S(name):
        c = ctrs.get(name)
        if c is None:
            g = nc.semaphore(name)
            h = g.__enter__()
            sem_keep.append(g)
            c = Ctr(h)
            ctrs[name] = c
        return c

    self_ctr = {}

    def op(e, fn, waits=(), inc=None, k=1):
        own = self_ctr.get(e)
        wl = [(c.h, v) for (c, v) in waits if v > 0 and c is not own]
        dd = {}
        for h, v in wl:
            dd[h] = max(dd.get(h, 0), v)
        wl = list(dd.items())
        if inc is not None:
            inc.n += k

        def run(eng, fn=fn, wl=wl, inc=inc, k=k):
            for h, v in wl:
                eng.wait_ge(h, v)
            ins = fn(eng)
            if inc is not None:
                ins.then_inc(inc.h, k)
        prog[e].append(run)

    sv = S("sv")
    ss = S("ss")
    stt = S("stt")
    sgp = S("sgp")
    scc_b = S("scc_b")
    scc_a = S("scc_a")
    scc_y = S("scc_y")
    s_in = S("s_in")
    s_x3 = S("s_x3")
    s_tv = S("s_tv")
    s_scr = S("s_scr")
    s_bco = S("s_bco")
    s_ag = S("s_ag")
    s_dts = S("s_dts")
    s_dtl = S("s_dtl")
    s_bcs = [S("s_bc0"), S("s_bc1"), S("s_bc2")]
    s_mrb = S("s_mrb")
    s_wf = [S("s_wf0"), S("s_wf1")]
    s_p = [S("s_p0"), S("s_p1")]
    s_pl = [S("s_pl0"), S("s_pl1")]
    s_wd = [S("s_wd0"), S("s_wd1")]
    s_y = S("s_y")
    s_fsp = [S("s_fsp0"), S("s_fsp1")]
    fs_slot_ms = [0, 0]
    s_out = S("s_out")

    self_ctr.update({"tensor": stt})

    def VOP(fn, waits=()):
        op("vector", fn, waits=list(waits) + [(sv, sv.n)], inc=sv)

    def SOP(fn, waits=()):
        op("scalar", fn, waits=list(waits) + [(ss, ss.n)], inc=ss)

    def GOP(fn, waits=()):
        op("gpsimd", fn, waits=list(waits) + [(sgp, sgp.n)], inc=sgp)

    def DMA(e, out_ap, in_ap, ctr, waits=()):
        op(e, lambda eng, o=out_ap, i=in_ap: eng.dma_start(out=o, in_=i),
           waits=waits, inc=ctr, k=16)

    DMA("sync", sb_dtbc[:, :], dtb, s_in)
    DMA("sync", sb_acol[:, :], alog, s_in)
    DMA("sync", sb_dpb[:, :], dp, s_in)
    DMA("sync", sb_sq[0:R, 0:ds], dtpwT, s_in)
    in0 = s_in.n
    VOP(lambda e: e.tensor_copy(sb_dtw[0:R, :], sb_sq[0:R, 0:ds]), waits=[(s_in, in0)])
    v_dtw = sv.n
    SOP(lambda e: e.activation(sb_acol[:, :], sb_acol[:, :], AF.Exp, bias=0.0, scale=1.0),
        waits=[(s_in, in0)])
    VOP(lambda e: e.tensor_scalar_mul(sb_acol[:, :], sb_acol[:, :], -1.0), waits=[(ss, ss.n)])
    v_acol = sv.n
    GOP(lambda e: e.memset(sb_cst[:, 0:1], 1e-5))
    GOP(lambda e: e.memset(sb_cst[:, 1:2], 1e-8))
    GOP(lambda e: e.memset(sb_cst[:, 2:3], 1.0))
    gp_cst = sgp.n
    GOP(lambda e: e.memset(sb_hst[:, :], 0.0))
    gp_hst0 = sgp.n

    if stop_after == "params":
        _finish_early(prog, ctrs, out_tok, sb_sq, S, op, T, D, ndc)
        return _instantiate(nc, prog)

    for tt in range(ntt):
        DMA("sync", sb_xtok[:, :], x_tok[tt * 128:(tt + 1) * 128, :], s_tv,
            waits=[(sv, sv.n)] if tt > 0 else [])
        VOP(lambda e: e.reduce_sum(sb_red[:, 0:1], sb_xtok[:, :], axis=AX.X),
            waits=[(s_tv, s_tv.n), (s_scr, s_scr.n)])
        VOP(lambda e: e.tensor_scalar_mul(sb_tv[:, 0:1], sb_red[:, 0:1], 1.0 / D))
        VOP(lambda e: e.tensor_scalar(sb_sq[:, 0:D], sb_xtok[:, :], sb_tv[:, 0:1], None, ALU.subtract))
        VOP(lambda e: e.tensor_mul(sb_sq2[:, 0:D], sb_sq[:, 0:D], sb_sq[:, 0:D]))
        VOP(lambda e: e.reduce_sum(sb_red[:, 1:2], sb_sq2[:, 0:D], axis=AX.X))
        VOP(lambda e: e.tensor_scalar_mul(sb_red[:, 2:3], sb_red[:, 1:2], 1.0 / D))
        SOP(lambda e: e.activation(sb_red[:, 3:4], sb_red[:, 2:3], AF.Ln, bias=sb_cst[:, 0:1], scale=1.0),
            waits=[(sv, sv.n), (sgp, gp_cst)])
        SOP(lambda e: e.activation(sb_tv[:, 1:2], sb_red[:, 3:4], AF.Exp, bias=0.0, scale=-0.5))
        VOP(lambda e: e.tensor_scalar(sb_sq[:, 0:D], sb_xtok[:, :], sb_tv[:, 0:1], sb_tv[:, 1:2],
                                      ALU.subtract, ALU.mult), waits=[(ss, ss.n)])
        VOP(lambda e: e.reduce_sum(sb_red[:, 4:5], sb_sq[:, 0:D], axis=AX.X))
        VOP(lambda e: e.tensor_scalar_mul(sb_tv[:, 2:3], sb_red[:, 4:5], 1.0 / D))
        VOP(lambda e: e.tensor_scalar(sb_sq2[:, 0:D], sb_sq[:, 0:D], sb_tv[:, 2:3], None, ALU.subtract))
        VOP(lambda e: e.tensor_mul(sb_sq2[:, 0:D], sb_sq2[:, 0:D], sb_sq2[:, 0:D]))
        VOP(lambda e: e.reduce_sum(sb_red[:, 5:6], sb_sq2[:, 0:D], axis=AX.X))
        VOP(lambda e: e.tensor_scalar_mul(sb_red[:, 6:7], sb_red[:, 5:6], 1.0 / D))
        SOP(lambda e: e.activation(sb_red[:, 7:8], sb_red[:, 6:7], AF.Ln, bias=sb_cst[:, 1:2], scale=1.0),
            waits=[(sv, sv.n), (sgp, gp_cst)])
        SOP(lambda e: e.activation(sb_tv[:, 3:4], sb_red[:, 7:8], AF.Exp, bias=0.0, scale=-0.5))
        VOP(lambda e: e.tensor_scalar(sb_sq2[:, 0:D], sb_sq[:, 0:D], sb_tv[:, 2:3], sb_tv[:, 3:4],
                                      ALU.subtract, ALU.mult), waits=[(ss, ss.n)])
        VOP(lambda e: e.tensor_reduce(sb_red[:, 8:9], sb_sq2[:, 0:D], AX.X, ALU.max,
                                      apply_absolute_value=True))
        VOP(lambda e: e.reciprocal(sb_red[:, 9:10], sb_red[:, 8:9]))
        VOP(lambda e: e.tensor_scalar_mul(sb_tv[:, 4:5], sb_red[:, 9:10], 127.0))
        VOP(lambda e: e.tensor_scalar_mul(sb_tv[:, 5:6], sb_red[:, 8:9], 1.0 / 127.0))
        for r in range(6):
            DMA("gpsimd", scr_tv[r:r + 1, tt * 128:(tt + 1) * 128], sb_tv[:, r:r + 1],
                s_scr, waits=[(sv, sv.n)])
    v_stats = sv.n
    scr_stats = s_scr.n

    if stop_after == "stats":
        _finish_early(prog, ctrs, out_tok, sb_sq, S, op, T, D, ndc)
        return _instantiate(nc, prog)

    op("gpsimd",
       lambda e: e.collective_compute("AllGather", ALU.bypass,
                                      replica_groups=[list(range(C))],
                                      ins=[scr_tv[0:2, :].opt()], outs=[agb_out.opt()]),
       waits=[(s_scr, scr_stats)], inc=scc_b, k=1)
    cc_agb = scc_b.n

    for dst, r in [(sb_mb, 0), (sb_rb, 1), (sb_m2b, 2), (sb_r2b, 3), (sb_ssb, 4), (sb_iv2, 5)]:
        DMA("gpsimd", dst[:, :], bass.AP(scr_tv_t, r * T, [[0, 128], [1, T]]),
            s_bco, waits=[(s_scr, scr_stats)])
    bco = s_bco.n

    if stop_after == "agb":
        _finish_early(prog, ctrs, out_tok, sb_sq, S, op, T, D, ndc)
        return _instantiate(nc, prog)

    for dc in range(ndc):
        DMA("sync", sb_sq[:, 0:160], xpwT[dc * 128:(dc + 1) * 128, :], s_x3,
            waits=[(sv, sv.n)])
        VOP(lambda e, dc=dc: e.tensor_copy(sb_xpw[:, dc * 160:(dc + 1) * 160], sb_sq[:, 0:160]),
            waits=[(s_x3, s_x3.n)])
    v_xpw = sv.n

    nxpj = T // NB
    xp_mm_dc = []
    for dc in range(ndc):
        DMA("sync", sb_xod[:, :], x_ownT[dc * 128:(dc + 1) * 128, :], s_x3,
            waits=[(sv, sv.n)] if dc > 0 else [(sv, v_xpw)])
        VOP(lambda e: e.tensor_sub(sb_xnTf[:, :], sb_xod[:, :], sb_mb[:, :]),
            waits=[(s_x3, s_x3.n), (s_bco, bco)])
        VOP(lambda e: e.tensor_mul(sb_xnTf[:, :], sb_xnTf[:, :], sb_rb[:, :]))
        par = dc % 2
        VOP(lambda e, par=par: e.tensor_copy(sb_xnb[par][:, :], sb_xnTf[:, :]),
            waits=[(stt, xp_mm_dc[dc - 2])] if dc >= 2 else [])
        v_xnb = sv.n
        for j in range(nxpj):
            op("tensor",
               lambda e, dc=dc, j=j, par=par: e.matmul(
                   ps[j][:, 0:NB], sb_xpw[:, dc * 160:dc * 160 + 128],
                   sb_xnb[par][:, j * NB:(j + 1) * NB],
                   start=(dc == 0), stop=(dc == ndc - 1)),
               waits=[(sv, v_xnb), (sv, v_xpw)] if j == 0 else ())
        for j in range(nxpj):
            op("tensor",
               lambda e, dc=dc, j=j, par=par: e.matmul(
                   ps[2 + j][0:32, 0:NB], sb_xpw[:, dc * 160 + 128:dc * 160 + 160],
                   sb_xnb[par][:, j * NB:(j + 1) * NB],
                   start=(dc == 0), stop=(dc == ndc - 1)),
               inc=(stt if j == nxpj - 1 else None))
        xp_mm_dc.append(stt.n)
        VOP(lambda e: e.tensor_sub(sb_xod[:, :], sb_xnTf[:, :], sb_m2b[:, :]))
        VOP(lambda e: e.tensor_mul(sb_xod[:, :], sb_xod[:, :], sb_r2b[:, :]))
        VOP(lambda e: e.tensor_mul(sb_xod[:, :], sb_xod[:, :], sb_ssb[:, :]))
        VOP(lambda e: e.tensor_scalar(sb_f16[:, :], sb_xod[:, :], 1536.0, None, ALU.add))
        VOP(lambda e, dc=dc: e.tensor_scalar(sb_kT[:, dc * T:(dc + 1) * T], sb_f16[:, :],
                                             1536.0, None, ALU.subtract))
    v_kT = sv.n
    mm_xp = stt.n
    SOP(lambda e: e.activation(sb_ag1[:, 0:NB], ps[0][:, 0:NB],
                               AF.Identity, bias=0.0, scale=1.0), waits=[(stt, mm_xp), (sv, v_kT)])
    if nxpj > 1:
        SOP(lambda e: e.activation(sb_ag1[:, NB:2 * NB], ps[1][:, 0:NB],
                                   AF.Identity, bias=0.0, scale=1.0))
    SOP(lambda e: e.activation(sb_ag2[0:32, 0:NB], ps[2][0:32, 0:NB],
                               AF.Identity, bias=0.0, scale=1.0))
    if nxpj > 1:
        SOP(lambda e: e.activation(sb_ag2[0:32, NB:2 * NB], ps[3][0:32, 0:NB],
                                   AF.Identity, bias=0.0, scale=1.0))
    ss_aga = ss.n
    DMA("gpsimd", aga_in[0:128, :], sb_ag1[:, :], s_scr, waits=[(ss, ss_aga)])
    DMA("gpsimd", aga_in[128:160, :], sb_ag2[0:32, :], s_scr)
    op("gpsimd",
       lambda e: e.collective_compute("AllGather", ALU.bypass,
                                      replica_groups=[list(range(C))],
                                      ins=[aga_in.opt()], outs=[aga_out.opt()]),
       waits=[(s_scr, s_scr.n)], inc=scc_a, k=1)
    cc_aga = scc_a.n

    if stop_after == "kx":
        _finish_early(prog, ctrs, out_tok, sb_sq, S, op, T, D, ndc)
        return _instantiate(nc, prog)

    dt_relu_ms = [0, 0]
    for dt in range(nds):
        for blk in range(C):
            DMA("sync", sb_dlow[:, 0:T], aga_out[blk * 160:blk * 160 + 128, :], s_ag,
                waits=[(scc_a, cc_aga)] + ([(stt, stt.n)] if (dt or blk) else []))
            for j in range(T // NB):
                it = (dt * C + blk) * (T // NB) + j
                bank = 2 + (it % 2)
                op("tensor",
                   lambda e, dt=dt, j=j, bank=bank: e.matmul(ps[bank][:, 0:NB],
                                                             sb_dtw[0:R, dt * 128:(dt + 1) * 128],
                                                             sb_dlow[0:R, j * NB:(j + 1) * NB],
                                                             start=True, stop=True),
                   waits=[(s_ag, s_ag.n), (sv, v_dtw), (ss, dt_relu_ms[it % 2]), (sv, sv.n)],
                   inc=stt)
                mm_dt = stt.n
                SOP(lambda e, dt=dt, bank=bank: e.activation(sb_sp1[:, 0:NB], ps[bank][:, 0:NB], AF.Abs,
                                                             bias=sb_dtbc[:, dt:dt + 1], scale=1.0),
                    waits=[(stt, mm_dt)])
                SOP(lambda e: e.activation(sb_sp2[:, 0:NB], sb_sp1[:, 0:NB], AF.Exp,
                                           bias=0.0, scale=-1.0))
                SOP(lambda e: e.activation(sb_sp1[:, 0:NB], sb_sp2[:, 0:NB], AF.Ln,
                                           bias=sb_cst[:, 2:3], scale=1.0))
                SOP(lambda e, dt=dt, bank=bank: e.activation(sb_sp3[:, 0:NB], ps[bank][:, 0:NB], AF.Relu,
                                                             bias=sb_dtbc[:, dt:dt + 1], scale=1.0),
                    waits=[(s_dts, s_dts.n)])
                dt_relu_ms[it % 2] = ss.n
                VOP(lambda e: e.tensor_add(sb_sp3[:, 0:NB], sb_sp3[:, 0:NB], sb_sp1[:, 0:NB]),
                    waits=[(ss, ss.n), (s_dts, s_dts.n)])
                DMA("gpsimd", scr_dtT[dt * 128:(dt + 1) * 128, blk * T + j * NB:blk * T + (j + 1) * NB],
                    sb_sp3[:, 0:NB], s_dts, waits=[(sv, sv.n)])
    dts_all = s_dts.n

    scan_chunk_ready = {}
    bc_last_use = [0, 0, 0]
    bc_loaded = {}

    def emit_scan_bc(dt, ch, n):
        ring = n % 3
        blk0 = (ch * TC) // T
        w = [(scc_a, cc_aga)] if (dt, ch, n) == (0, 0, 0) else [(sv, bc_last_use[ring])]
        row_b = (blk0 * 160 + 128 + n) * T
        row_c = (blk0 * 160 + 144 + n) * T
        DMA("gpsimd", sb_bc[ring][:, 0:TC].rearrange("p (b t) -> p b t", b=nblk_c),
            bass.AP(aga_out_t, row_b, [[0, 128], [160 * T, nblk_c], [1, T]]),
            s_bcs[ring], waits=w)
        DMA("gpsimd", sb_bc[ring][:, TC:2 * TC].rearrange("p (b t) -> p b t", b=nblk_c),
            bass.AP(aga_out_t, row_c, [[0, 128], [160 * T, nblk_c], [1, T]]),
            s_bcs[ring])
        bc_loaded[(dt, ch, n)] = s_bcs[ring].n

    def emit_scan_chunk_prep(dt, ch):
        DMA("gpsimd", sb_dtT[:, :], scr_dtT[dt * 128:(dt + 1) * 128, ch * TC:(ch + 1) * TC],
            s_dtl, waits=[(s_dts, dts_all), (sv, sv.n), (ss, ss.n)])
        DMA("gpsimd", sb_xdc[:, :], x_dT[dt * 128:(dt + 1) * 128, ch * TC:(ch + 1) * TC], s_dtl)
        blk0 = (ch * TC) // T
        for b in range(nblk_c):
            DMA("gpsimd", sb_mab[:, b * T:(b + 1) * T],
                bass.AP(agb_out_t, (2 * (blk0 + b)) * T, [[0, 128], [1, T]]), s_mrb,
                waits=[(scc_b, cc_agb), (sv, sv.n)] if b == 0 else [])
            DMA("gpsimd", sb_rab[:, b * T:(b + 1) * T],
                bass.AP(agb_out_t, (2 * (blk0 + b) + 1) * T, [[0, 128], [1, T]]), s_mrb)
        VOP(lambda e: e.tensor_sub(sb_xdc[:, :], sb_xdc[:, :], sb_mab[:, :]),
            waits=[(s_dtl, s_dtl.n), (s_mrb, s_mrb.n)])
        VOP(lambda e: e.tensor_mul(sb_xdc[:, :], sb_xdc[:, :], sb_rab[:, :]))
        VOP(lambda e: e.tensor_mul(sb_dtx[:, :], sb_dtT[:, :], sb_xdc[:, :]))
        scan_chunk_ready[(dt, ch)] = sv.n
        emit_scan_bc(dt, ch, 0)
        emit_scan_bc(dt, ch, 1)
        emit_scan_exp(dt, ch, 0)
        emit_scan_exp(dt, ch, 1)

    scan_exp_ms = {}
    scan_done_ms = {}

    def emit_scan_exp(dt, ch, n):
        slot = n % 3
        w = [(sv, scan_chunk_ready[(dt, ch)])]
        prev = (dt, ch, n - 3)
        if prev in scan_done_ms:
            w.append((sv, scan_done_ms[prev]))
        SOP(lambda e, dt=dt, n=n, slot=slot: e.activation(
            sb_ab3[slot][:, :], sb_dtT[:, :], AF.Exp, bias=0.0,
            scale=sb_acol[:, dt * N + n:dt * N + n + 1]), waits=w)
        scan_exp_ms[(dt, ch, n)] = ss.n

    def emit_scan_group(dt, ch, n):
        ring = n % 3
        slot = n % 3
        ahead = n + 2
        if ahead < N:
            if (dt, ch, ahead) not in bc_loaded:
                emit_scan_bc(dt, ch, ahead)
            if (dt, ch, ahead) not in scan_exp_ms:
                emit_scan_exp(dt, ch, ahead)
        VOP(lambda e, ring=ring: e.tensor_mul(sb_bx[:, :], sb_dtx[:, :], sb_bc[ring][:, 0:TC]),
            waits=[(s_bcs[ring], bc_loaded[(dt, ch, n)])])
        VOP(lambda e, dt=dt, n=n, slot=slot: e.tensor_tensor_scan(
            sb_h[:, :], sb_ab3[slot][:, :], sb_bx[:, :],
            sb_hst[:, dt * N + n:dt * N + n + 1], ALU.mult, ALU.add),
            waits=[(ss, scan_exp_ms[(dt, ch, n)]), (sgp, gp_hst0)])
        scan_done_ms[(dt, ch, n)] = sv.n
        VOP(lambda e, dt=dt, n=n: e.tensor_copy(sb_hst[:, dt * N + n:dt * N + n + 1],
                                                sb_h[:, TC - 1:TC]))
        VOP(lambda e, ring=ring: e.tensor_mul(sb_ytmp[:, :], sb_h[:, :], sb_bc[ring][:, TC:2 * TC]))
        if n == 0:
            VOP(lambda e: e.tensor_copy(sb_yacc[:, :], sb_ytmp[:, :]),
                waits=[(s_y, s_y.n)])
        else:
            VOP(lambda e: e.tensor_add(sb_yacc[:, :], sb_yacc[:, :], sb_ytmp[:, :]))
        bc_last_use[ring] = sv.n

    def emit_scan_chunk_finish(dt, ch):
        blk0 = (ch * TC) // T
        for b in range(nblk_c):
            DMA("gpsimd", y_in[(blk0 + b) * ds + dt * 128:(blk0 + b) * ds + (dt + 1) * 128, :],
                sb_yacc[:, b * T:(b + 1) * T], s_y, waits=[(sv, sv.n)] if b == 0 else [])

    scan_steps = []
    for dt in range(nds):
        for ch in range(nch):
            scan_steps.append(("prep", dt, ch, 0))
            for n in range(N):
                scan_steps.append(("group", dt, ch, n))
            scan_steps.append(("finish", dt, ch, 0))
    scan_pos = [0]

    def emit_some_scan(k):
        if stop_after == "upnoscan":
            return
        done = 0
        while done < k and scan_pos[0] < len(scan_steps):
            kind, dt, ch, n = scan_steps[scan_pos[0]]
            if kind == "prep":
                emit_scan_chunk_prep(dt, ch)
            elif kind == "group":
                emit_scan_group(dt, ch, n)
            else:
                emit_scan_chunk_finish(dt, ch)
            scan_pos[0] += 1
            done += 1

    total_up_slots = nhr * nht_r * ntb
    scan_per_slot = max(1, -(-len(scan_steps) // total_up_slots))

    if stop_after == "p0":
        _finish_early(prog, ctrs, out_tok, sb_sq, S, op, T, D, ndc)
        return _instantiate(nc, prog)

    up_psum_free = {0: 0, 1: 0}
    up_hr_ms = []
    wf_sign_ms = [0, 0]
    q_dma_ms = [0, 0]
    ev_user_ms = [0, 0]
    for hr in range(nhr):
        spar = hr % 2
        for dc in range(ndc):
            for gm, (wsrc, sgl) in enumerate([(wgT, sb_sgn_g), (wuT, sb_sgn_u)]):
                sgt = sgl[spar]
                par = (dc * 2 + gm) % 2
                w = [(ss, wf_sign_ms[par])]
                if hr >= 2 and dc == 0 and gm == 0:
                    w.append((stt, up_hr_ms[hr - 2]))
                DMA("sync", sb_wf[par][:, :], wsrc[dc * 128:(dc + 1) * 128, hr * HRS:(hr + 1) * HRS],
                    s_wf[par], waits=w)
                SOP(lambda e, par=par, dc=dc, sgt=sgt: e.activation(
                    sgt[:, dc * HRS:(dc + 1) * HRS], sb_wf[par][:, :], AF.Sign,
                    bias=0.0, scale=1.0),
                    waits=[(s_wf[par], s_wf[par].n)])
                wf_sign_ms[par] = ss.n
        sgn_done = ss.n
        for ht in range(nht_r):
            for tb in range(ntb):
                hg = hr * nht_r + ht
                slot = (ht * ntb + tb) % 2
                gb, ub = 2 * slot, 2 * slot + 1
                tw = [(ss, sgn_done), (sv, v_kT), (ss, up_psum_free[slot])]
                for dc in range(ndc):
                    op("tensor",
                       lambda e, dc=dc, ht=ht, tb=tb, gb=gb, spar=spar: e.matmul(
                           ps[gb][:, 0:NB], sb_sgn_g[spar][:, dc * HRS + ht * 128:dc * HRS + ht * 128 + 128],
                           sb_kT[:, dc * T + tb * NB:dc * T + (tb + 1) * NB],
                           start=(dc == 0), stop=(dc == ndc - 1)),
                       waits=tw if dc == 0 else ())
                for dc in range(ndc):
                    last = dc == ndc - 1
                    op("tensor",
                       lambda e, dc=dc, ht=ht, tb=tb, ub=ub, last=last, spar=spar: e.matmul(
                           ps[ub][:, 0:NB], sb_sgn_u[spar][:, dc * HRS + ht * 128:dc * HRS + ht * 128 + 128],
                           sb_kT[:, dc * T + tb * NB:dc * T + (tb + 1) * NB],
                           start=(dc == 0), stop=last),
                       inc=(stt if last else None))
                mm_up = stt.n
                SOP(lambda e, gb=gb, slot=slot: e.activation(sb_evg[slot][:, :], ps[gb][:, 0:NB],
                                                             AF.Identity, bias=0.0, scale=1.0),
                    waits=[(stt, mm_up), (sv, ev_user_ms[slot])])
                SOP(lambda e, ub=ub, slot=slot: e.activation(sb_evu[slot][:, :], ps[ub][:, 0:NB],
                                                             AF.Identity, bias=0.0, scale=1.0))
                up_psum_free[slot] = ss.n
                ev_ss = ss.n
                VOP(lambda e, tb=tb, slot=slot: e.tensor_tensor(sb_ev1[:, :], sb_evg[slot][:, :],
                                                                sb_iv2[:, tb * NB:(tb + 1) * NB], ALU.mult),
                    waits=[(ss, ev_ss), (s_bco, bco)])
                SOP(lambda e: e.activation(sb_ev2[:, :], sb_ev1[:, :], AF.Tanh, bias=0.0, scale=0.5),
                    waits=[(sv, sv.n)])
                VOP(lambda e: e.tensor_scalar(sb_ev3[:, :], sb_ev2[:, :], 0.5, 0.5, ALU.mult, ALU.add),
                    waits=[(ss, ss.n)])
                VOP(lambda e: e.tensor_mul(sb_ev4[:, :], sb_ev1[:, :], sb_ev3[:, :]))
                qp = slot
                VOP(lambda e, qp=qp, slot=slot: e.tensor_mul(sb_q[qp][:, :], sb_ev4[:, :], sb_evu[slot][:, :]),
                    waits=[(s_p[qp], q_dma_ms[qp])])
                ev_user_ms[slot] = sv.n
                VOP(lambda e, tb=tb, qp=qp: e.tensor_mul(sb_q[qp][:, :], sb_q[qp][:, :],
                                                         sb_iv2[:, tb * NB:(tb + 1) * NB]))
                DMA("sync", p_scr[hg * 128:(hg + 1) * 128, tb * NB:(tb + 1) * NB], sb_q[qp][:, :],
                    s_p[qp], waits=[(sv, sv.n)])
                q_dma_ms[qp] = s_p[qp].n
                emit_some_scan(scan_per_slot)
        up_hr_ms.append(stt.n)
    emit_some_scan(len(scan_steps))
    p_all = (s_p[0].n, s_p[1].n)
    y_all = s_y.n

    if stop_after == "up":
        _finish_early(prog, ctrs, out_tok, sb_sq, S, op, T, D, ndc)
        return _instantiate(nc, prog)
    if stop_after == "upnoscan":
        _finish_early(prog, ctrs, out_tok, sb_sq, S, op, T, D, ndc)
        return _instantiate(nc, prog)

    def cc_y_fn(e):
        with allow_ici_all_to_all():
            return e.collective_compute("AllToAll", ALU.bypass,
                                        replica_groups=[list(range(C))],
                                        ins=[y_in.opt()], outs=[y_out.opt()])
    op("gpsimd", cc_y_fn, waits=[(s_y, y_all)], inc=scc_y, k=1)
    cc_y = scc_y.n

    dn_ms = []
    for oq in range(noq):
        for hh in range(nh // 2):
            par = hh % 2
            gi = oq * (nh // 2) + hh
            w_slot = [(stt, dn_ms[gi - 2])] if gi >= 2 else []
            DMA("sync",
                sb_wdf[par][:, :].rearrange("p (a f) -> p a f", a=2),
                wdT[hh * 256:(hh + 1) * 256, oq * OQ:(oq + 1) * OQ].rearrange("(a p) f -> p a f", p=128),
                s_wd[par], waits=[(sv, sv.n)] + w_slot)
            DMA("sync",
                sb_pt[par][:, :].rearrange("p (a f) -> p a f", a=2),
                p_scr[hh * 256:(hh + 1) * 256, :].rearrange("(a p) f -> p a f", p=128),
                s_pl[par],
                waits=([(s_p[0], p_all[0]), (s_p[1], p_all[1])] if gi == 0 else []) + w_slot)
            VOP(lambda e, par=par: e.tensor_scalar(sb_wds[par][:, :], sb_wdf[par][:, :],
                                                   0.0, 0.5, ALU.is_ge, ALU.subtract),
                waits=[(s_wd[par], s_wd[par].n)])
            sgn_d = sv.n
            for hi in range(2):
                h = hh * 2 + hi
                for ot in range(no_t):
                    for tb in range(ntb):
                        bank = ot * ntb + tb
                        w = []
                        if hi == 0 and ot == 0 and tb == 0:
                            w = [(sv, sgn_d), (s_pl[par], s_pl[par].n)]
                            if h == 0 and oq > 0:
                                w.append((ss, ss.n))
                        op("tensor",
                           lambda e, par=par, hi=hi, ot=ot, tb=tb, h=h, bank=bank: e.matmul(
                               ps[bank][:, 0:NB],
                               sb_wds[par][:, hi * OQ + ot * 128:hi * OQ + (ot + 1) * 128],
                               sb_pt[par][:, hi * T + tb * NB:hi * T + (tb + 1) * NB],
                               start=(h == 0), stop=(h == nh - 1)),
                           waits=w,
                           inc=(stt if (hi == 1 and ot == no_t - 1 and tb == ntb - 1) else None))
            dn_ms.append(stt.n)
        mm_dn = stt.n
        for ot in range(no_t):
            for tb in range(ntb):
                bank = ot * ntb + tb
                oc = oq * no_t + ot
                fpar = bank % 2
                SOP(lambda e, bank=bank, fpar=fpar: e.activation(
                    sb_fs[fpar][:, :], ps[bank][:, 0:NB],
                    AF.Identity, bias=0.0, scale=2.0),
                    waits=[(stt, mm_dn), (s_fsp[fpar], fs_slot_ms[fpar])])
                DMA("gpsimd", ffn_scr[oc * 128:(oc + 1) * 128, tb * NB:(tb + 1) * NB],
                    sb_fs[fpar][:, :], s_fsp[fpar], waits=[(ss, ss.n)])
                fs_slot_ms[fpar] = s_fsp[fpar].n
    ss_dn = ss.n

    if stop_after == "down":
        _finish_early(prog, ctrs, out_tok, sb_sq, S, op, T, D, ndc)
        return _instantiate(nc, prog)

    GOP(lambda e: e.memset(sb_ident[:, :], 1.0), waits=[(sv, sv.n), (ss, ss.n)])
    GOP(lambda e: e.affine_select(sb_ident[:, :], sb_ident[:, :], [[1, 128]],
                                  ALU.is_equal, 0.0, base=0, channel_multiplier=-1),
        waits=[(sgp, sgp.n)])
    gp_ident = sgp.n
    out_slot_ms = [0, 0]
    for dc in range(ndc):
        DMA("sync", sb_xod[:, :], x_ownT[dc * 128:(dc + 1) * 128, :], s_x3,
            waits=[(sv, sv.n), (ss, ss.n)] if dc == 0 else [(sv, sv.n)])
        DMA("sync", sb_ycol[:, :], y_out[dc * 128:(dc + 1) * 128, :], s_x3,
            waits=[(scc_y, cc_y), (ss, ss.n)] if dc == 0 else [])
        DMA("sync", sb_fload[:, :], ffn_scr[dc * 128:(dc + 1) * 128, :], s_x3,
            waits=[(s_fsp[0], s_fsp[0].n), (s_fsp[1], s_fsp[1].n)] if dc == 0 else [])
        x3n = s_x3.n
        VOP(lambda e: e.tensor_sub(sb_xnTf[:, :], sb_xod[:, :], sb_mb[:, :]),
            waits=[(s_x3, x3n), (stt, stt.n)])
        VOP(lambda e: e.tensor_mul(sb_xnTf[:, :], sb_xnTf[:, :], sb_rb[:, :]))
        VOP(lambda e, dc=dc: e.scalar_tensor_tensor(sb_acc[:, :], sb_xnTf[:, :],
                                                    sb_dpb[:, dc:dc + 1], sb_xod[:, :],
                                                    ALU.mult, ALU.add))
        VOP(lambda e: e.tensor_add(sb_acc[:, :], sb_acc[:, :], sb_ycol[:, :]))
        VOP(lambda e: e.tensor_add(sb_acc[:, :], sb_acc[:, :], sb_fload[:, :]),
            waits=[(ss, ss_dn)])
        acc_done = sv.n
        for tt in range(ntt):
            pslot = tt % 2
            op("tensor",
               lambda e, tt=tt, pslot=pslot: e.transpose(ps[pslot][:, 0:128],
                                                         sb_acc[:, tt * 128:(tt + 1) * 128],
                                                         sb_ident[:, :]),
               waits=[(sv, acc_done), (sgp, gp_ident)] +
                     ([(ss, ss.n)] if (dc * ntt + tt) >= 2 else []),
               inc=stt)
            tr_done = stt.n
            SOP(lambda e, pslot=pslot: e.copy(sb_ot[pslot][:, :], ps[pslot][:, 0:128]),
                waits=[(stt, tr_done), (s_out, out_slot_ms[pslot])])
            DMA("sync", out_tok[tt * 128:(tt + 1) * 128, dc * 128:(dc + 1) * 128],
                sb_ot[pslot][:, :], s_out, waits=[(ss, ss.n)])
            out_slot_ms[pslot] = s_out.n
    prog["sync"].append(lambda eng, h=s_out.h, v=s_out.n: eng.wait_ge(h, v))
    prog["gpsimd"].append(lambda eng, h=s_out.h, v=s_out.n: eng.wait_ge(h, v))

    return _instantiate(nc, prog)


def _finish_early(prog, ctrs, out_tok, sb_src, S, op, T, D, ndc):
    s_out = ctrs["s_out"]
    sv = ctrs["sv"]
    ss = ctrs["ss"]
    W = min(D, sb_src.shape[1])
    for tt in range(T // 128):
        for c0 in range(0, D, W):
            op("sync", lambda eng, tt=tt, c0=c0: eng.dma_start(
                out=out_tok[tt * 128:(tt + 1) * 128, c0:c0 + W],
                in_=sb_src[:, 0:W]), waits=[(sv, sv.n), (ss, ss.n)] if (tt == 0 and c0 == 0) else [],
                inc=s_out, k=16)
    prog["sync"].append(lambda eng, h=s_out.h, v=s_out.n: eng.wait_ge(h, v))
    prog["gpsimd"].append(lambda eng, h=s_out.h, v=s_out.n: eng.wait_ge(h, v))


def _instantiate(nc, prog):
    with nc.Block() as block:
        @block.sync
        def _(eng):
            for fn in prog["sync"]:
                fn(eng)

        @block.gpsimd
        def _(eng):
            for fn in prog["gpsimd"]:
                fn(eng)

        @block.tensor
        def _(eng):
            for fn in prog["tensor"]:
                fn(eng)

        @block.vector
        def _(eng):
            for fn in prog["vector"]:
                fn(eng)

        @block.scalar
        def _(eng):
            for fn in prog["scalar"]:
                fn(eng)

    return nc


def make_in_maps(cfg, inputs):
    D, H, L, N, R, C = cfg["D"], cfg["H"], cfg["L"], cfg["N"], cfg["R"], cfg["C"]
    T = L // C
    ds = D // C
    x = np.ascontiguousarray(np.asarray(inputs["L0_attn_x"], dtype=np.float32)[0])
    xpw = np.ascontiguousarray(np.asarray(inputs["L0_attn_x_proj_w"], np.float32).T)
    dtpw = np.ascontiguousarray(np.asarray(inputs["L0_attn_dt_proj_w"], np.float32).T)
    dtbv = np.asarray(inputs["L0_attn_dt_proj_b"], np.float32)
    al = np.asarray(inputs["L0_attn_A_log"], np.float32)
    dpv = np.ascontiguousarray(np.asarray(inputs["L0_attn_D_param"], np.float32).reshape(D // 128, 128).T)
    wg = np.ascontiguousarray(np.asarray(inputs["L0_ffn_W_g"], np.float32).T)
    wu = np.ascontiguousarray(np.asarray(inputs["L0_ffn_W_u"], np.float32).T)
    wd = np.ascontiguousarray(np.asarray(inputs["L0_ffn_W_d"], np.float32).T)
    xT = np.ascontiguousarray(x.T)
    in_maps = []
    for c in range(C):
        in_maps.append({
            "x_tok": np.ascontiguousarray(x[c * T:(c + 1) * T, :]),
            "x_ownT": np.ascontiguousarray(xT[:, c * T:(c + 1) * T]),
            "x_dT": np.ascontiguousarray(xT[c * ds:(c + 1) * ds, :]),
            "xpwT": xpw,
            "dtpwT": np.ascontiguousarray(dtpw[:, c * ds:(c + 1) * ds]),
            "dtb": np.ascontiguousarray(dtbv[c * ds:(c + 1) * ds].reshape(ds // 128, 128).T),
            "alog": np.ascontiguousarray(
                al[c * ds:(c + 1) * ds, :].reshape(ds // 128, 128, N).transpose(1, 0, 2).reshape(128, -1)),
            "dp": dpv,
            "wgT": wg,
            "wuT": wu,
            "wdT": wd,
        })
    return in_maps


def kernel(**inputs):
    cfg = FULL_CFG
    nc = build_nc(cfg)
    in_maps = make_in_maps(cfg, inputs)
    res = run_bass_kernel_spmd(nc, in_maps, core_ids=list(range(cfg["C"])))
    out = np.concatenate([np.asarray(r["out_tok"]) for r in res.results], axis=0)
    return out[None].astype(np.float32)


# version 11
# speedup vs baseline: 1.2754x; 1.2512x over previous
import numpy as np
import concourse.bass as bass
import concourse.mybir as mybir
from concourse.bass_utils import run_bass_kernel_spmd
from concourse.aot_env import allow_ici_all_to_all

F32 = mybir.dt.float32
BF16 = mybir.dt.bfloat16
F16 = mybir.dt.float16
AF = mybir.ActivationFunctionType
ALU = mybir.AluOpType
AX = mybir.AxisListType

FULL_CFG = dict(D=2048, H=8192, L=8192, N=16, R=128, C=8)


class Ctr:
    def __init__(self, handle):
        self.h = handle
        self.n = 0


def build_nc(cfg):
    stop_after = cfg.get("stop_after", "full")
    D, H, L, N, R, C = cfg["D"], cfg["H"], cfg["L"], cfg["N"], cfg["R"], cfg["C"]
    T = L // C
    ds = D // C
    nds = ds // 128
    ndc = D // 128
    NB = min(512, T)
    ntb = T // NB
    HRS = min(256, H)
    nhr = H // HRS
    nht_r = HRS // 128
    nh = H // 128
    OQ = min(512, D)
    noq = D // OQ
    no_t = OQ // 128
    TC = min(2048, L)
    nch = L // TC
    nblk_c = max(1, TC // T)
    ntt = T // 128
    assert TC % T == 0 and R == 128 and ds % 128 == 0

    nc = bass.Bass("TRN2", target_bir_lowering=False, detect_race_conditions=False)

    x_tok = nc.dram_tensor("x_tok", [T, D], F32, kind="ExternalInput").ap()
    x_ownT = nc.dram_tensor("x_ownT", [D, T], F32, kind="ExternalInput").ap()
    x_dT = nc.dram_tensor("x_dT", [ds, L], F32, kind="ExternalInput").ap()
    xpwT = nc.dram_tensor("xpwT", [D, 160], F32, kind="ExternalInput").ap()
    dtpwT = nc.dram_tensor("dtpwT", [R, ds], F32, kind="ExternalInput").ap()
    dtb = nc.dram_tensor("dtb", [128, nds], F32, kind="ExternalInput").ap()
    alog = nc.dram_tensor("alog", [128, nds * N], F32, kind="ExternalInput").ap()
    dp = nc.dram_tensor("dp", [128, D // 128], F32, kind="ExternalInput").ap()
    wgT_t = nc.dram_tensor("wgT", [D, H], F32, kind="ExternalInput")
    wuT_t = nc.dram_tensor("wuT", [D, H], F32, kind="ExternalInput")
    wdT_t = nc.dram_tensor("wdT", [H, D], F32, kind="ExternalInput")
    wgT = wgT_t.ap()
    wuT = wuT_t.ap()
    wdT = wdT_t.ap()
    out_tok = nc.dram_tensor("out_tok", [T, D], F32, kind="ExternalOutput").ap()

    scr_tv_t = nc.dram_tensor("scr_tv", [8, T], F32, kind="Internal")
    aga_in_t = nc.dram_tensor("aga_in", [160, T], BF16, kind="Internal")
    aga_out_t = nc.dram_tensor("aga_out", [160 * C, T], BF16, kind="Internal")
    agb_out_t = nc.dram_tensor("agb_out", [2 * C, T], F32, kind="Internal")
    scr_dtT_t = nc.dram_tensor("scr_dtT", [ds, L], F32, kind="Internal")
    p_scr_t = nc.dram_tensor("p_scr", [H, T], BF16, kind="Internal")
    ffn_scr_t = nc.dram_tensor("ffn_scr", [D, T], BF16, kind="Internal")
    y_in_t = nc.dram_tensor("y_a2a_in", [C * ds, T], BF16, kind="Internal")
    y_out_t = nc.dram_tensor("y_a2a_out", [C * ds, T], BF16, kind="Internal")
    scr_tv = scr_tv_t.ap()
    aga_in = aga_in_t.ap()
    aga_out = aga_out_t.ap()
    agb_out = agb_out_t.ap()
    scr_dtT = scr_dtT_t.ap()
    p_scr = p_scr_t.ap()
    ffn_scr = ffn_scr_t.ap()
    y_in = y_in_t.ap()
    y_out = y_out_t.ap()

    A = nc.alloc_sbuf_tensor
    AAT = nc.alloc_sbuf_tensor_at

    def addr_of(h):
        return nc.lookup_mloc(h).addr

    sb_xtok = A("sb_xtok", [128, D], F32)
    sb_sq = A("sb_sq", [128, D], F32)
    sb_sq2 = A("sb_sq2", [128, D], F32)
    sb_p0a = A("sb_p0a", [128, 3072], F32)
    sb_tv = A("sb_tv", [128, 8], F32)
    sb_red = A("sb_red", [128, 12], F32)
    sb_xod = A("sb_xod", [128, T], F32)
    sb_xnTf = A("sb_xnTf", [128, T], F32)
    sb_xnb = [A(f"sb_xnb{i}", [128, T], BF16) for i in range(2)]
    sb_kT = A("sb_kT", [128, ndc * T], BF16)
    sb_mb = A("sb_mb", [128, T], F32)
    sb_rb = A("sb_rb", [128, T], F32)
    sb_m2b = A("sb_m2b", [128, T], F32)
    sb_r2b = A("sb_r2b", [128, T], F32)
    sb_ssb = A("sb_ssb", [128, T], F32)
    sb_iv2 = A("sb_iv2", [128, T], BF16)
    sb_xpw = A("sb_xpw", [128, ndc * 160], BF16)
    sb_dtw = A("sb_dtw", [128, ds], BF16)
    sb_ag2 = A("sb_ag2", [32, T], BF16)
    sb_dtbc = A("sb_dtbc", [128, nds], F32)
    sb_acol = A("sb_acol", [128, nds * N], F32)
    sb_sgn_g = [A(f"sb_sgn_g{i}", [128, ndc * HRS], BF16) for i in range(2)]
    sb_sgn_u = [A(f"sb_sgn_u{i}", [128, ndc * HRS], BF16) for i in range(2)]
    sb_q = [A(f"sb_q{i}", [128, NB], BF16) for i in range(2)]
    sb_evg = [A(f"sb_evg{i}", [128, NB], BF16) for i in range(2)]
    sb_evu = [A(f"sb_evu{i}", [128, NB], BF16) for i in range(2)]
    sb_ev1 = A("sb_ev1", [128, NB], BF16)
    sb_ev2 = A("sb_ev2", [128, NB], BF16)
    sb_ev3 = A("sb_ev3", [128, NB], BF16)
    sb_ev4 = A("sb_ev4", [128, NB], BF16)
    sb_dtT = A("sb_dtT", [128, TC], F32)
    sb_xdc = A("sb_xdc", [128, TC], F32)
    sb_h = A("sb_h", [128, TC], BF16)
    sb_bc = [A(f"sb_bc{i}", [128, 2 * TC], BF16) for i in range(3)]
    sb_hst = A("sb_hst", [128, max(1, nds * N)], F32)
    sb_ytmp = A("sb_ytmp", [128, TC], BF16)
    sb_yacc = A("sb_yacc", [128, TC], BF16)

    sb_dpb = A("sb_dpb", [128, ndc], F32)
    sb_cst = A("sb_cst", [128, 4], F32)
    a_xtok = addr_of(sb_xtok)
    sb_pt = [AAT(f"sb_pt{i}", [128, 2 * T], BF16, offset=a_xtok + i * (4 * T)) for i in range(2)]
    a_sq = addr_of(sb_sq)
    sb_wds = None
    a_sq2 = addr_of(sb_sq2)
    sb_mab = AAT("sb_mab", [128, TC], BF16, offset=a_sq2)
    sb_rab = AAT("sb_rab", [128, TC], BF16, offset=a_sq2 + 2 * TC)
    a_p0a = addr_of(sb_p0a)
    sb_sp1 = AAT("sb_sp1", [128, NB], F32, offset=a_p0a)
    sb_sp2 = AAT("sb_sp2", [128, NB], F32, offset=a_p0a + 4 * NB)
    sb_sp3 = AAT("sb_sp3", [128, NB], F32, offset=a_p0a + 8 * NB)
    sb_f16 = AAT("sb_f16", [128, T], F16, offset=a_p0a + 12 * NB)
    sb_dlow = AAT("sb_dlow", [128, T], BF16, offset=a_p0a + 12 * NB + 2 * T)
    sb_ag1 = AAT("sb_ag1", [128, T], BF16, offset=a_p0a + 12 * NB + 4 * T)
    sb_dtx = AAT("sb_dtx", [128, TC], BF16, offset=a_p0a)
    sb_ab3 = None
    sb_bx = AAT("sb_bx", [128, TC], BF16, offset=a_p0a + 4 * TC)
    a_dtT = addr_of(sb_dtT)
    sb_acc = AAT("sb_acc", [128, T], F32, offset=a_dtT)
    sb_ycol = AAT("sb_ycol", [128, T], BF16, offset=a_dtT + 4 * T)
    sb_fload = AAT("sb_fload", [128, T], BF16, offset=a_dtT + 6 * T)
    a_xdc = addr_of(sb_xdc)
    sb_ot = [AAT(f"sb_ot{i}", [128, 128], F32, offset=a_xdc + i * 512) for i in range(2)]
    sb_ident = AAT("sb_ident", [128, 128], F32, offset=a_xdc + 1024)
    def _fit(name, shape, dtype, host, host_bytes):
        nbytes = shape[1] * (2 if dtype in (BF16, F16) else 4)
        if nbytes <= host_bytes:
            return AAT(name, shape, dtype, offset=addr_of(host))
        return A(name, shape, dtype)
    sb_ab3 = [AAT("sb_ab0", [128, TC], BF16, offset=a_p0a + 2 * TC),
              _fit("sb_ab1", [128, TC], BF16, sb_xod, 4 * T),
              _fit("sb_ab2", [128, TC], BF16, sb_xnTf, 4 * T)]
    sb_wds = [_fit(f"sb_wds{i}", [128, 2 * OQ], BF16, sb_xnb[i], 2 * T) for i in range(2)]
    sb_fs = [AAT(f"sb_fs{i}", [128, NB], BF16, offset=addr_of(sb_q[i])) for i in range(2)]
    ps = [nc.alloc_psum_tensor(f"ps{i}", [128, 512], F32) for i in range(8)]

    prog = {e: [] for e in ["sync", "gpsimd", "tensor", "vector", "scalar"]}
    ctrs = {}
    sem_keep = []

    def S(name):
        c = ctrs.get(name)
        if c is None:
            g = nc.semaphore(name)
            h = g.__enter__()
            sem_keep.append(g)
            c = Ctr(h)
            ctrs[name] = c
        return c

    self_ctr = {}

    def op(e, fn, waits=(), inc=None, k=1):
        own = self_ctr.get(e)
        wl = [(c.h, v) for (c, v) in waits if v > 0 and c is not own]
        dd = {}
        for h, v in wl:
            dd[h] = max(dd.get(h, 0), v)
        wl = list(dd.items())
        if inc is not None:
            inc.n += k

        def run(eng, fn=fn, wl=wl, inc=inc, k=k):
            for h, v in wl:
                eng.wait_ge(h, v)
            ins = fn(eng)
            if inc is not None:
                ins.then_inc(inc.h, k)
        prog[e].append(run)

    sv = S("sv")
    ss = S("ss")
    stt = S("stt")
    sgp = S("sgp")
    scc_b = S("scc_b")
    scc_a = S("scc_a")
    scc_y = S("scc_y")
    s_in = S("s_in")
    s_x3 = S("s_x3")
    s_tv = S("s_tv")
    s_scr = S("s_scr")
    s_bco = S("s_bco")
    s_ag = S("s_ag")
    s_dts = S("s_dts")
    s_dtl = S("s_dtl")
    s_bcs = [S("s_bc0"), S("s_bc1"), S("s_bc2")]
    s_mrb = S("s_mrb")
    s_wf = [S("s_wf0"), S("s_wf1")]
    s_p = [S("s_p0"), S("s_p1")]
    s_pl = [S("s_pl0"), S("s_pl1")]
    s_wd = [S("s_wd0"), S("s_wd1")]
    s_y = S("s_y")
    s_fsp = [S("s_fsp0"), S("s_fsp1")]
    fs_slot_ms = [0, 0]
    s_out = S("s_out")

    self_ctr.update({"tensor": stt})

    def VOP(fn, waits=()):
        op("vector", fn, waits=list(waits) + [(sv, sv.n)], inc=sv)

    def SOP(fn, waits=()):
        op("scalar", fn, waits=list(waits) + [(ss, ss.n)], inc=ss)

    def GOP(fn, waits=()):
        op("gpsimd", fn, waits=list(waits) + [(sgp, sgp.n)], inc=sgp)

    def DMA(e, out_ap, in_ap, ctr, waits=()):
        op(e, lambda eng, o=out_ap, i=in_ap: eng.dma_start(out=o, in_=i),
           waits=waits, inc=ctr, k=16)

    DMA("sync", sb_dtbc[:, :], dtb, s_in)
    DMA("sync", sb_acol[:, :], alog, s_in)
    DMA("sync", sb_dpb[:, :], dp, s_in)
    DMA("sync", sb_sq[0:R, 0:ds], dtpwT, s_in)
    in0 = s_in.n
    VOP(lambda e: e.tensor_copy(sb_dtw[0:R, :], sb_sq[0:R, 0:ds]), waits=[(s_in, in0)])
    v_dtw = sv.n
    SOP(lambda e: e.activation(sb_acol[:, :], sb_acol[:, :], AF.Exp, bias=0.0, scale=1.0),
        waits=[(s_in, in0)])
    VOP(lambda e: e.tensor_scalar_mul(sb_acol[:, :], sb_acol[:, :], -1.0), waits=[(ss, ss.n)])
    v_acol = sv.n
    GOP(lambda e: e.memset(sb_cst[:, 0:1], 1e-5))
    GOP(lambda e: e.memset(sb_cst[:, 1:2], 1e-8))
    GOP(lambda e: e.memset(sb_cst[:, 2:3], 1.0))
    gp_cst = sgp.n
    GOP(lambda e: e.memset(sb_hst[:, :], 0.0))
    gp_hst0 = sgp.n

    if stop_after == "params":
        _finish_early(prog, ctrs, out_tok, sb_sq, S, op, T, D, ndc)
        return _instantiate(nc, prog)

    for tt in range(ntt):
        DMA("sync", sb_xtok[:, :], x_tok[tt * 128:(tt + 1) * 128, :], s_tv,
            waits=[(sv, sv.n)] if tt > 0 else [])
        VOP(lambda e: e.reduce_sum(sb_red[:, 0:1], sb_xtok[:, :], axis=AX.X),
            waits=[(s_tv, s_tv.n), (s_scr, s_scr.n)])
        VOP(lambda e: e.tensor_scalar_mul(sb_tv[:, 0:1], sb_red[:, 0:1], 1.0 / D))
        VOP(lambda e: e.tensor_scalar(sb_sq[:, 0:D], sb_xtok[:, :], sb_tv[:, 0:1], None, ALU.subtract))
        VOP(lambda e: e.tensor_mul(sb_sq2[:, 0:D], sb_sq[:, 0:D], sb_sq[:, 0:D]))
        VOP(lambda e: e.reduce_sum(sb_red[:, 1:2], sb_sq2[:, 0:D], axis=AX.X))
        VOP(lambda e: e.tensor_scalar_mul(sb_red[:, 2:3], sb_red[:, 1:2], 1.0 / D))
        SOP(lambda e: e.activation(sb_red[:, 3:4], sb_red[:, 2:3], AF.Ln, bias=sb_cst[:, 0:1], scale=1.0),
            waits=[(sv, sv.n), (sgp, gp_cst)])
        SOP(lambda e: e.activation(sb_tv[:, 1:2], sb_red[:, 3:4], AF.Exp, bias=0.0, scale=-0.5))
        VOP(lambda e: e.tensor_scalar(sb_sq[:, 0:D], sb_xtok[:, :], sb_tv[:, 0:1], sb_tv[:, 1:2],
                                      ALU.subtract, ALU.mult), waits=[(ss, ss.n)])
        VOP(lambda e: e.reduce_sum(sb_red[:, 4:5], sb_sq[:, 0:D], axis=AX.X))
        VOP(lambda e: e.tensor_scalar_mul(sb_tv[:, 2:3], sb_red[:, 4:5], 1.0 / D))
        VOP(lambda e: e.tensor_scalar(sb_sq2[:, 0:D], sb_sq[:, 0:D], sb_tv[:, 2:3], None, ALU.subtract))
        VOP(lambda e: e.tensor_mul(sb_sq2[:, 0:D], sb_sq2[:, 0:D], sb_sq2[:, 0:D]))
        VOP(lambda e: e.reduce_sum(sb_red[:, 5:6], sb_sq2[:, 0:D], axis=AX.X))
        VOP(lambda e: e.tensor_scalar_mul(sb_red[:, 6:7], sb_red[:, 5:6], 1.0 / D))
        SOP(lambda e: e.activation(sb_red[:, 7:8], sb_red[:, 6:7], AF.Ln, bias=sb_cst[:, 1:2], scale=1.0),
            waits=[(sv, sv.n), (sgp, gp_cst)])
        SOP(lambda e: e.activation(sb_tv[:, 3:4], sb_red[:, 7:8], AF.Exp, bias=0.0, scale=-0.5))
        VOP(lambda e: e.tensor_scalar(sb_sq2[:, 0:D], sb_sq[:, 0:D], sb_tv[:, 2:3], sb_tv[:, 3:4],
                                      ALU.subtract, ALU.mult), waits=[(ss, ss.n)])
        VOP(lambda e: e.tensor_reduce(sb_red[:, 8:9], sb_sq2[:, 0:D], AX.X, ALU.max,
                                      apply_absolute_value=True))
        VOP(lambda e: e.reciprocal(sb_red[:, 9:10], sb_red[:, 8:9]))
        VOP(lambda e: e.tensor_scalar_mul(sb_tv[:, 4:5], sb_red[:, 9:10], 127.0))
        VOP(lambda e: e.tensor_scalar_mul(sb_tv[:, 5:6], sb_red[:, 8:9], 2.0 / 127.0))
        for r in range(6):
            DMA("gpsimd", scr_tv[r:r + 1, tt * 128:(tt + 1) * 128], sb_tv[:, r:r + 1],
                s_scr, waits=[(sv, sv.n)])
    v_stats = sv.n
    scr_stats = s_scr.n

    if stop_after == "stats":
        _finish_early(prog, ctrs, out_tok, sb_sq, S, op, T, D, ndc)
        return _instantiate(nc, prog)

    op("gpsimd",
       lambda e: e.collective_compute("AllGather", ALU.bypass,
                                      replica_groups=[list(range(C))],
                                      ins=[scr_tv[0:2, :].opt()], outs=[agb_out.opt()]),
       waits=[(s_scr, scr_stats)], inc=scc_b, k=1)
    cc_agb = scc_b.n

    for dst, r in [(sb_mb, 0), (sb_rb, 1), (sb_m2b, 2), (sb_r2b, 3), (sb_ssb, 4), (sb_iv2, 5)]:
        DMA("gpsimd", dst[:, :], bass.AP(scr_tv_t, r * T, [[0, 128], [1, T]]),
            s_bco, waits=[(s_scr, scr_stats)])
    bco = s_bco.n

    if stop_after == "agb":
        _finish_early(prog, ctrs, out_tok, sb_sq, S, op, T, D, ndc)
        return _instantiate(nc, prog)

    for dc in range(ndc):
        DMA("sync", sb_sq[:, 0:160], xpwT[dc * 128:(dc + 1) * 128, :], s_x3,
            waits=[(sv, sv.n)])
        VOP(lambda e, dc=dc: e.tensor_copy(sb_xpw[:, dc * 160:(dc + 1) * 160], sb_sq[:, 0:160]),
            waits=[(s_x3, s_x3.n)])
    v_xpw = sv.n

    nxpj = T // NB
    xp_mm_dc = []
    for dc in range(ndc):
        DMA("sync", sb_xod[:, :], x_ownT[dc * 128:(dc + 1) * 128, :], s_x3,
            waits=[(sv, sv.n)] if dc > 0 else [(sv, v_xpw)])
        VOP(lambda e: e.tensor_sub(sb_xnTf[:, :], sb_xod[:, :], sb_mb[:, :]),
            waits=[(s_x3, s_x3.n), (s_bco, bco)])
        VOP(lambda e: e.tensor_mul(sb_xnTf[:, :], sb_xnTf[:, :], sb_rb[:, :]))
        par = dc % 2
        VOP(lambda e, par=par: e.tensor_copy(sb_xnb[par][:, :], sb_xnTf[:, :]),
            waits=[(stt, xp_mm_dc[dc - 2])] if dc >= 2 else [])
        v_xnb = sv.n
        for j in range(nxpj):
            op("tensor",
               lambda e, dc=dc, j=j, par=par: e.matmul(
                   ps[j][:, 0:NB], sb_xpw[:, dc * 160:dc * 160 + 128],
                   sb_xnb[par][:, j * NB:(j + 1) * NB],
                   start=(dc == 0), stop=(dc == ndc - 1)),
               waits=[(sv, v_xnb), (sv, v_xpw)] if j == 0 else ())
        for j in range(nxpj):
            op("tensor",
               lambda e, dc=dc, j=j, par=par: e.matmul(
                   ps[2 + j][0:32, 0:NB], sb_xpw[:, dc * 160 + 128:dc * 160 + 160],
                   sb_xnb[par][:, j * NB:(j + 1) * NB],
                   start=(dc == 0), stop=(dc == ndc - 1)),
               inc=(stt if j == nxpj - 1 else None))
        xp_mm_dc.append(stt.n)
        VOP(lambda e: e.tensor_sub(sb_xod[:, :], sb_xnTf[:, :], sb_m2b[:, :]))
        VOP(lambda e: e.tensor_mul(sb_xod[:, :], sb_xod[:, :], sb_r2b[:, :]))
        VOP(lambda e: e.tensor_mul(sb_xod[:, :], sb_xod[:, :], sb_ssb[:, :]))
        VOP(lambda e: e.tensor_scalar(sb_f16[:, :], sb_xod[:, :], 1536.0, None, ALU.add))
        VOP(lambda e, dc=dc: e.tensor_scalar(sb_kT[:, dc * T:(dc + 1) * T], sb_f16[:, :],
                                             1536.0, None, ALU.subtract))
    v_kT = sv.n
    mm_xp = stt.n
    SOP(lambda e: e.activation(sb_ag1[:, 0:NB], ps[0][:, 0:NB],
                               AF.Identity, bias=0.0, scale=1.0), waits=[(stt, mm_xp), (sv, v_kT)])
    if nxpj > 1:
        SOP(lambda e: e.activation(sb_ag1[:, NB:2 * NB], ps[1][:, 0:NB],
                                   AF.Identity, bias=0.0, scale=1.0))
    SOP(lambda e: e.activation(sb_ag2[0:32, 0:NB], ps[2][0:32, 0:NB],
                               AF.Identity, bias=0.0, scale=1.0))
    if nxpj > 1:
        SOP(lambda e: e.activation(sb_ag2[0:32, NB:2 * NB], ps[3][0:32, 0:NB],
                                   AF.Identity, bias=0.0, scale=1.0))
    ss_aga = ss.n
    DMA("gpsimd", aga_in[0:128, :], sb_ag1[:, :], s_scr, waits=[(ss, ss_aga)])
    DMA("gpsimd", aga_in[128:160, :], sb_ag2[0:32, :], s_scr)
    op("gpsimd",
       lambda e: e.collective_compute("AllGather", ALU.bypass,
                                      replica_groups=[list(range(C))],
                                      ins=[aga_in.opt()], outs=[aga_out.opt()]),
       waits=[(s_scr, s_scr.n)], inc=scc_a, k=1)
    cc_aga = scc_a.n

    if stop_after == "kx":
        _finish_early(prog, ctrs, out_tok, sb_sq, S, op, T, D, ndc)
        return _instantiate(nc, prog)

    dt_relu_ms = [0, 0]
    for dt in range(nds):
        for blk in range(C):
            DMA("sync", sb_dlow[:, 0:T], aga_out[blk * 160:blk * 160 + 128, :], s_ag,
                waits=[(scc_a, cc_aga)] + ([(stt, stt.n)] if (dt or blk) else []))
            for j in range(T // NB):
                it = (dt * C + blk) * (T // NB) + j
                bank = 2 + (it % 2)
                op("tensor",
                   lambda e, dt=dt, j=j, bank=bank: e.matmul(ps[bank][:, 0:NB],
                                                             sb_dtw[0:R, dt * 128:(dt + 1) * 128],
                                                             sb_dlow[0:R, j * NB:(j + 1) * NB],
                                                             start=True, stop=True),
                   waits=[(s_ag, s_ag.n), (sv, v_dtw), (ss, dt_relu_ms[it % 2]), (sv, sv.n)],
                   inc=stt)
                mm_dt = stt.n
                SOP(lambda e, dt=dt, bank=bank: e.activation(sb_sp1[:, 0:NB], ps[bank][:, 0:NB], AF.Abs,
                                                             bias=sb_dtbc[:, dt:dt + 1], scale=1.0),
                    waits=[(stt, mm_dt)])
                SOP(lambda e: e.activation(sb_sp2[:, 0:NB], sb_sp1[:, 0:NB], AF.Exp,
                                           bias=0.0, scale=-1.0))
                SOP(lambda e: e.activation(sb_sp1[:, 0:NB], sb_sp2[:, 0:NB], AF.Ln,
                                           bias=sb_cst[:, 2:3], scale=1.0))
                SOP(lambda e, dt=dt, bank=bank: e.activation(sb_sp3[:, 0:NB], ps[bank][:, 0:NB], AF.Relu,
                                                             bias=sb_dtbc[:, dt:dt + 1], scale=1.0),
                    waits=[(s_dts, s_dts.n)])
                dt_relu_ms[it % 2] = ss.n
                VOP(lambda e: e.tensor_add(sb_sp3[:, 0:NB], sb_sp3[:, 0:NB], sb_sp1[:, 0:NB]),
                    waits=[(ss, ss.n), (s_dts, s_dts.n)])
                DMA("gpsimd", scr_dtT[dt * 128:(dt + 1) * 128, blk * T + j * NB:blk * T + (j + 1) * NB],
                    sb_sp3[:, 0:NB], s_dts, waits=[(sv, sv.n)])
    dts_all = s_dts.n

    scan_chunk_ready = {}
    bc_last_use = [0, 0, 0]
    bc_loaded = {}

    def emit_scan_bc(dt, ch, n):
        ring = n % 3
        blk0 = (ch * TC) // T
        w = [(scc_a, cc_aga)] if (dt, ch, n) == (0, 0, 0) else [(sv, bc_last_use[ring])]
        row_b = (blk0 * 160 + 128 + n) * T
        row_c = (blk0 * 160 + 144 + n) * T
        DMA("gpsimd", sb_bc[ring][:, 0:TC].rearrange("p (b t) -> p b t", b=nblk_c),
            bass.AP(aga_out_t, row_b, [[0, 128], [160 * T, nblk_c], [1, T]]),
            s_bcs[ring], waits=w)
        DMA("gpsimd", sb_bc[ring][:, TC:2 * TC].rearrange("p (b t) -> p b t", b=nblk_c),
            bass.AP(aga_out_t, row_c, [[0, 128], [160 * T, nblk_c], [1, T]]),
            s_bcs[ring])
        bc_loaded[(dt, ch, n)] = s_bcs[ring].n

    def emit_scan_chunk_prep(dt, ch):
        DMA("gpsimd", sb_dtT[:, :], scr_dtT[dt * 128:(dt + 1) * 128, ch * TC:(ch + 1) * TC],
            s_dtl, waits=[(s_dts, dts_all), (sv, sv.n), (ss, ss.n)])
        DMA("gpsimd", sb_xdc[:, :], x_dT[dt * 128:(dt + 1) * 128, ch * TC:(ch + 1) * TC], s_dtl)
        blk0 = (ch * TC) // T
        for b in range(nblk_c):
            DMA("gpsimd", sb_mab[:, b * T:(b + 1) * T],
                bass.AP(agb_out_t, (2 * (blk0 + b)) * T, [[0, 128], [1, T]]), s_mrb,
                waits=[(scc_b, cc_agb), (sv, sv.n)] if b == 0 else [])
            DMA("gpsimd", sb_rab[:, b * T:(b + 1) * T],
                bass.AP(agb_out_t, (2 * (blk0 + b) + 1) * T, [[0, 128], [1, T]]), s_mrb)
        VOP(lambda e: e.tensor_sub(sb_xdc[:, :], sb_xdc[:, :], sb_mab[:, :]),
            waits=[(s_dtl, s_dtl.n), (s_mrb, s_mrb.n)])
        VOP(lambda e: e.tensor_mul(sb_xdc[:, :], sb_xdc[:, :], sb_rab[:, :]))
        VOP(lambda e: e.tensor_mul(sb_dtx[:, :], sb_dtT[:, :], sb_xdc[:, :]))
        scan_chunk_ready[(dt, ch)] = sv.n
        emit_scan_bc(dt, ch, 0)
        emit_scan_bc(dt, ch, 1)
        emit_scan_exp(dt, ch, 0)
        emit_scan_exp(dt, ch, 1)

    scan_exp_ms = {}
    scan_done_ms = {}

    def emit_scan_exp(dt, ch, n):
        slot = n % 3
        w = [(sv, scan_chunk_ready[(dt, ch)])]
        prev = (dt, ch, n - 3)
        if prev in scan_done_ms:
            w.append((sv, scan_done_ms[prev]))
        SOP(lambda e, dt=dt, n=n, slot=slot: e.activation(
            sb_ab3[slot][:, :], sb_dtT[:, :], AF.Exp, bias=0.0,
            scale=sb_acol[:, dt * N + n:dt * N + n + 1]), waits=w)
        scan_exp_ms[(dt, ch, n)] = ss.n

    def emit_scan_group(dt, ch, n):
        ring = n % 3
        slot = n % 3
        ahead = n + 2
        if ahead < N:
            if (dt, ch, ahead) not in bc_loaded:
                emit_scan_bc(dt, ch, ahead)
            if (dt, ch, ahead) not in scan_exp_ms:
                emit_scan_exp(dt, ch, ahead)
        VOP(lambda e, ring=ring: e.tensor_mul(sb_bx[:, :], sb_dtx[:, :], sb_bc[ring][:, 0:TC]),
            waits=[(s_bcs[ring], bc_loaded[(dt, ch, n)])])
        VOP(lambda e, dt=dt, n=n, slot=slot: e.tensor_tensor_scan(
            sb_h[:, :], sb_ab3[slot][:, :], sb_bx[:, :],
            sb_hst[:, dt * N + n:dt * N + n + 1], ALU.mult, ALU.add),
            waits=[(ss, scan_exp_ms[(dt, ch, n)]), (sgp, gp_hst0)])
        scan_done_ms[(dt, ch, n)] = sv.n
        VOP(lambda e, dt=dt, n=n: e.tensor_copy(sb_hst[:, dt * N + n:dt * N + n + 1],
                                                sb_h[:, TC - 1:TC]))
        VOP(lambda e, ring=ring: e.tensor_mul(sb_ytmp[:, :], sb_h[:, :], sb_bc[ring][:, TC:2 * TC]))
        if n == 0:
            VOP(lambda e: e.tensor_copy(sb_yacc[:, :], sb_ytmp[:, :]),
                waits=[(s_y, s_y.n)])
        else:
            VOP(lambda e: e.tensor_add(sb_yacc[:, :], sb_yacc[:, :], sb_ytmp[:, :]))
        bc_last_use[ring] = sv.n

    def emit_scan_chunk_finish(dt, ch):
        blk0 = (ch * TC) // T
        for b in range(nblk_c):
            DMA("gpsimd", y_in[(blk0 + b) * ds + dt * 128:(blk0 + b) * ds + (dt + 1) * 128, :],
                sb_yacc[:, b * T:(b + 1) * T], s_y, waits=[(sv, sv.n)] if b == 0 else [])

    scan_steps = []
    for dt in range(nds):
        for ch in range(nch):
            scan_steps.append(("prep", dt, ch, 0))
            for n in range(N):
                scan_steps.append(("group", dt, ch, n))
            scan_steps.append(("finish", dt, ch, 0))
    scan_pos = [0]

    def emit_some_scan(k):
        if stop_after == "upnoscan":
            return
        done = 0
        while done < k and scan_pos[0] < len(scan_steps):
            kind, dt, ch, n = scan_steps[scan_pos[0]]
            if kind == "prep":
                emit_scan_chunk_prep(dt, ch)
            elif kind == "group":
                emit_scan_group(dt, ch, n)
            else:
                emit_scan_chunk_finish(dt, ch)
            scan_pos[0] += 1
            done += 1

    total_up_slots = nhr * nht_r * ntb
    scan_per_slot = max(1, -(-len(scan_steps) // total_up_slots))

    if stop_after == "p0":
        _finish_early(prog, ctrs, out_tok, sb_sq, S, op, T, D, ndc)
        return _instantiate(nc, prog)

    up_psum_free = {0: 0, 1: 0}
    up_hr_ms = []
    q_dma_ms = [0, 0]
    ev_user_ms = [0, 0]
    up_sgn_v = [0, 0]
    for hr in range(nhr):
        spar = hr % 2
        for gm, (wsrc_t, sgl) in enumerate([(wgT_t, sb_sgn_g), (wuT_t, sb_sgn_u)]):
            sgt = sgl[spar]
            w = [(sv, up_sgn_v[spar])]
            if hr >= 2 and gm == 0:
                w.append((stt, up_hr_ms[hr - 2]))
            src_ap = bass.AP(wsrc_t, hr * HRS, [[H, 128], [128 * H, ndc], [1, HRS]])
            dst_ap = sgt[:, :].rearrange("p (a f) -> p a f", a=ndc)
            DMA("gpsimd", dst_ap, src_ap, s_wf[gm], waits=w)
            VOP(lambda e, sgt=sgt: e.tensor_scalar(sgt[:, :], sgt[:, :], 0.0, 0.5,
                                                   ALU.is_ge, ALU.subtract),
                waits=[(s_wf[gm], s_wf[gm].n)])
        up_sgn_v[spar] = sv.n
        sgn_done_v = sv.n
        for ht in range(nht_r):
            for tb in range(ntb):
                hg = hr * nht_r + ht
                slot = (ht * ntb + tb) % 2
                gb, ub = 2 * slot, 2 * slot + 1
                tw = [(sv, sgn_done_v), (sv, v_kT), (ss, up_psum_free[slot])]
                for dc in range(ndc):
                    op("tensor",
                       lambda e, dc=dc, ht=ht, tb=tb, gb=gb, spar=spar: e.matmul(
                           ps[gb][:, 0:NB], sb_sgn_g[spar][:, dc * HRS + ht * 128:dc * HRS + ht * 128 + 128],
                           sb_kT[:, dc * T + tb * NB:dc * T + (tb + 1) * NB],
                           start=(dc == 0), stop=(dc == ndc - 1)),
                       waits=tw if dc == 0 else ())
                for dc in range(ndc):
                    last = dc == ndc - 1
                    op("tensor",
                       lambda e, dc=dc, ht=ht, tb=tb, ub=ub, last=last, spar=spar: e.matmul(
                           ps[ub][:, 0:NB], sb_sgn_u[spar][:, dc * HRS + ht * 128:dc * HRS + ht * 128 + 128],
                           sb_kT[:, dc * T + tb * NB:dc * T + (tb + 1) * NB],
                           start=(dc == 0), stop=last),
                       inc=(stt if last else None))
                mm_up = stt.n
                SOP(lambda e, gb=gb, slot=slot: e.activation(sb_evg[slot][:, :], ps[gb][:, 0:NB],
                                                             AF.Identity, bias=0.0, scale=1.0),
                    waits=[(stt, mm_up), (sv, ev_user_ms[slot])])
                SOP(lambda e, ub=ub, slot=slot: e.activation(sb_evu[slot][:, :], ps[ub][:, 0:NB],
                                                             AF.Identity, bias=0.0, scale=1.0))
                up_psum_free[slot] = ss.n
                ev_ss = ss.n
                VOP(lambda e, tb=tb, slot=slot: e.tensor_tensor(sb_ev1[:, :], sb_evg[slot][:, :],
                                                                sb_iv2[:, tb * NB:(tb + 1) * NB], ALU.mult),
                    waits=[(ss, ev_ss), (s_bco, bco)])
                SOP(lambda e: e.activation(sb_ev2[:, :], sb_ev1[:, :], AF.Tanh, bias=0.0, scale=0.5),
                    waits=[(sv, sv.n)])
                VOP(lambda e: e.tensor_scalar(sb_ev3[:, :], sb_ev2[:, :], 0.5, 0.5, ALU.mult, ALU.add),
                    waits=[(ss, ss.n)])
                VOP(lambda e: e.tensor_mul(sb_ev4[:, :], sb_ev1[:, :], sb_ev3[:, :]))
                qp = slot
                VOP(lambda e, qp=qp, slot=slot: e.tensor_mul(sb_q[qp][:, :], sb_ev4[:, :], sb_evu[slot][:, :]),
                    waits=[(s_p[qp], q_dma_ms[qp])])
                ev_user_ms[slot] = sv.n
                VOP(lambda e, tb=tb, qp=qp: e.tensor_mul(sb_q[qp][:, :], sb_q[qp][:, :],
                                                         sb_iv2[:, tb * NB:(tb + 1) * NB]))
                DMA("sync", p_scr[hg * 128:(hg + 1) * 128, tb * NB:(tb + 1) * NB], sb_q[qp][:, :],
                    s_p[qp], waits=[(sv, sv.n)])
                q_dma_ms[qp] = s_p[qp].n
                emit_some_scan(scan_per_slot)
        up_hr_ms.append(stt.n)
    emit_some_scan(len(scan_steps))
    p_all = (s_p[0].n, s_p[1].n)
    y_all = s_y.n

    if stop_after == "up":
        _finish_early(prog, ctrs, out_tok, sb_sq, S, op, T, D, ndc)
        return _instantiate(nc, prog)
    if stop_after == "upnoscan":
        _finish_early(prog, ctrs, out_tok, sb_sq, S, op, T, D, ndc)
        return _instantiate(nc, prog)

    def cc_y_fn(e):
        with allow_ici_all_to_all():
            return e.collective_compute("AllToAll", ALU.bypass,
                                        replica_groups=[list(range(C))],
                                        ins=[y_in.opt()], outs=[y_out.opt()])
    op("gpsimd", cc_y_fn, waits=[(s_y, y_all)], inc=scc_y, k=1)
    cc_y = scc_y.n

    dn_ms = []
    dn_sgn_ms = [0, 0]
    for oq in range(noq):
        for hh in range(nh // 2):
            par = hh % 2
            gi = oq * (nh // 2) + hh
            w_slot = [(stt, dn_ms[gi - 2])] if gi >= 2 else []
            DMA("gpsimd",
                sb_wds[par][:, :].rearrange("p (a f) -> p a f", a=2),
                bass.AP(wdT_t, (hh * 256) * D + oq * OQ, [[D, 128], [128 * D, 2], [1, OQ]]),
                s_wd[par], waits=[(sv, dn_sgn_ms[par])] + w_slot)
            DMA("sync",
                sb_pt[par][:, :].rearrange("p (a f) -> p a f", a=2),
                p_scr[hh * 256:(hh + 1) * 256, :].rearrange("(a p) f -> p a f", p=128),
                s_pl[par],
                waits=([(s_p[0], p_all[0]), (s_p[1], p_all[1])] if gi == 0 else []) + w_slot)
            VOP(lambda e, par=par: e.tensor_scalar(sb_wds[par][:, :], sb_wds[par][:, :],
                                                   0.0, 0.5, ALU.is_ge, ALU.subtract),
                waits=[(s_wd[par], s_wd[par].n)])
            dn_sgn_ms[par] = sv.n
            sgn_d = sv.n
            for hi in range(2):
                h = hh * 2 + hi
                for ot in range(no_t):
                    for tb in range(ntb):
                        bank = ot * ntb + tb
                        w = []
                        if hi == 0 and ot == 0 and tb == 0:
                            w = [(sv, sgn_d), (s_pl[par], s_pl[par].n)]
                            if h == 0 and oq > 0:
                                w.append((ss, ss.n))
                        op("tensor",
                           lambda e, par=par, hi=hi, ot=ot, tb=tb, h=h, bank=bank: e.matmul(
                               ps[bank][:, 0:NB],
                               sb_wds[par][:, hi * OQ + ot * 128:hi * OQ + (ot + 1) * 128],
                               sb_pt[par][:, hi * T + tb * NB:hi * T + (tb + 1) * NB],
                               start=(h == 0), stop=(h == nh - 1)),
                           waits=w,
                           inc=(stt if (hi == 1 and ot == no_t - 1 and tb == ntb - 1) else None))
            dn_ms.append(stt.n)
        mm_dn = stt.n
        for ot in range(no_t):
            for tb in range(ntb):
                bank = ot * ntb + tb
                oc = oq * no_t + ot
                fpar = bank % 2
                SOP(lambda e, bank=bank, fpar=fpar: e.activation(
                    sb_fs[fpar][:, :], ps[bank][:, 0:NB],
                    AF.Identity, bias=0.0, scale=2.0),
                    waits=[(stt, mm_dn), (s_fsp[fpar], fs_slot_ms[fpar])])
                DMA("gpsimd", ffn_scr[oc * 128:(oc + 1) * 128, tb * NB:(tb + 1) * NB],
                    sb_fs[fpar][:, :], s_fsp[fpar], waits=[(ss, ss.n)])
                fs_slot_ms[fpar] = s_fsp[fpar].n
    ss_dn = ss.n

    if stop_after == "down":
        _finish_early(prog, ctrs, out_tok, sb_sq, S, op, T, D, ndc)
        return _instantiate(nc, prog)

    GOP(lambda e: e.memset(sb_ident[:, :], 1.0), waits=[(sv, sv.n), (ss, ss.n)])
    GOP(lambda e: e.affine_select(sb_ident[:, :], sb_ident[:, :], [[1, 128]],
                                  ALU.is_equal, 0.0, base=0, channel_multiplier=-1),
        waits=[(sgp, sgp.n)])
    gp_ident = sgp.n
    out_slot_ms = [0, 0]
    for dc in range(ndc):
        DMA("sync", sb_xod[:, :], x_ownT[dc * 128:(dc + 1) * 128, :], s_x3,
            waits=[(sv, sv.n), (ss, ss.n)] if dc == 0 else [(sv, sv.n)])
        DMA("sync", sb_ycol[:, :], y_out[dc * 128:(dc + 1) * 128, :], s_x3,
            waits=[(scc_y, cc_y), (ss, ss.n)] if dc == 0 else [])
        DMA("sync", sb_fload[:, :], ffn_scr[dc * 128:(dc + 1) * 128, :], s_x3,
            waits=[(s_fsp[0], s_fsp[0].n), (s_fsp[1], s_fsp[1].n)] if dc == 0 else [])
        x3n = s_x3.n
        VOP(lambda e: e.tensor_sub(sb_xnTf[:, :], sb_xod[:, :], sb_mb[:, :]),
            waits=[(s_x3, x3n), (stt, stt.n)])
        VOP(lambda e: e.tensor_mul(sb_xnTf[:, :], sb_xnTf[:, :], sb_rb[:, :]))
        VOP(lambda e, dc=dc: e.scalar_tensor_tensor(sb_acc[:, :], sb_xnTf[:, :],
                                                    sb_dpb[:, dc:dc + 1], sb_xod[:, :],
                                                    ALU.mult, ALU.add))
        VOP(lambda e: e.tensor_add(sb_acc[:, :], sb_acc[:, :], sb_ycol[:, :]))
        VOP(lambda e: e.tensor_add(sb_acc[:, :], sb_acc[:, :], sb_fload[:, :]),
            waits=[(ss, ss_dn)])
        acc_done = sv.n
        for tt in range(ntt):
            pslot = tt % 2
            op("tensor",
               lambda e, tt=tt, pslot=pslot: e.transpose(ps[pslot][:, 0:128],
                                                         sb_acc[:, tt * 128:(tt + 1) * 128],
                                                         sb_ident[:, :]),
               waits=[(sv, acc_done), (sgp, gp_ident)] +
                     ([(ss, ss.n)] if (dc * ntt + tt) >= 2 else []),
               inc=stt)
            tr_done = stt.n
            SOP(lambda e, pslot=pslot: e.copy(sb_ot[pslot][:, :], ps[pslot][:, 0:128]),
                waits=[(stt, tr_done), (s_out, out_slot_ms[pslot])])
            DMA("sync", out_tok[tt * 128:(tt + 1) * 128, dc * 128:(dc + 1) * 128],
                sb_ot[pslot][:, :], s_out, waits=[(ss, ss.n)])
            out_slot_ms[pslot] = s_out.n
    prog["sync"].append(lambda eng, h=s_out.h, v=s_out.n: eng.wait_ge(h, v))
    prog["gpsimd"].append(lambda eng, h=s_out.h, v=s_out.n: eng.wait_ge(h, v))

    return _instantiate(nc, prog)


def _finish_early(prog, ctrs, out_tok, sb_src, S, op, T, D, ndc):
    s_out = ctrs["s_out"]
    sv = ctrs["sv"]
    ss = ctrs["ss"]
    W = min(D, sb_src.shape[1])
    for tt in range(T // 128):
        for c0 in range(0, D, W):
            op("sync", lambda eng, tt=tt, c0=c0: eng.dma_start(
                out=out_tok[tt * 128:(tt + 1) * 128, c0:c0 + W],
                in_=sb_src[:, 0:W]), waits=[(sv, sv.n), (ss, ss.n)] if (tt == 0 and c0 == 0) else [],
                inc=s_out, k=16)
    prog["sync"].append(lambda eng, h=s_out.h, v=s_out.n: eng.wait_ge(h, v))
    prog["gpsimd"].append(lambda eng, h=s_out.h, v=s_out.n: eng.wait_ge(h, v))


def _instantiate(nc, prog):
    with nc.Block() as block:
        @block.sync
        def _(eng):
            for fn in prog["sync"]:
                fn(eng)

        @block.gpsimd
        def _(eng):
            for fn in prog["gpsimd"]:
                fn(eng)

        @block.tensor
        def _(eng):
            for fn in prog["tensor"]:
                fn(eng)

        @block.vector
        def _(eng):
            for fn in prog["vector"]:
                fn(eng)

        @block.scalar
        def _(eng):
            for fn in prog["scalar"]:
                fn(eng)

    return nc


def make_in_maps(cfg, inputs):
    D, H, L, N, R, C = cfg["D"], cfg["H"], cfg["L"], cfg["N"], cfg["R"], cfg["C"]
    T = L // C
    ds = D // C
    x = np.ascontiguousarray(np.asarray(inputs["L0_attn_x"], dtype=np.float32)[0])
    xpw = np.ascontiguousarray(np.asarray(inputs["L0_attn_x_proj_w"], np.float32).T)
    dtpw = np.ascontiguousarray(np.asarray(inputs["L0_attn_dt_proj_w"], np.float32).T)
    dtbv = np.asarray(inputs["L0_attn_dt_proj_b"], np.float32)
    al = np.asarray(inputs["L0_attn_A_log"], np.float32)
    dpv = np.ascontiguousarray(np.asarray(inputs["L0_attn_D_param"], np.float32).reshape(D // 128, 128).T)
    wg = np.ascontiguousarray(np.asarray(inputs["L0_ffn_W_g"], np.float32).T)
    wu = np.ascontiguousarray(np.asarray(inputs["L0_ffn_W_u"], np.float32).T)
    wd = np.ascontiguousarray(np.asarray(inputs["L0_ffn_W_d"], np.float32).T)
    xT = np.ascontiguousarray(x.T)
    in_maps = []
    for c in range(C):
        in_maps.append({
            "x_tok": np.ascontiguousarray(x[c * T:(c + 1) * T, :]),
            "x_ownT": np.ascontiguousarray(xT[:, c * T:(c + 1) * T]),
            "x_dT": np.ascontiguousarray(xT[c * ds:(c + 1) * ds, :]),
            "xpwT": xpw,
            "dtpwT": np.ascontiguousarray(dtpw[:, c * ds:(c + 1) * ds]),
            "dtb": np.ascontiguousarray(dtbv[c * ds:(c + 1) * ds].reshape(ds // 128, 128).T),
            "alog": np.ascontiguousarray(
                al[c * ds:(c + 1) * ds, :].reshape(ds // 128, 128, N).transpose(1, 0, 2).reshape(128, -1)),
            "dp": dpv,
            "wgT": wg,
            "wuT": wu,
            "wdT": wd,
        })
    return in_maps


def kernel(**inputs):
    cfg = FULL_CFG
    nc = build_nc(cfg)
    in_maps = make_in_maps(cfg, inputs)
    res = run_bass_kernel_spmd(nc, in_maps, core_ids=list(range(cfg["C"])))
    out = np.concatenate([np.asarray(r["out_tok"]) for r in res.results], axis=0)
    return out[None].astype(np.float32)


# version 12
# speedup vs baseline: 1.3483x; 1.0572x over previous
import numpy as np
import concourse.bass as bass
import concourse.mybir as mybir
from concourse.bass_utils import run_bass_kernel_spmd
from concourse.aot_env import allow_ici_all_to_all

F32 = mybir.dt.float32
BF16 = mybir.dt.bfloat16
F16 = mybir.dt.float16
AF = mybir.ActivationFunctionType
ALU = mybir.AluOpType
AX = mybir.AxisListType

FULL_CFG = dict(D=2048, H=8192, L=8192, N=16, R=128, C=8)


class Ctr:
    def __init__(self, handle):
        self.h = handle
        self.n = 0


def build_nc(cfg):
    stop_after = cfg.get("stop_after", "full")
    D, H, L, N, R, C = cfg["D"], cfg["H"], cfg["L"], cfg["N"], cfg["R"], cfg["C"]
    T = L // C
    ds = D // C
    nds = ds // 128
    ndc = D // 128
    NB = min(512, T)
    ntb = T // NB
    HRS = min(256, H)
    nhr = H // HRS
    nht_r = HRS // 128
    nh = H // 128
    OQ = min(512, D)
    noq = D // OQ
    no_t = OQ // 128
    TC = min(2048, L)
    nch = L // TC
    nblk_c = max(1, TC // T)
    ntt = T // 128
    assert TC % T == 0 and R == 128 and ds % 128 == 0

    nc = bass.Bass("TRN2", target_bir_lowering=False, detect_race_conditions=False)

    x_tok = nc.dram_tensor("x_tok", [T, D], F32, kind="ExternalInput").ap()
    x_ownT = nc.dram_tensor("x_ownT", [D, T], F32, kind="ExternalInput").ap()
    x_dT = nc.dram_tensor("x_dT", [ds, L], F32, kind="ExternalInput").ap()
    xpwT = nc.dram_tensor("xpwT", [D, 160], F32, kind="ExternalInput").ap()
    dtpwT = nc.dram_tensor("dtpwT", [R, ds], F32, kind="ExternalInput").ap()
    dtb = nc.dram_tensor("dtb", [128, nds], F32, kind="ExternalInput").ap()
    alog = nc.dram_tensor("alog", [128, nds * N], F32, kind="ExternalInput").ap()
    dp = nc.dram_tensor("dp", [128, D // 128], F32, kind="ExternalInput").ap()
    wgT_t = nc.dram_tensor("wgT", [D, H], F32, kind="ExternalInput")
    wuT_t = nc.dram_tensor("wuT", [D, H], F32, kind="ExternalInput")
    wdT_t = nc.dram_tensor("wdT", [H, D], F32, kind="ExternalInput")
    wgT = wgT_t.ap()
    wuT = wuT_t.ap()
    wdT = wdT_t.ap()
    out_tok = nc.dram_tensor("out_tok", [T, D], F32, kind="ExternalOutput").ap()

    scr_tv_t = nc.dram_tensor("scr_tv", [8, T], F32, kind="Internal")
    aga_in_t = nc.dram_tensor("aga_in", [160, T], BF16, kind="Internal")
    aga_out_t = nc.dram_tensor("aga_out", [160 * C, T], BF16, kind="Internal")
    agb_out_t = nc.dram_tensor("agb_out", [2 * C, T], F32, kind="Internal")
    scr_dtT_t = nc.dram_tensor("scr_dtT", [ds, L], F32, kind="Internal")
    p_scr_t = nc.dram_tensor("p_scr", [H, T], BF16, kind="Internal")
    ffn_scr_t = nc.dram_tensor("ffn_scr", [D, T], BF16, kind="Internal")
    y_in_t = nc.dram_tensor("y_a2a_in", [C * ds, T], BF16, kind="Internal")
    y_out_t = nc.dram_tensor("y_a2a_out", [C * ds, T], BF16, kind="Internal")
    scr_tv = scr_tv_t.ap()
    aga_in = aga_in_t.ap()
    aga_out = aga_out_t.ap()
    agb_out = agb_out_t.ap()
    scr_dtT = scr_dtT_t.ap()
    p_scr = p_scr_t.ap()
    ffn_scr = ffn_scr_t.ap()
    y_in = y_in_t.ap()
    y_out = y_out_t.ap()

    A = nc.alloc_sbuf_tensor
    AAT = nc.alloc_sbuf_tensor_at

    def addr_of(h):
        return nc.lookup_mloc(h).addr

    sb_xtok = A("sb_xtok", [128, D], F32)
    sb_sq = A("sb_sq", [128, D], F32)
    sb_sq2 = A("sb_sq2", [128, D], F32)
    sb_p0a = A("sb_p0a", [128, 3072], F32)
    sb_tv = A("sb_tv", [128, 8], F32)
    sb_red = A("sb_red", [128, 12], F32)
    sb_xod = A("sb_xod", [128, T], F32)
    sb_xnTf = A("sb_xnTf", [128, T], F32)
    sb_xnb = [A(f"sb_xnb{i}", [128, T], BF16) for i in range(2)]
    sb_kT = A("sb_kT", [128, ndc * T], BF16)
    sb_mb = A("sb_mb", [128, T], F32)
    sb_rb = A("sb_rb", [128, T], F32)
    sb_Ab = A("sb_Ab", [128, T], F32)
    sb_Bb = A("sb_Bb", [128, T], F32)
    sb_iv2 = A("sb_iv2", [128, T], BF16)
    sb_xpw = A("sb_xpw", [128, ndc * 160], BF16)
    sb_dtw = A("sb_dtw", [128, ds], BF16)
    sb_ag2 = A("sb_ag2", [32, T], BF16)
    sb_dtbc = A("sb_dtbc", [128, nds], F32)
    sb_acol = A("sb_acol", [128, nds * N], F32)
    sb_sgn_g = [A(f"sb_sgn_g{i}", [128, ndc * HRS], BF16) for i in range(2)]
    sb_sgn_u = [A(f"sb_sgn_u{i}", [128, ndc * HRS], BF16) for i in range(2)]
    sb_q = [A(f"sb_q{i}", [128, NB], BF16) for i in range(2)]
    sb_evg = [A(f"sb_evg{i}", [128, NB], BF16) for i in range(2)]
    sb_evu = [A(f"sb_evu{i}", [128, NB], BF16) for i in range(2)]
    sb_ev1 = A("sb_ev1", [128, NB], BF16)
    sb_ev2 = A("sb_ev2", [128, NB], BF16)
    sb_ev3 = A("sb_ev3", [128, NB], BF16)
    sb_ev4 = A("sb_ev4", [128, NB], BF16)
    sb_dtT = A("sb_dtT", [128, TC], F32)
    sb_xdc = A("sb_xdc", [128, TC], F32)
    sb_h = A("sb_h", [128, TC], BF16)
    sb_bc = [A(f"sb_bc{i}", [128, 2 * TC], BF16) for i in range(3)]
    sb_hst = A("sb_hst", [128, max(1, nds * N)], F32)
    sb_ytmp = A("sb_ytmp", [128, TC], BF16)
    sb_yacc = A("sb_yacc", [128, TC], BF16)

    sb_dpb = A("sb_dpb", [128, ndc], F32)
    sb_cst = A("sb_cst", [128, 4], F32)
    a_xtok = addr_of(sb_xtok)
    sb_pt = [AAT(f"sb_pt{i}", [128, 2 * T], BF16, offset=a_xtok + i * (4 * T)) for i in range(2)]
    a_sq = addr_of(sb_sq)
    sb_wds = None
    a_sq2 = addr_of(sb_sq2)
    sb_mab = AAT("sb_mab", [128, TC], BF16, offset=a_sq2)
    sb_rab = AAT("sb_rab", [128, TC], BF16, offset=a_sq2 + 2 * TC)
    a_p0a = addr_of(sb_p0a)
    sb_sp1 = AAT("sb_sp1", [128, NB], F32, offset=a_p0a)
    sb_sp2 = AAT("sb_sp2", [128, NB], F32, offset=a_p0a + 4 * NB)
    sb_sp3 = AAT("sb_sp3", [128, NB], F32, offset=a_p0a + 8 * NB)
    sb_f16 = AAT("sb_f16", [128, T], F16, offset=a_p0a + 12 * NB)
    sb_dlow = AAT("sb_dlow", [128, T], BF16, offset=a_p0a + 12 * NB + 2 * T)
    sb_ag1 = AAT("sb_ag1", [128, T], BF16, offset=a_p0a + 12 * NB + 4 * T)
    sb_dtx = AAT("sb_dtx", [128, TC], BF16, offset=a_p0a)
    sb_ab3 = None
    sb_bx = AAT("sb_bx", [128, TC], BF16, offset=a_p0a + 4 * TC)
    a_dtT = addr_of(sb_dtT)
    sb_acc = [AAT(f"sb_acc{i}", [128, T], F32, offset=a_dtT + i * 4 * T) for i in range(2)]
    a_h = addr_of(sb_h)
    sb_ycol = [AAT(f"sb_ycol{i}", [128, T], BF16, offset=a_h + i * 2 * T) for i in range(2)]
    a_yt = addr_of(sb_ytmp)
    sb_fload = [AAT(f"sb_fload{i}", [128, T], BF16, offset=a_yt + i * 2 * T) for i in range(2)]
    a_xdc = addr_of(sb_xdc)
    sb_ot = [AAT(f"sb_ot{i}", [128, 128], F32, offset=a_xdc + i * 512) for i in range(2)]
    sb_ident = AAT("sb_ident", [128, 128], F32, offset=a_xdc + 1024)
    def _fit(name, shape, dtype, host, host_bytes):
        nbytes = shape[1] * (2 if dtype in (BF16, F16) else 4)
        if nbytes <= host_bytes:
            return AAT(name, shape, dtype, offset=addr_of(host))
        return A(name, shape, dtype)
    sb_ab3 = [AAT("sb_ab0", [128, TC], BF16, offset=a_p0a + 2 * TC),
              _fit("sb_ab1", [128, TC], BF16, sb_xod, 4 * T),
              _fit("sb_ab2", [128, TC], BF16, sb_xnTf, 4 * T)]
    HH4 = min(4, nh)
    sb_wds = [AAT(f"sb_wds{i}", [128, HH4 * OQ], BF16, offset=a_sq + i * (2 * HH4 * OQ)) for i in range(2)]
    sb_fs = [AAT(f"sb_fs{i}", [128, NB], BF16, offset=addr_of(sb_q[i])) for i in range(2)]
    ps = [nc.alloc_psum_tensor(f"ps{i}", [128, 512], F32) for i in range(8)]

    prog = {e: [] for e in ["sync", "gpsimd", "tensor", "vector", "scalar"]}
    ctrs = {}
    sem_keep = []

    def S(name):
        c = ctrs.get(name)
        if c is None:
            g = nc.semaphore(name)
            h = g.__enter__()
            sem_keep.append(g)
            c = Ctr(h)
            ctrs[name] = c
        return c

    self_ctr = {}

    def op(e, fn, waits=(), inc=None, k=1):
        own = self_ctr.get(e)
        wl = [(c.h, v) for (c, v) in waits if v > 0 and c is not own]
        dd = {}
        for h, v in wl:
            dd[h] = max(dd.get(h, 0), v)
        wl = list(dd.items())
        if inc is not None:
            inc.n += k

        def run(eng, fn=fn, wl=wl, inc=inc, k=k):
            for h, v in wl:
                eng.wait_ge(h, v)
            ins = fn(eng)
            if inc is not None:
                ins.then_inc(inc.h, k)
        prog[e].append(run)

    sv = S("sv")
    ss = S("ss")
    stt = S("stt")
    sgp = S("sgp")
    scc_b = S("scc_b")
    scc_a = S("scc_a")
    scc_y = S("scc_y")
    s_in = S("s_in")
    s_x3 = S("s_x3")
    s_tv = S("s_tv")
    s_scr = S("s_scr")
    s_bco = S("s_bco")
    s_ag = S("s_ag")
    s_dts = S("s_dts")
    s_dtl = S("s_dtl")
    s_bcs = [S("s_bc0"), S("s_bc1"), S("s_bc2")]
    s_mrb = S("s_mrb")
    s_wf = [S("s_wf0"), S("s_wf1")]
    s_p = [S("s_p0"), S("s_p1")]
    s_pl = [S("s_pl0"), S("s_pl1")]
    s_wd = [S("s_wd0"), S("s_wd1")]
    s_y = S("s_y")
    s_fsp = [S("s_fsp0"), S("s_fsp1")]
    fs_slot_ms = [0, 0]
    s_out = S("s_out")

    self_ctr.update({"tensor": stt})

    def VOP(fn, waits=()):
        op("vector", fn, waits=list(waits) + [(sv, sv.n)], inc=sv)

    def SOP(fn, waits=()):
        op("scalar", fn, waits=list(waits) + [(ss, ss.n)], inc=ss)

    def GOP(fn, waits=()):
        op("gpsimd", fn, waits=list(waits) + [(sgp, sgp.n)], inc=sgp)

    def DMA(e, out_ap, in_ap, ctr, waits=()):
        op(e, lambda eng, o=out_ap, i=in_ap: eng.dma_start(out=o, in_=i),
           waits=waits, inc=ctr, k=16)

    DMA("sync", sb_dtbc[:, :], dtb, s_in)
    DMA("sync", sb_acol[:, :], alog, s_in)
    DMA("sync", sb_dpb[:, :], dp, s_in)
    DMA("sync", sb_sq[0:R, 0:ds], dtpwT, s_in)
    in0 = s_in.n
    VOP(lambda e: e.tensor_copy(sb_dtw[0:R, :], sb_sq[0:R, 0:ds]), waits=[(s_in, in0)])
    v_dtw = sv.n
    SOP(lambda e: e.activation(sb_acol[:, :], sb_acol[:, :], AF.Exp, bias=0.0, scale=1.0),
        waits=[(s_in, in0)])
    VOP(lambda e: e.tensor_scalar_mul(sb_acol[:, :], sb_acol[:, :], -1.0), waits=[(ss, ss.n)])
    v_acol = sv.n
    GOP(lambda e: e.memset(sb_cst[:, 0:1], 1e-5))
    GOP(lambda e: e.memset(sb_cst[:, 1:2], 1e-8))
    GOP(lambda e: e.memset(sb_cst[:, 2:3], 1.0))
    gp_cst = sgp.n
    GOP(lambda e: e.memset(sb_hst[:, :], 0.0))
    gp_hst0 = sgp.n

    if stop_after == "params":
        _finish_early(prog, ctrs, out_tok, sb_sq, S, op, T, D, ndc)
        return _instantiate(nc, prog)

    for tt in range(ntt):
        DMA("sync", sb_xtok[:, :], x_tok[tt * 128:(tt + 1) * 128, :], s_tv,
            waits=[(sv, sv.n), (ss, ss.n)] if tt > 0 else [])
        SOP(lambda e: e.activation(sb_sq[:, 0:D], sb_xtok[:, :], AF.Identity, bias=0.0,
                                   scale=1.0, accum_out=sb_red[:, 0:1]),
            waits=[(s_tv, s_tv.n), (s_scr, s_scr.n)])
        SOP(lambda e: e.activation(sb_sq[:, 0:D], sb_xtok[:, :], AF.Square, bias=0.0,
                                   scale=1.0, accum_out=sb_red[:, 1:2]))
        ss_sums = ss.n
        VOP(lambda e: e.tensor_scalar_mul(sb_tv[:, 0:1], sb_red[:, 0:1], 1.0 / D),
            waits=[(ss, ss_sums)])
        VOP(lambda e: e.tensor_mul(sb_red[:, 2:3], sb_tv[:, 0:1], sb_tv[:, 0:1]))
        VOP(lambda e: e.tensor_scalar_mul(sb_red[:, 3:4], sb_red[:, 1:2], 1.0 / D))
        VOP(lambda e: e.tensor_sub(sb_red[:, 4:5], sb_red[:, 3:4], sb_red[:, 2:3]))
        SOP(lambda e: e.activation(sb_red[:, 5:6], sb_red[:, 4:5], AF.Ln, bias=sb_cst[:, 0:1], scale=1.0),
            waits=[(sv, sv.n), (sgp, gp_cst)])
        SOP(lambda e: e.activation(sb_tv[:, 1:2], sb_red[:, 5:6], AF.Exp, bias=0.0, scale=-0.5))
        ss_rstd = ss.n
        VOP(lambda e: e.tensor_scalar(sb_sq[:, 0:D], sb_xtok[:, :], sb_tv[:, 0:1], sb_tv[:, 1:2],
                                      ALU.subtract, ALU.mult), waits=[(ss, ss_rstd)])
        v_xn = sv.n
        SOP(lambda e: e.activation(sb_sq2[:, 0:D], sb_sq[:, 0:D], AF.Identity, bias=0.0,
                                   scale=1.0, accum_out=sb_red[:, 6:7]), waits=[(sv, v_xn)])
        SOP(lambda e: e.activation(sb_sq2[:, 0:D], sb_sq[:, 0:D], AF.Square, bias=0.0,
                                   scale=1.0, accum_out=sb_red[:, 7:8]))
        ss_sums2 = ss.n
        VOP(lambda e: e.tensor_scalar_mul(sb_tv[:, 6:7], sb_red[:, 6:7], 1.0 / D),
            waits=[(ss, ss_sums2)])
        VOP(lambda e: e.tensor_mul(sb_red[:, 8:9], sb_tv[:, 6:7], sb_tv[:, 6:7]))
        VOP(lambda e: e.tensor_scalar_mul(sb_red[:, 9:10], sb_red[:, 7:8], 1.0 / D))
        VOP(lambda e: e.tensor_sub(sb_red[:, 10:11], sb_red[:, 9:10], sb_red[:, 8:9]))
        SOP(lambda e: e.activation(sb_red[:, 11:12], sb_red[:, 10:11], AF.Ln, bias=sb_cst[:, 1:2], scale=1.0),
            waits=[(sv, sv.n)])
        SOP(lambda e: e.activation(sb_tv[:, 7:8], sb_red[:, 11:12], AF.Exp, bias=0.0, scale=-0.5))
        ss_rstd2 = ss.n
        VOP(lambda e: e.tensor_scalar(sb_sq2[:, 0:D], sb_sq[:, 0:D], sb_tv[:, 6:7], sb_tv[:, 7:8],
                                      ALU.subtract, ALU.mult), waits=[(ss, ss_rstd2)])
        VOP(lambda e: e.tensor_reduce(sb_red[:, 8:9], sb_sq2[:, 0:D], AX.X, ALU.max,
                                      apply_absolute_value=True))
        VOP(lambda e: e.reciprocal(sb_red[:, 9:10], sb_red[:, 8:9]))
        VOP(lambda e: e.tensor_scalar_mul(sb_red[:, 10:11], sb_red[:, 9:10], 127.0))
        VOP(lambda e: e.tensor_scalar_mul(sb_tv[:, 4:5], sb_red[:, 8:9], 2.0 / 127.0))
        VOP(lambda e: e.tensor_mul(sb_red[:, 11:12], sb_tv[:, 7:8], sb_red[:, 10:11]))
        VOP(lambda e: e.tensor_mul(sb_tv[:, 2:3], sb_tv[:, 1:2], sb_red[:, 11:12]))
        VOP(lambda e: e.tensor_mul(sb_red[:, 9:10], sb_tv[:, 0:1], sb_tv[:, 1:2]))
        VOP(lambda e: e.tensor_add(sb_red[:, 9:10], sb_red[:, 9:10], sb_tv[:, 6:7]))
        VOP(lambda e: e.tensor_mul(sb_tv[:, 3:4], sb_red[:, 9:10], sb_red[:, 11:12]))
        for r in range(5):
            DMA("gpsimd", scr_tv[r:r + 1, tt * 128:(tt + 1) * 128], sb_tv[:, r:r + 1],
                s_scr, waits=[(sv, sv.n)])
    v_stats = sv.n
    scr_stats = s_scr.n

    op("gpsimd",
       lambda e: e.collective_compute("AllGather", ALU.bypass,
                                      replica_groups=[list(range(C))],
                                      ins=[scr_tv[0:2, :].opt()], outs=[agb_out.opt()]),
       waits=[(s_scr, scr_stats)], inc=scc_b, k=1)
    cc_agb = scc_b.n

    for dst, r in [(sb_mb, 0), (sb_rb, 1), (sb_Ab, 2), (sb_Bb, 3), (sb_iv2, 4)]:
        DMA("gpsimd", dst[:, :], bass.AP(scr_tv_t, r * T, [[0, 128], [1, T]]),
            s_bco, waits=[(s_scr, scr_stats)])
    bco = s_bco.n

    if stop_after == "agb":
        _finish_early(prog, ctrs, out_tok, sb_sq, S, op, T, D, ndc)
        return _instantiate(nc, prog)

    for dc in range(ndc):
        DMA("sync", sb_sq[:, 0:160], xpwT[dc * 128:(dc + 1) * 128, :], s_x3,
            waits=[(sv, sv.n)])
        VOP(lambda e, dc=dc: e.tensor_copy(sb_xpw[:, dc * 160:(dc + 1) * 160], sb_sq[:, 0:160]),
            waits=[(s_x3, s_x3.n)])
    v_xpw = sv.n

    nxpj = T // NB
    xp_mm_dc = []
    for dc in range(ndc):
        DMA("sync", sb_xod[:, :], x_ownT[dc * 128:(dc + 1) * 128, :], s_x3,
            waits=[(sv, sv.n)] if dc > 0 else [(sv, v_xpw)])
        par = dc % 2
        VOP(lambda e: e.tensor_sub(sb_xnTf[:, :], sb_xod[:, :], sb_mb[:, :]),
            waits=[(s_x3, s_x3.n), (s_bco, bco)])
        VOP(lambda e, par=par: e.tensor_mul(sb_xnb[par][:, :], sb_xnTf[:, :], sb_rb[:, :]),
            waits=[(stt, xp_mm_dc[dc - 2])] if dc >= 2 else [])
        v_xnb = sv.n
        for j in range(nxpj):
            op("tensor",
               lambda e, dc=dc, j=j, par=par: e.matmul(
                   ps[j][:, 0:NB], sb_xpw[:, dc * 160:dc * 160 + 128],
                   sb_xnb[par][:, j * NB:(j + 1) * NB],
                   start=(dc == 0), stop=(dc == ndc - 1)),
               waits=[(sv, v_xnb), (sv, v_xpw)] if j == 0 else ())
        for j in range(nxpj):
            op("tensor",
               lambda e, dc=dc, j=j, par=par: e.matmul(
                   ps[2 + j][0:32, 0:NB], sb_xpw[:, dc * 160 + 128:dc * 160 + 160],
                   sb_xnb[par][:, j * NB:(j + 1) * NB],
                   start=(dc == 0), stop=(dc == ndc - 1)),
               inc=(stt if j == nxpj - 1 else None))
        xp_mm_dc.append(stt.n)
        VOP(lambda e: e.tensor_mul(sb_xnTf[:, :], sb_xod[:, :], sb_Ab[:, :]))
        VOP(lambda e: e.tensor_sub(sb_xnTf[:, :], sb_xnTf[:, :], sb_Bb[:, :]))
        VOP(lambda e: e.tensor_scalar(sb_f16[:, :], sb_xnTf[:, :], 1536.0, None, ALU.add))
        VOP(lambda e, dc=dc: e.tensor_scalar(sb_kT[:, dc * T:(dc + 1) * T], sb_f16[:, :],
                                             1536.0, None, ALU.subtract))
    v_kT = sv.n
    mm_xp = stt.n
    SOP(lambda e: e.activation(sb_ag1[:, 0:NB], ps[0][:, 0:NB],
                               AF.Identity, bias=0.0, scale=1.0), waits=[(stt, mm_xp), (sv, v_kT)])
    if nxpj > 1:
        SOP(lambda e: e.activation(sb_ag1[:, NB:2 * NB], ps[1][:, 0:NB],
                                   AF.Identity, bias=0.0, scale=1.0))
    SOP(lambda e: e.activation(sb_ag2[0:32, 0:NB], ps[2][0:32, 0:NB],
                               AF.Identity, bias=0.0, scale=1.0))
    if nxpj > 1:
        SOP(lambda e: e.activation(sb_ag2[0:32, NB:2 * NB], ps[3][0:32, 0:NB],
                                   AF.Identity, bias=0.0, scale=1.0))
    ss_aga = ss.n
    DMA("gpsimd", aga_in[0:128, :], sb_ag1[:, :], s_scr, waits=[(ss, ss_aga)])
    DMA("gpsimd", aga_in[128:160, :], sb_ag2[0:32, :], s_scr)
    op("gpsimd",
       lambda e: e.collective_compute("AllGather", ALU.bypass,
                                      replica_groups=[list(range(C))],
                                      ins=[aga_in.opt()], outs=[aga_out.opt()]),
       waits=[(s_scr, s_scr.n)], inc=scc_a, k=1)
    cc_aga = scc_a.n

    if stop_after == "kx":
        _finish_early(prog, ctrs, out_tok, sb_sq, S, op, T, D, ndc)
        return _instantiate(nc, prog)

    dt_relu_ms = [0, 0]
    for dt in range(nds):
        for blk in range(C):
            DMA("sync", sb_dlow[:, 0:T], aga_out[blk * 160:blk * 160 + 128, :], s_ag,
                waits=[(scc_a, cc_aga)] + ([(stt, stt.n)] if (dt or blk) else []))
            for j in range(T // NB):
                it = (dt * C + blk) * (T // NB) + j
                bank = 2 + (it % 2)
                op("tensor",
                   lambda e, dt=dt, j=j, bank=bank: e.matmul(ps[bank][:, 0:NB],
                                                             sb_dtw[0:R, dt * 128:(dt + 1) * 128],
                                                             sb_dlow[0:R, j * NB:(j + 1) * NB],
                                                             start=True, stop=True),
                   waits=[(s_ag, s_ag.n), (sv, v_dtw), (ss, dt_relu_ms[it % 2]), (sv, sv.n)],
                   inc=stt)
                mm_dt = stt.n
                SOP(lambda e, dt=dt, bank=bank: e.activation(sb_sp1[:, 0:NB], ps[bank][:, 0:NB], AF.Abs,
                                                             bias=sb_dtbc[:, dt:dt + 1], scale=1.0),
                    waits=[(stt, mm_dt)])
                SOP(lambda e: e.activation(sb_sp2[:, 0:NB], sb_sp1[:, 0:NB], AF.Exp,
                                           bias=0.0, scale=-1.0))
                SOP(lambda e: e.activation(sb_sp1[:, 0:NB], sb_sp2[:, 0:NB], AF.Ln,
                                           bias=sb_cst[:, 2:3], scale=1.0))
                SOP(lambda e, dt=dt, bank=bank: e.activation(sb_sp3[:, 0:NB], ps[bank][:, 0:NB], AF.Relu,
                                                             bias=sb_dtbc[:, dt:dt + 1], scale=1.0),
                    waits=[(s_dts, s_dts.n)])
                dt_relu_ms[it % 2] = ss.n
                VOP(lambda e: e.tensor_add(sb_sp3[:, 0:NB], sb_sp3[:, 0:NB], sb_sp1[:, 0:NB]),
                    waits=[(ss, ss.n), (s_dts, s_dts.n)])
                DMA("gpsimd", scr_dtT[dt * 128:(dt + 1) * 128, blk * T + j * NB:blk * T + (j + 1) * NB],
                    sb_sp3[:, 0:NB], s_dts, waits=[(sv, sv.n)])
    dts_all = s_dts.n

    scan_chunk_ready = {}
    bc_last_use = [0, 0, 0]
    bc_loaded = {}

    def emit_scan_bc(dt, ch, n):
        ring = n % 3
        blk0 = (ch * TC) // T
        w = [(scc_a, cc_aga)] if (dt, ch, n) == (0, 0, 0) else [(sv, bc_last_use[ring])]
        row_b = (blk0 * 160 + 128 + n) * T
        row_c = (blk0 * 160 + 144 + n) * T
        DMA("gpsimd", sb_bc[ring][:, 0:TC].rearrange("p (b t) -> p b t", b=nblk_c),
            bass.AP(aga_out_t, row_b, [[0, 128], [160 * T, nblk_c], [1, T]]),
            s_bcs[ring], waits=w)
        DMA("gpsimd", sb_bc[ring][:, TC:2 * TC].rearrange("p (b t) -> p b t", b=nblk_c),
            bass.AP(aga_out_t, row_c, [[0, 128], [160 * T, nblk_c], [1, T]]),
            s_bcs[ring])
        bc_loaded[(dt, ch, n)] = s_bcs[ring].n

    def emit_scan_chunk_prep(dt, ch):
        DMA("gpsimd", sb_dtT[:, :], scr_dtT[dt * 128:(dt + 1) * 128, ch * TC:(ch + 1) * TC],
            s_dtl, waits=[(s_dts, dts_all), (sv, sv.n), (ss, ss.n)])
        DMA("gpsimd", sb_xdc[:, :], x_dT[dt * 128:(dt + 1) * 128, ch * TC:(ch + 1) * TC], s_dtl)
        blk0 = (ch * TC) // T
        for b in range(nblk_c):
            DMA("gpsimd", sb_mab[:, b * T:(b + 1) * T],
                bass.AP(agb_out_t, (2 * (blk0 + b)) * T, [[0, 128], [1, T]]), s_mrb,
                waits=[(scc_b, cc_agb), (sv, sv.n)] if b == 0 else [])
            DMA("gpsimd", sb_rab[:, b * T:(b + 1) * T],
                bass.AP(agb_out_t, (2 * (blk0 + b) + 1) * T, [[0, 128], [1, T]]), s_mrb)
        VOP(lambda e: e.tensor_sub(sb_xdc[:, :], sb_xdc[:, :], sb_mab[:, :]),
            waits=[(s_dtl, s_dtl.n), (s_mrb, s_mrb.n)])
        VOP(lambda e: e.tensor_mul(sb_xdc[:, :], sb_xdc[:, :], sb_rab[:, :]))
        VOP(lambda e: e.tensor_mul(sb_dtx[:, :], sb_dtT[:, :], sb_xdc[:, :]))
        scan_chunk_ready[(dt, ch)] = sv.n
        emit_scan_bc(dt, ch, 0)
        emit_scan_bc(dt, ch, 1)
        emit_scan_exp(dt, ch, 0)
        emit_scan_exp(dt, ch, 1)

    scan_exp_ms = {}
    scan_done_ms = {}

    def emit_scan_exp(dt, ch, n):
        slot = n % 3
        w = [(sv, scan_chunk_ready[(dt, ch)])]
        prev = (dt, ch, n - 3)
        if prev in scan_done_ms:
            w.append((sv, scan_done_ms[prev]))
        SOP(lambda e, dt=dt, n=n, slot=slot: e.activation(
            sb_ab3[slot][:, :], sb_dtT[:, :], AF.Exp, bias=0.0,
            scale=sb_acol[:, dt * N + n:dt * N + n + 1]), waits=w)
        scan_exp_ms[(dt, ch, n)] = ss.n

    def emit_scan_group(dt, ch, n):
        ring = n % 3
        slot = n % 3
        ahead = n + 2
        if ahead < N:
            if (dt, ch, ahead) not in bc_loaded:
                emit_scan_bc(dt, ch, ahead)
            if (dt, ch, ahead) not in scan_exp_ms:
                emit_scan_exp(dt, ch, ahead)
        VOP(lambda e, ring=ring: e.tensor_mul(sb_bx[:, :], sb_dtx[:, :], sb_bc[ring][:, 0:TC]),
            waits=[(s_bcs[ring], bc_loaded[(dt, ch, n)])])
        VOP(lambda e, dt=dt, n=n, slot=slot: e.tensor_tensor_scan(
            sb_h[:, :], sb_ab3[slot][:, :], sb_bx[:, :],
            sb_hst[:, dt * N + n:dt * N + n + 1], ALU.mult, ALU.add),
            waits=[(ss, scan_exp_ms[(dt, ch, n)]), (sgp, gp_hst0)])
        scan_done_ms[(dt, ch, n)] = sv.n
        VOP(lambda e, dt=dt, n=n: e.tensor_copy(sb_hst[:, dt * N + n:dt * N + n + 1],
                                                sb_h[:, TC - 1:TC]))
        VOP(lambda e, ring=ring: e.tensor_mul(sb_ytmp[:, :], sb_h[:, :], sb_bc[ring][:, TC:2 * TC]))
        if n == 0:
            VOP(lambda e: e.tensor_copy(sb_yacc[:, :], sb_ytmp[:, :]),
                waits=[(s_y, s_y.n)])
        else:
            VOP(lambda e: e.tensor_add(sb_yacc[:, :], sb_yacc[:, :], sb_ytmp[:, :]))
        bc_last_use[ring] = sv.n

    def emit_scan_chunk_finish(dt, ch):
        blk0 = (ch * TC) // T
        for b in range(nblk_c):
            DMA("gpsimd", y_in[(blk0 + b) * ds + dt * 128:(blk0 + b) * ds + (dt + 1) * 128, :],
                sb_yacc[:, b * T:(b + 1) * T], s_y, waits=[(sv, sv.n)] if b == 0 else [])

    scan_steps = []
    for dt in range(nds):
        for ch in range(nch):
            scan_steps.append(("prep", dt, ch, 0))
            for n in range(N):
                scan_steps.append(("group", dt, ch, n))
            scan_steps.append(("finish", dt, ch, 0))
    scan_pos = [0]

    def emit_some_scan(k):
        if stop_after == "upnoscan":
            return
        done = 0
        while done < k and scan_pos[0] < len(scan_steps):
            kind, dt, ch, n = scan_steps[scan_pos[0]]
            if kind == "prep":
                emit_scan_chunk_prep(dt, ch)
            elif kind == "group":
                emit_scan_group(dt, ch, n)
            else:
                emit_scan_chunk_finish(dt, ch)
            scan_pos[0] += 1
            done += 1

    total_up_slots = nhr * nht_r * ntb
    scan_per_slot = max(1, -(-len(scan_steps) // total_up_slots))

    if stop_after == "p0":
        _finish_early(prog, ctrs, out_tok, sb_sq, S, op, T, D, ndc)
        return _instantiate(nc, prog)

    up_psum_free = {0: 0, 1: 0}
    up_hr_ms = []
    q_dma_ms = [0, 0]
    ev_user_ms = [0, 0]
    up_sgn_v = [0, 0]
    for hr in range(nhr):
        spar = hr % 2
        for gm, (wsrc_t, sgl) in enumerate([(wgT_t, sb_sgn_g), (wuT_t, sb_sgn_u)]):
            sgt = sgl[spar]
            w = [(sv, up_sgn_v[spar])]
            if hr >= 2 and gm == 0:
                w.append((stt, up_hr_ms[hr - 2]))
            src_ap = bass.AP(wsrc_t, hr * HRS, [[H, 128], [128 * H, ndc], [1, HRS]])
            dst_ap = sgt[:, :].rearrange("p (a f) -> p a f", a=ndc)
            DMA("gpsimd", dst_ap, src_ap, s_wf[gm], waits=w)
            VOP(lambda e, sgt=sgt: e.tensor_scalar(sgt[:, :], sgt[:, :], 0.0, 0.5,
                                                   ALU.is_ge, ALU.subtract),
                waits=[(s_wf[gm], s_wf[gm].n)])
        up_sgn_v[spar] = sv.n
        sgn_done_v = sv.n
        for ht in range(nht_r):
            for tb in range(ntb):
                hg = hr * nht_r + ht
                slot = (ht * ntb + tb) % 2
                gb, ub = 2 * slot, 2 * slot + 1
                tw = [(sv, sgn_done_v), (sv, v_kT), (ss, up_psum_free[slot])]
                for dc in range(ndc):
                    op("tensor",
                       lambda e, dc=dc, ht=ht, tb=tb, gb=gb, spar=spar: e.matmul(
                           ps[gb][:, 0:NB], sb_sgn_g[spar][:, dc * HRS + ht * 128:dc * HRS + ht * 128 + 128],
                           sb_kT[:, dc * T + tb * NB:dc * T + (tb + 1) * NB],
                           start=(dc == 0), stop=(dc == ndc - 1)),
                       waits=tw if dc == 0 else ())
                for dc in range(ndc):
                    last = dc == ndc - 1
                    op("tensor",
                       lambda e, dc=dc, ht=ht, tb=tb, ub=ub, last=last, spar=spar: e.matmul(
                           ps[ub][:, 0:NB], sb_sgn_u[spar][:, dc * HRS + ht * 128:dc * HRS + ht * 128 + 128],
                           sb_kT[:, dc * T + tb * NB:dc * T + (tb + 1) * NB],
                           start=(dc == 0), stop=last),
                       inc=(stt if last else None))
                mm_up = stt.n
                SOP(lambda e, gb=gb, slot=slot: e.activation(sb_evg[slot][:, :], ps[gb][:, 0:NB],
                                                             AF.Identity, bias=0.0, scale=1.0),
                    waits=[(stt, mm_up), (sv, ev_user_ms[slot])])
                SOP(lambda e, ub=ub, slot=slot: e.activation(sb_evu[slot][:, :], ps[ub][:, 0:NB],
                                                             AF.Identity, bias=0.0, scale=1.0))
                up_psum_free[slot] = ss.n
                ev_ss = ss.n
                VOP(lambda e, tb=tb, slot=slot: e.tensor_tensor(sb_ev1[:, :], sb_evg[slot][:, :],
                                                                sb_iv2[:, tb * NB:(tb + 1) * NB], ALU.mult),
                    waits=[(ss, ev_ss), (s_bco, bco)])
                SOP(lambda e: e.activation(sb_ev2[:, :], sb_ev1[:, :], AF.Tanh, bias=0.0, scale=0.5),
                    waits=[(sv, sv.n)])
                VOP(lambda e: e.tensor_scalar(sb_ev3[:, :], sb_ev2[:, :], 0.5, 0.5, ALU.mult, ALU.add),
                    waits=[(ss, ss.n)])
                VOP(lambda e: e.tensor_mul(sb_ev4[:, :], sb_ev1[:, :], sb_ev3[:, :]))
                qp = slot
                VOP(lambda e, qp=qp, slot=slot: e.tensor_mul(sb_q[qp][:, :], sb_ev4[:, :], sb_evu[slot][:, :]),
                    waits=[(s_p[qp], q_dma_ms[qp])])
                ev_user_ms[slot] = sv.n
                VOP(lambda e, tb=tb, qp=qp: e.tensor_mul(sb_q[qp][:, :], sb_q[qp][:, :],
                                                         sb_iv2[:, tb * NB:(tb + 1) * NB]))
                DMA("sync", p_scr[hg * 128:(hg + 1) * 128, tb * NB:(tb + 1) * NB], sb_q[qp][:, :],
                    s_p[qp], waits=[(sv, sv.n)])
                q_dma_ms[qp] = s_p[qp].n
                emit_some_scan(scan_per_slot)
        up_hr_ms.append(stt.n)
    emit_some_scan(len(scan_steps))
    p_all = (s_p[0].n, s_p[1].n)
    y_all = s_y.n

    if stop_after == "up":
        _finish_early(prog, ctrs, out_tok, sb_sq, S, op, T, D, ndc)
        return _instantiate(nc, prog)
    if stop_after == "upnoscan":
        _finish_early(prog, ctrs, out_tok, sb_sq, S, op, T, D, ndc)
        return _instantiate(nc, prog)

    def cc_y_fn(e):
        with allow_ici_all_to_all():
            return e.collective_compute("AllToAll", ALU.bypass,
                                        replica_groups=[list(range(C))],
                                        ins=[y_in.opt()], outs=[y_out.opt()])
    op("gpsimd", cc_y_fn, waits=[(s_y, y_all)], inc=scc_y, k=1)
    cc_y = scc_y.n

    dn_ms = []
    dn_sgn_ms = [0, 0]
    for oq in range(noq):
        for h4 in range(nh // HH4):
            par4 = h4 % 2
            g4 = oq * (nh // HH4) + h4
            w4 = [(stt, dn_ms[2 * g4 - 3])] if g4 >= 2 else []
            DMA("gpsimd",
                sb_wds[par4][:, :].rearrange("p (a f) -> p a f", a=HH4),
                bass.AP(wdT_t, (h4 * HH4 * 128) * D + oq * OQ, [[D, 128], [128 * D, HH4], [1, OQ]]),
                s_wd[par4], waits=[(sv, dn_sgn_ms[par4])] + w4)
            VOP(lambda e, par4=par4: e.tensor_scalar(sb_wds[par4][:, :], sb_wds[par4][:, :],
                                                     0.0, 0.5, ALU.is_ge, ALU.subtract),
                waits=[(s_wd[par4], s_wd[par4].n)])
            dn_sgn_ms[par4] = sv.n
            sgn_d = sv.n
            for hh2 in range(HH4 // 2):
                hh = h4 * (HH4 // 2) + hh2
                par = hh % 2
                gi = oq * (nh // 2) + hh
                w_slot = [(stt, dn_ms[gi - 2])] if gi >= 2 else []
                DMA("sync",
                    sb_pt[par][:, :].rearrange("p (a f) -> p a f", a=2),
                    p_scr[hh * 256:(hh + 1) * 256, :].rearrange("(a p) f -> p a f", p=128),
                    s_pl[par],
                    waits=([(s_p[0], p_all[0]), (s_p[1], p_all[1])] if gi == 0 else []) + w_slot)
                for hi in range(2):
                    h = hh * 2 + hi
                    hq = hh2 * 2 + hi
                    for ot in range(no_t):
                        for tb in range(ntb):
                            bank = ot * ntb + tb
                            w = []
                            if hi == 0 and ot == 0 and tb == 0:
                                w = [(sv, sgn_d), (s_pl[par], s_pl[par].n)]
                                if h == 0 and oq > 0:
                                    w.append((ss, ss.n))
                            op("tensor",
                               lambda e, par=par, par4=par4, hq=hq, hi=hi, ot=ot, tb=tb, h=h, bank=bank: e.matmul(
                                   ps[bank][:, 0:NB],
                                   sb_wds[par4][:, hq * OQ + ot * 128:hq * OQ + (ot + 1) * 128],
                                   sb_pt[par][:, hi * T + tb * NB:hi * T + (tb + 1) * NB],
                                   start=(h == 0), stop=(h == nh - 1)),
                               waits=w,
                               inc=(stt if (hi == 1 and ot == no_t - 1 and tb == ntb - 1) else None))
                dn_ms.append(stt.n)
        mm_dn = stt.n
        for ot in range(no_t):
            for tb in range(ntb):
                bank = ot * ntb + tb
                oc = oq * no_t + ot
                fpar = bank % 2
                SOP(lambda e, bank=bank, fpar=fpar: e.activation(
                    sb_fs[fpar][:, :], ps[bank][:, 0:NB],
                    AF.Identity, bias=0.0, scale=2.0),
                    waits=[(stt, mm_dn), (s_fsp[fpar], fs_slot_ms[fpar])])
                DMA("gpsimd", ffn_scr[oc * 128:(oc + 1) * 128, tb * NB:(tb + 1) * NB],
                    sb_fs[fpar][:, :], s_fsp[fpar], waits=[(ss, ss.n)])
                fs_slot_ms[fpar] = s_fsp[fpar].n
    ss_dn = ss.n

    if stop_after == "down":
        _finish_early(prog, ctrs, out_tok, sb_sq, S, op, T, D, ndc)
        return _instantiate(nc, prog)

    GOP(lambda e: e.memset(sb_ident[:, :], 1.0), waits=[(sv, sv.n), (ss, ss.n)])
    GOP(lambda e: e.affine_select(sb_ident[:, :], sb_ident[:, :], [[1, 128]],
                                  ALU.is_equal, 0.0, base=0, channel_multiplier=-1),
        waits=[(sgp, sgp.n)])
    gp_ident = sgp.n
    out_slot_ms = [0, 0]
    s_p3 = [S("s_p30"), S("s_p31")]
    p3_user_ms = [0, 0]
    acc_tr_ms = [0, 0]
    for dc in range(ndc):
        par = dc % 2
        w0 = [(sv, p3_user_ms[par])]
        if dc == 0:
            w0 += [(sv, sv.n), (ss, ss.n)]
        DMA("sync", sb_xod[:, :], x_ownT[dc * 128:(dc + 1) * 128, :], s_x3,
            waits=([(sv, sv.n), (ss, ss.n)] if dc == 0 else [(sv, sv.n)]))
        DMA("sync", sb_ycol[par][:, :], y_out[dc * 128:(dc + 1) * 128, :], s_p3[par],
            waits=([(scc_y, cc_y), (ss, ss.n)] if dc == 0 else []) + [(sv, p3_user_ms[par])])
        DMA("sync", sb_fload[par][:, :], ffn_scr[dc * 128:(dc + 1) * 128, :], s_p3[par],
            waits=([(s_fsp[0], s_fsp[0].n), (s_fsp[1], s_fsp[1].n)] if dc == 0 else []))
        x3n = s_x3.n
        p3n = s_p3[par].n
        VOP(lambda e: e.tensor_sub(sb_xnTf[:, :], sb_xod[:, :], sb_mb[:, :]),
            waits=[(s_x3, x3n), (stt, stt.n)] if dc == 0 else [(s_x3, x3n)])
        VOP(lambda e: e.tensor_mul(sb_xnTf[:, :], sb_xnTf[:, :], sb_rb[:, :]))
        VOP(lambda e, dc=dc, par=par: e.scalar_tensor_tensor(sb_acc[par][:, :], sb_xnTf[:, :],
                                                             sb_dpb[:, dc:dc + 1], sb_xod[:, :],
                                                             ALU.mult, ALU.add),
            waits=[(stt, acc_tr_ms[par])])
        VOP(lambda e, par=par: e.tensor_add(sb_acc[par][:, :], sb_acc[par][:, :], sb_ycol[par][:, :]),
            waits=[(s_p3[par], p3n)])
        VOP(lambda e, par=par: e.tensor_add(sb_acc[par][:, :], sb_acc[par][:, :], sb_fload[par][:, :]),
            waits=[(ss, ss_dn)] if dc == 0 else [])
        acc_done = sv.n
        p3_user_ms[par] = sv.n
        for tt in range(ntt):
            pslot = tt % 2
            op("tensor",
               lambda e, tt=tt, pslot=pslot, par=par: e.transpose(ps[pslot][:, 0:128],
                                                                  sb_acc[par][:, tt * 128:(tt + 1) * 128],
                                                                  sb_ident[:, :]),
               waits=[(sv, acc_done), (sgp, gp_ident)] +
                     ([(ss, ss.n)] if (dc * ntt + tt) >= 2 else []),
               inc=stt)
            tr_done = stt.n
            SOP(lambda e, pslot=pslot: e.copy(sb_ot[pslot][:, :], ps[pslot][:, 0:128]),
                waits=[(stt, tr_done), (s_out, out_slot_ms[pslot])])
            DMA("sync", out_tok[tt * 128:(tt + 1) * 128, dc * 128:(dc + 1) * 128],
                sb_ot[pslot][:, :], s_out, waits=[(ss, ss.n)])
            out_slot_ms[pslot] = s_out.n
        acc_tr_ms[par] = stt.n
    prog["sync"].append(lambda eng, h=s_out.h, v=s_out.n: eng.wait_ge(h, v))
    prog["gpsimd"].append(lambda eng, h=s_out.h, v=s_out.n: eng.wait_ge(h, v))

    return _instantiate(nc, prog)


def _finish_early(prog, ctrs, out_tok, sb_src, S, op, T, D, ndc):
    s_out = ctrs["s_out"]
    sv = ctrs["sv"]
    ss = ctrs["ss"]
    W = min(D, sb_src.shape[1])
    for tt in range(T // 128):
        for c0 in range(0, D, W):
            op("sync", lambda eng, tt=tt, c0=c0: eng.dma_start(
                out=out_tok[tt * 128:(tt + 1) * 128, c0:c0 + W],
                in_=sb_src[:, 0:W]), waits=[(sv, sv.n), (ss, ss.n)] if (tt == 0 and c0 == 0) else [],
                inc=s_out, k=16)
    prog["sync"].append(lambda eng, h=s_out.h, v=s_out.n: eng.wait_ge(h, v))
    prog["gpsimd"].append(lambda eng, h=s_out.h, v=s_out.n: eng.wait_ge(h, v))


def _instantiate(nc, prog):
    with nc.Block() as block:
        @block.sync
        def _(eng):
            for fn in prog["sync"]:
                fn(eng)

        @block.gpsimd
        def _(eng):
            for fn in prog["gpsimd"]:
                fn(eng)

        @block.tensor
        def _(eng):
            for fn in prog["tensor"]:
                fn(eng)

        @block.vector
        def _(eng):
            for fn in prog["vector"]:
                fn(eng)

        @block.scalar
        def _(eng):
            for fn in prog["scalar"]:
                fn(eng)

    return nc


def make_in_maps(cfg, inputs):
    D, H, L, N, R, C = cfg["D"], cfg["H"], cfg["L"], cfg["N"], cfg["R"], cfg["C"]
    T = L // C
    ds = D // C
    x = np.ascontiguousarray(np.asarray(inputs["L0_attn_x"], dtype=np.float32)[0])
    xpw = np.ascontiguousarray(np.asarray(inputs["L0_attn_x_proj_w"], np.float32).T)
    dtpw = np.ascontiguousarray(np.asarray(inputs["L0_attn_dt_proj_w"], np.float32).T)
    dtbv = np.asarray(inputs["L0_attn_dt_proj_b"], np.float32)
    al = np.asarray(inputs["L0_attn_A_log"], np.float32)
    dpv = np.ascontiguousarray(np.asarray(inputs["L0_attn_D_param"], np.float32).reshape(D // 128, 128).T)
    wg = np.ascontiguousarray(np.asarray(inputs["L0_ffn_W_g"], np.float32).T)
    wu = np.ascontiguousarray(np.asarray(inputs["L0_ffn_W_u"], np.float32).T)
    wd = np.ascontiguousarray(np.asarray(inputs["L0_ffn_W_d"], np.float32).T)
    xT = np.ascontiguousarray(x.T)
    in_maps = []
    for c in range(C):
        in_maps.append({
            "x_tok": np.ascontiguousarray(x[c * T:(c + 1) * T, :]),
            "x_ownT": np.ascontiguousarray(xT[:, c * T:(c + 1) * T]),
            "x_dT": np.ascontiguousarray(xT[c * ds:(c + 1) * ds, :]),
            "xpwT": xpw,
            "dtpwT": np.ascontiguousarray(dtpw[:, c * ds:(c + 1) * ds]),
            "dtb": np.ascontiguousarray(dtbv[c * ds:(c + 1) * ds].reshape(ds // 128, 128).T),
            "alog": np.ascontiguousarray(
                al[c * ds:(c + 1) * ds, :].reshape(ds // 128, 128, N).transpose(1, 0, 2).reshape(128, -1)),
            "dp": dpv,
            "wgT": wg,
            "wuT": wu,
            "wdT": wd,
        })
    return in_maps


def kernel(**inputs):
    cfg = FULL_CFG
    nc = build_nc(cfg)
    in_maps = make_in_maps(cfg, inputs)
    res = run_bass_kernel_spmd(nc, in_maps, core_ids=list(range(cfg["C"])))
    out = np.concatenate([np.asarray(r["out_tok"]) for r in res.results], axis=0)
    return out[None].astype(np.float32)


# version 15
# speedup vs baseline: 1.3947x; 1.0343x over previous
import numpy as np
import concourse.bass as bass
import concourse.mybir as mybir
from concourse.bass_utils import run_bass_kernel_spmd
from concourse.aot_env import allow_ici_all_to_all

F32 = mybir.dt.float32
BF16 = mybir.dt.bfloat16
F16 = mybir.dt.float16
AF = mybir.ActivationFunctionType
ALU = mybir.AluOpType
AX = mybir.AxisListType

FULL_CFG = dict(D=2048, H=8192, L=8192, N=16, R=128, C=8)


class Ctr:
    def __init__(self, handle):
        self.h = handle
        self.n = 0


def build_nc(cfg):
    stop_after = cfg.get("stop_after", "full")
    D, H, L, N, R, C = cfg["D"], cfg["H"], cfg["L"], cfg["N"], cfg["R"], cfg["C"]
    T = L // C
    ds = D // C
    nds = ds // 128
    ndc = D // 128
    NB = min(512, T)
    ntb = T // NB
    HRS = min(256, H)
    nhr = H // HRS
    nht_r = HRS // 128
    nh = H // 128
    OQ = min(512, D)
    noq = D // OQ
    no_t = OQ // 128
    TC = min(2048, L)
    nch = L // TC
    nblk_c = max(1, TC // T)
    ntt = T // 128
    assert TC % T == 0 and R == 128 and ds % 128 == 0

    nc = bass.Bass("TRN2", target_bir_lowering=False, detect_race_conditions=False)

    x_tok = nc.dram_tensor("x_tok", [T, D], F32, kind="ExternalInput").ap()
    x_ownT = nc.dram_tensor("x_ownT", [D, T], F32, kind="ExternalInput").ap()
    x_dT = nc.dram_tensor("x_dT", [ds, L], F32, kind="ExternalInput").ap()
    xpwT = nc.dram_tensor("xpwT", [D, 160], F32, kind="ExternalInput").ap()
    dtpwT = nc.dram_tensor("dtpwT", [R, ds], F32, kind="ExternalInput").ap()
    dtb = nc.dram_tensor("dtb", [128, nds], F32, kind="ExternalInput").ap()
    alog = nc.dram_tensor("alog", [128, nds * N], F32, kind="ExternalInput").ap()
    dp = nc.dram_tensor("dp", [128, D // 128], F32, kind="ExternalInput").ap()
    wgT_t = nc.dram_tensor("wgT", [D, H], F32, kind="ExternalInput")
    wuT_t = nc.dram_tensor("wuT", [D, H], F32, kind="ExternalInput")
    wdT_t = nc.dram_tensor("wdT", [H, D], F32, kind="ExternalInput")
    wgT = wgT_t.ap()
    wuT = wuT_t.ap()
    wdT = wdT_t.ap()
    out_tok = nc.dram_tensor("out_tok", [T, D], F32, kind="ExternalOutput").ap()

    scr_tv_t = nc.dram_tensor("scr_tv", [8, T], F32, kind="Internal")
    aga_in_t = nc.dram_tensor("aga_in", [160, T], BF16, kind="Internal")
    aga_out_t = nc.dram_tensor("aga_out", [160 * C, T], BF16, kind="Internal")
    agb_out_t = nc.dram_tensor("agb_out", [2 * C, T], F32, kind="Internal")
    scr_dtT_t = nc.dram_tensor("scr_dtT", [ds, L], F32, kind="Internal")
    p_scr_t = nc.dram_tensor("p_scr", [H, T], BF16, kind="Internal")
    ffn_scr_t = nc.dram_tensor("ffn_scr", [D, T], BF16, kind="Internal")
    y_in_t = nc.dram_tensor("y_a2a_in", [C * ds, T], BF16, kind="Internal")
    y_out_t = nc.dram_tensor("y_a2a_out", [C * ds, T], BF16, kind="Internal")
    scr_tv = scr_tv_t.ap()
    aga_in = aga_in_t.ap()
    aga_out = aga_out_t.ap()
    agb_out = agb_out_t.ap()
    scr_dtT = scr_dtT_t.ap()
    p_scr = p_scr_t.ap()
    ffn_scr = ffn_scr_t.ap()
    y_in = y_in_t.ap()
    y_out = y_out_t.ap()

    A = nc.alloc_sbuf_tensor
    AAT = nc.alloc_sbuf_tensor_at

    def addr_of(h):
        return nc.lookup_mloc(h).addr

    sb_xtok = A("sb_xtok", [128, D], F32)
    sb_sq = A("sb_sq", [128, D], F32)
    sb_sq2 = A("sb_sq2", [128, D], F32)
    sb_p0a = A("sb_p0a", [128, 3072], F32)
    sb_tv = A("sb_tv", [128, 8], F32)
    sb_red = A("sb_red", [128, 12], F32)
    sb_xod = A("sb_xod", [128, T], F32)
    sb_xnTf = A("sb_xnTf", [128, T], F32)
    sb_xnb = [A(f"sb_xnb{i}", [128, T], BF16) for i in range(2)]
    sb_kT = A("sb_kT", [128, ndc * T], BF16)
    sb_mb = A("sb_mb", [128, T], F32)
    sb_rb = A("sb_rb", [128, T], F32)
    sb_Ab = A("sb_Ab", [128, T], F32)
    sb_Bb = A("sb_Bb", [128, T], F32)
    sb_iv2 = A("sb_iv2", [128, T], BF16)
    sb_xpw = A("sb_xpw", [128, ndc * 160], BF16)
    sb_dtw = A("sb_dtw", [128, ds], BF16)
    sb_ag2 = A("sb_ag2", [32, T], BF16)
    sb_dtbc = A("sb_dtbc", [128, nds], F32)
    sb_acol = A("sb_acol", [128, nds * N], F32)
    sb_sgn_g = [A(f"sb_sgn_g{i}", [128, ndc * HRS], BF16) for i in range(2)]
    sb_sgn_u = [A(f"sb_sgn_u{i}", [128, ndc * HRS], BF16) for i in range(2)]
    sb_q = [A(f"sb_q{i}", [128, NB], BF16) for i in range(2)]
    sb_evg = [A(f"sb_evg{i}", [128, NB], BF16) for i in range(2)]
    sb_evu = [A(f"sb_evu{i}", [128, NB], BF16) for i in range(2)]
    sb_ev1 = A("sb_ev1", [128, NB], BF16)
    sb_ev2 = A("sb_ev2", [128, NB], BF16)
    sb_ev3 = A("sb_ev3", [128, NB], BF16)
    sb_ev4 = A("sb_ev4", [128, NB], BF16)
    sb_dtT = A("sb_dtT", [128, TC], F32)
    sb_xdc = A("sb_xdc", [128, TC], F32)
    sb_h = A("sb_h", [128, TC], BF16)
    sb_bc = [A(f"sb_bc{i}", [128, 2 * TC], BF16) for i in range(3)]
    sb_hst = A("sb_hst", [128, max(1, nds * N)], F32)
    sb_ytmp = A("sb_ytmp", [128, TC], BF16)
    sb_yacc = A("sb_yacc", [128, TC], BF16)

    sb_dpb = A("sb_dpb", [128, ndc], F32)
    sb_cst = A("sb_cst", [128, 4], F32)
    a_xtok = addr_of(sb_xtok)
    sb_pt = [AAT(f"sb_pt{i}", [128, 2 * T], BF16, offset=a_xtok + i * (4 * T)) for i in range(2)]
    a_sq = addr_of(sb_sq)
    sb_wds = None
    a_sq2 = addr_of(sb_sq2)
    sb_mab = AAT("sb_mab", [128, TC], BF16, offset=a_sq2)
    sb_rab = AAT("sb_rab", [128, TC], BF16, offset=a_sq2 + 2 * TC)
    a_p0a = addr_of(sb_p0a)
    sb_sp1 = AAT("sb_sp1", [128, NB], F32, offset=a_p0a)
    sb_sp2 = AAT("sb_sp2", [128, NB], F32, offset=a_p0a + 4 * NB)
    sb_sp3 = AAT("sb_sp3", [128, NB], F32, offset=a_p0a + 8 * NB)
    sb_f16 = AAT("sb_f16", [128, T], F16, offset=a_p0a + 12 * NB)
    sb_dlow = AAT("sb_dlow", [128, T], BF16, offset=a_p0a + 12 * NB + 2 * T)
    sb_ag1 = AAT("sb_ag1", [128, T], BF16, offset=a_p0a + 12 * NB + 4 * T)
    sb_dtx = AAT("sb_dtx", [128, TC], BF16, offset=a_p0a)
    sb_ab3 = None
    sb_bx = AAT("sb_bx", [128, TC], BF16, offset=a_p0a + 4 * TC)
    a_dtT = addr_of(sb_dtT)
    sb_acc = [AAT(f"sb_acc{i}", [128, T], F32, offset=a_dtT + i * 4 * T) for i in range(2)]
    a_h = addr_of(sb_h)
    sb_ycol = [AAT(f"sb_ycol{i}", [128, T], BF16, offset=a_h + i * 2 * T) for i in range(2)]
    a_yt = addr_of(sb_ytmp)
    sb_fload = [AAT(f"sb_fload{i}", [128, T], BF16, offset=a_yt + i * 2 * T) for i in range(2)]
    a_xdc = addr_of(sb_xdc)
    sb_ot = [AAT(f"sb_ot{i}", [128, 128], F32, offset=a_xdc + i * 512) for i in range(2)]
    sb_ident = AAT("sb_ident", [128, 128], F32, offset=a_xdc + 1024)
    def _fit(name, shape, dtype, host, host_bytes):
        nbytes = shape[1] * (2 if dtype in (BF16, F16) else 4)
        if nbytes <= host_bytes:
            return AAT(name, shape, dtype, offset=addr_of(host))
        return A(name, shape, dtype)
    sb_ab3 = [AAT("sb_ab0", [128, TC], BF16, offset=a_p0a + 2 * TC),
              _fit("sb_ab1", [128, TC], BF16, sb_xod, 4 * T),
              _fit("sb_ab2", [128, TC], BF16, sb_xnTf, 4 * T)]
    HH4 = min(4, nh)
    sb_wds = [AAT(f"sb_wds{i}", [128, HH4 * OQ], BF16, offset=a_sq + i * (2 * HH4 * OQ)) for i in range(2)]
    sb_fs = [AAT(f"sb_fs{i}", [128, NB], BF16, offset=addr_of(sb_q[i])) for i in range(2)]
    ps = [nc.alloc_psum_tensor(f"ps{i}", [128, 512], F32) for i in range(8)]

    prog = {e: [] for e in ["sync", "gpsimd", "tensor", "vector", "scalar"]}
    ctrs = {}
    sem_keep = []

    def S(name):
        c = ctrs.get(name)
        if c is None:
            g = nc.semaphore(name)
            h = g.__enter__()
            sem_keep.append(g)
            c = Ctr(h)
            ctrs[name] = c
        return c

    self_ctr = {}

    def op(e, fn, waits=(), inc=None, k=1):
        own = self_ctr.get(e)
        wl = [(c.h, v) for (c, v) in waits if v > 0 and c is not own]
        dd = {}
        for h, v in wl:
            dd[h] = max(dd.get(h, 0), v)
        wl = list(dd.items())
        if inc is not None:
            inc.n += k

        def run(eng, fn=fn, wl=wl, inc=inc, k=k):
            for h, v in wl:
                eng.wait_ge(h, v)
            ins = fn(eng)
            if inc is not None:
                ins.then_inc(inc.h, k)
        prog[e].append(run)

    sv = S("sv")
    ss = S("ss")
    stt = S("stt")
    sgp = S("sgp")
    scc_b = S("scc_b")
    scc_a = S("scc_a")
    scc_y = S("scc_y")
    s_in = S("s_in")
    s_x3 = S("s_x3")
    s_tv = S("s_tv")
    s_scr = S("s_scr")
    s_bco = S("s_bco")
    s_ag = S("s_ag")
    s_dts = S("s_dts")
    s_dtl = S("s_dtl")
    s_bcs = [S("s_bc0"), S("s_bc1"), S("s_bc2")]
    s_mrb = S("s_mrb")
    s_wf = [S("s_wf0"), S("s_wf1")]
    s_p = [S("s_p0"), S("s_p1")]
    s_pl = [S("s_pl0"), S("s_pl1")]
    s_wd = [S("s_wd0"), S("s_wd1")]
    s_y = S("s_y")
    s_fsp = [S("s_fsp0"), S("s_fsp1")]
    fs_slot_ms = [0, 0]
    s_out = S("s_out")

    self_ctr.update({"tensor": stt})

    def VOP(fn, waits=(), fifo=False):
        op("vector", fn, waits=list(waits) + ([] if fifo else [(sv, sv.n)]), inc=sv)

    def SOP(fn, waits=(), fifo=False):
        op("scalar", fn, waits=list(waits) + ([] if fifo else [(ss, ss.n)]), inc=ss)

    def GOP(fn, waits=()):
        op("gpsimd", fn, waits=list(waits) + [(sgp, sgp.n)], inc=sgp)

    def DMA(e, out_ap, in_ap, ctr, waits=()):
        op(e, lambda eng, o=out_ap, i=in_ap: eng.dma_start(out=o, in_=i),
           waits=waits, inc=ctr, k=16)

    DMA("sync", sb_dtbc[:, :], dtb, s_in)
    DMA("sync", sb_acol[:, :], alog, s_in)
    DMA("sync", sb_dpb[:, :], dp, s_in)
    DMA("sync", sb_sq[0:R, 0:ds], dtpwT, s_in)
    in0 = s_in.n
    VOP(lambda e: e.tensor_copy(sb_dtw[0:R, :], sb_sq[0:R, 0:ds]), waits=[(s_in, in0)])
    v_dtw = sv.n
    SOP(lambda e: e.activation(sb_acol[:, :], sb_acol[:, :], AF.Exp, bias=0.0, scale=1.0),
        waits=[(s_in, in0)])
    VOP(lambda e: e.tensor_scalar_mul(sb_acol[:, :], sb_acol[:, :], -1.0), waits=[(ss, ss.n)])
    v_acol = sv.n
    GOP(lambda e: e.memset(sb_cst[:, 0:1], 1e-5))
    GOP(lambda e: e.memset(sb_cst[:, 1:2], 1e-8))
    GOP(lambda e: e.memset(sb_cst[:, 2:3], 1.0))
    gp_cst = sgp.n
    GOP(lambda e: e.memset(sb_hst[:, :], 0.0))
    gp_hst0 = sgp.n

    if stop_after == "params":
        _finish_early(prog, ctrs, out_tok, sb_sq, S, op, T, D, ndc)
        return _instantiate(nc, prog)

    for tt in range(ntt):
        DMA("sync", sb_xtok[:, :], x_tok[tt * 128:(tt + 1) * 128, :], s_tv,
            waits=[(sv, sv.n), (ss, ss.n)] if tt > 0 else [])
        SOP(lambda e: e.activation(sb_sq[:, 0:D], sb_xtok[:, :], AF.Identity, bias=0.0,
                                   scale=1.0, accum_out=sb_red[:, 0:1]),
            waits=[(s_tv, s_tv.n), (s_scr, s_scr.n)])
        SOP(lambda e: e.activation(sb_sq[:, 0:D], sb_xtok[:, :], AF.Square, bias=0.0,
                                   scale=1.0, accum_out=sb_red[:, 1:2]))
        ss_sums = ss.n
        VOP(lambda e: e.tensor_scalar_mul(sb_tv[:, 0:1], sb_red[:, 0:1], 1.0 / D),
            waits=[(ss, ss_sums)])
        VOP(lambda e: e.tensor_mul(sb_red[:, 2:3], sb_tv[:, 0:1], sb_tv[:, 0:1]))
        VOP(lambda e: e.tensor_scalar_mul(sb_red[:, 3:4], sb_red[:, 1:2], 1.0 / D))
        VOP(lambda e: e.tensor_sub(sb_red[:, 4:5], sb_red[:, 3:4], sb_red[:, 2:3]))
        SOP(lambda e: e.activation(sb_red[:, 5:6], sb_red[:, 4:5], AF.Ln, bias=sb_cst[:, 0:1], scale=1.0),
            waits=[(sv, sv.n), (sgp, gp_cst)])
        SOP(lambda e: e.activation(sb_tv[:, 1:2], sb_red[:, 5:6], AF.Exp, bias=0.0, scale=-0.5))
        ss_rstd = ss.n
        VOP(lambda e: e.tensor_scalar(sb_sq[:, 0:D], sb_xtok[:, :], sb_tv[:, 0:1], sb_tv[:, 1:2],
                                      ALU.subtract, ALU.mult), waits=[(ss, ss_rstd)])
        v_xn = sv.n
        SOP(lambda e: e.activation(sb_sq2[:, 0:D], sb_sq[:, 0:D], AF.Identity, bias=0.0,
                                   scale=1.0, accum_out=sb_red[:, 6:7]), waits=[(sv, v_xn)])
        SOP(lambda e: e.activation(sb_sq2[:, 0:D], sb_sq[:, 0:D], AF.Square, bias=0.0,
                                   scale=1.0, accum_out=sb_red[:, 7:8]))
        ss_sums2 = ss.n
        VOP(lambda e: e.tensor_scalar_mul(sb_tv[:, 6:7], sb_red[:, 6:7], 1.0 / D),
            waits=[(ss, ss_sums2)])
        VOP(lambda e: e.tensor_mul(sb_red[:, 8:9], sb_tv[:, 6:7], sb_tv[:, 6:7]))
        VOP(lambda e: e.tensor_scalar_mul(sb_red[:, 9:10], sb_red[:, 7:8], 1.0 / D))
        VOP(lambda e: e.tensor_sub(sb_red[:, 10:11], sb_red[:, 9:10], sb_red[:, 8:9]))
        SOP(lambda e: e.activation(sb_red[:, 11:12], sb_red[:, 10:11], AF.Ln, bias=sb_cst[:, 1:2], scale=1.0),
            waits=[(sv, sv.n)])
        SOP(lambda e: e.activation(sb_tv[:, 7:8], sb_red[:, 11:12], AF.Exp, bias=0.0, scale=-0.5))
        ss_rstd2 = ss.n
        VOP(lambda e: e.tensor_scalar(sb_sq2[:, 0:D], sb_sq[:, 0:D], sb_tv[:, 6:7], sb_tv[:, 7:8],
                                      ALU.subtract, ALU.mult), waits=[(ss, ss_rstd2)])
        VOP(lambda e: e.tensor_reduce(sb_red[:, 8:9], sb_sq2[:, 0:D], AX.X, ALU.max,
                                      apply_absolute_value=True))
        VOP(lambda e: e.reciprocal(sb_red[:, 9:10], sb_red[:, 8:9]))
        VOP(lambda e: e.tensor_scalar_mul(sb_red[:, 10:11], sb_red[:, 9:10], 127.0))
        VOP(lambda e: e.tensor_scalar_mul(sb_tv[:, 4:5], sb_red[:, 8:9], 2.0 / 127.0))
        VOP(lambda e: e.tensor_mul(sb_red[:, 11:12], sb_tv[:, 7:8], sb_red[:, 10:11]))
        VOP(lambda e: e.tensor_mul(sb_tv[:, 2:3], sb_tv[:, 1:2], sb_red[:, 11:12]))
        VOP(lambda e: e.tensor_mul(sb_red[:, 9:10], sb_tv[:, 0:1], sb_tv[:, 1:2]))
        VOP(lambda e: e.tensor_add(sb_red[:, 9:10], sb_red[:, 9:10], sb_tv[:, 6:7]))
        VOP(lambda e: e.tensor_mul(sb_tv[:, 3:4], sb_red[:, 9:10], sb_red[:, 11:12]))
        for r in range(5):
            DMA("gpsimd", scr_tv[r:r + 1, tt * 128:(tt + 1) * 128], sb_tv[:, r:r + 1],
                s_scr, waits=[(sv, sv.n)])
    v_stats = sv.n
    scr_stats = s_scr.n

    op("gpsimd",
       lambda e: e.collective_compute("AllGather", ALU.bypass,
                                      replica_groups=[list(range(C))],
                                      ins=[scr_tv[0:2, :].opt()], outs=[agb_out.opt()]),
       waits=[(s_scr, scr_stats)], inc=scc_b, k=1)
    cc_agb = scc_b.n

    for dst, r in [(sb_mb, 0), (sb_rb, 1), (sb_Ab, 2), (sb_Bb, 3), (sb_iv2, 4)]:
        DMA("gpsimd", dst[:, :], bass.AP(scr_tv_t, r * T, [[0, 128], [1, T]]),
            s_bco, waits=[(s_scr, scr_stats)])
    bco = s_bco.n

    if stop_after == "agb":
        _finish_early(prog, ctrs, out_tok, sb_sq, S, op, T, D, ndc)
        return _instantiate(nc, prog)

    for dc in range(ndc):
        DMA("sync", sb_sq[:, 0:160], xpwT[dc * 128:(dc + 1) * 128, :], s_x3,
            waits=[(sv, sv.n)])
        VOP(lambda e, dc=dc: e.tensor_copy(sb_xpw[:, dc * 160:(dc + 1) * 160], sb_sq[:, 0:160]),
            waits=[(s_x3, s_x3.n)])
    v_xpw = sv.n

    nxpj = T // NB
    xp_mm_dc = []
    for dc in range(ndc):
        DMA("sync", sb_xod[:, :], x_ownT[dc * 128:(dc + 1) * 128, :], s_x3,
            waits=[(sv, sv.n)] if dc > 0 else [(sv, v_xpw)])
        par = dc % 2
        VOP(lambda e: e.tensor_sub(sb_xnTf[:, :], sb_xod[:, :], sb_mb[:, :]),
            waits=[(s_x3, s_x3.n), (s_bco, bco)])
        VOP(lambda e, par=par: e.tensor_mul(sb_xnb[par][:, :], sb_xnTf[:, :], sb_rb[:, :]),
            waits=([(stt, xp_mm_dc[dc - 2])] if dc >= 2 else []))
        v_xnb = sv.n
        for j in range(nxpj):
            op("tensor",
               lambda e, dc=dc, j=j, par=par: e.matmul(
                   ps[j][:, 0:NB], sb_xpw[:, dc * 160:dc * 160 + 128],
                   sb_xnb[par][:, j * NB:(j + 1) * NB],
                   start=(dc == 0), stop=(dc == ndc - 1)),
               waits=[(sv, v_xnb), (sv, v_xpw)] if j == 0 else ())
        for j in range(nxpj):
            op("tensor",
               lambda e, dc=dc, j=j, par=par: e.matmul(
                   ps[2 + j][0:32, 0:NB], sb_xpw[:, dc * 160 + 128:dc * 160 + 160],
                   sb_xnb[par][:, j * NB:(j + 1) * NB],
                   start=(dc == 0), stop=(dc == ndc - 1)),
               inc=(stt if j == nxpj - 1 else None))
        xp_mm_dc.append(stt.n)
        VOP(lambda e: e.tensor_mul(sb_xnTf[:, :], sb_xod[:, :], sb_Ab[:, :]))
        VOP(lambda e: e.tensor_sub(sb_xnTf[:, :], sb_xnTf[:, :], sb_Bb[:, :]))
        VOP(lambda e: e.tensor_scalar(sb_f16[:, :], sb_xnTf[:, :], 1536.0, None, ALU.add))
        VOP(lambda e, dc=dc: e.tensor_scalar(sb_kT[:, dc * T:(dc + 1) * T], sb_f16[:, :],
                                             1536.0, None, ALU.subtract))
    v_kT = sv.n
    mm_xp = stt.n
    SOP(lambda e: e.activation(sb_ag1[:, 0:NB], ps[0][:, 0:NB],
                               AF.Identity, bias=0.0, scale=1.0), waits=[(stt, mm_xp), (sv, v_kT)])
    if nxpj > 1:
        SOP(lambda e: e.activation(sb_ag1[:, NB:2 * NB], ps[1][:, 0:NB],
                                   AF.Identity, bias=0.0, scale=1.0))
    SOP(lambda e: e.activation(sb_ag2[0:32, 0:NB], ps[2][0:32, 0:NB],
                               AF.Identity, bias=0.0, scale=1.0))
    if nxpj > 1:
        SOP(lambda e: e.activation(sb_ag2[0:32, NB:2 * NB], ps[3][0:32, 0:NB],
                                   AF.Identity, bias=0.0, scale=1.0))
    ss_aga = ss.n
    DMA("gpsimd", aga_in[0:128, :], sb_ag1[:, :], s_scr, waits=[(ss, ss_aga)])
    DMA("gpsimd", aga_in[128:160, :], sb_ag2[0:32, :], s_scr)
    op("gpsimd",
       lambda e: e.collective_compute("AllGather", ALU.bypass,
                                      replica_groups=[list(range(C))],
                                      ins=[aga_in.opt()], outs=[aga_out.opt()]),
       waits=[(s_scr, s_scr.n)], inc=scc_a, k=1)
    cc_aga = scc_a.n

    if stop_after == "kx":
        _finish_early(prog, ctrs, out_tok, sb_sq, S, op, T, D, ndc)
        return _instantiate(nc, prog)

    dt_relu_ms = [0, 0]
    for dt in range(nds):
        for blk in range(C):
            DMA("sync", sb_dlow[:, 0:T], aga_out[blk * 160:blk * 160 + 128, :], s_ag,
                waits=[(scc_a, cc_aga)] + ([(stt, stt.n)] if (dt or blk) else []))
            for j in range(T // NB):
                it = (dt * C + blk) * (T // NB) + j
                bank = 2 + (it % 2)
                op("tensor",
                   lambda e, dt=dt, j=j, bank=bank: e.matmul(ps[bank][:, 0:NB],
                                                             sb_dtw[0:R, dt * 128:(dt + 1) * 128],
                                                             sb_dlow[0:R, j * NB:(j + 1) * NB],
                                                             start=True, stop=True),
                   waits=[(s_ag, s_ag.n), (sv, v_dtw), (ss, dt_relu_ms[it % 2]), (sv, sv.n)],
                   inc=stt)
                mm_dt = stt.n
                SOP(lambda e, dt=dt, bank=bank: e.activation(sb_sp1[:, 0:NB], ps[bank][:, 0:NB], AF.Abs,
                                                             bias=sb_dtbc[:, dt:dt + 1], scale=1.0),
                    waits=[(stt, mm_dt)])
                SOP(lambda e: e.activation(sb_sp2[:, 0:NB], sb_sp1[:, 0:NB], AF.Exp,
                                           bias=0.0, scale=-1.0))
                SOP(lambda e: e.activation(sb_sp1[:, 0:NB], sb_sp2[:, 0:NB], AF.Ln,
                                           bias=sb_cst[:, 2:3], scale=1.0))
                SOP(lambda e, dt=dt, bank=bank: e.activation(sb_sp3[:, 0:NB], ps[bank][:, 0:NB], AF.Relu,
                                                             bias=sb_dtbc[:, dt:dt + 1], scale=1.0),
                    waits=[(s_dts, s_dts.n)])
                dt_relu_ms[it % 2] = ss.n
                VOP(lambda e: e.tensor_add(sb_sp3[:, 0:NB], sb_sp3[:, 0:NB], sb_sp1[:, 0:NB]),
                    waits=[(ss, ss.n), (s_dts, s_dts.n)])
                DMA("gpsimd", scr_dtT[dt * 128:(dt + 1) * 128, blk * T + j * NB:blk * T + (j + 1) * NB],
                    sb_sp3[:, 0:NB], s_dts, waits=[(sv, sv.n)])
    dts_all = s_dts.n

    scan_chunk_ready = {}
    bc_last_use = [0, 0, 0]
    bc_loaded = {}

    def emit_scan_bc(dt, ch, n):
        ring = n % 3
        blk0 = (ch * TC) // T
        w = [(scc_a, cc_aga)] if (dt, ch, n) == (0, 0, 0) else [(sv, bc_last_use[ring])]
        row_b = (blk0 * 160 + 128 + n) * T
        row_c = (blk0 * 160 + 144 + n) * T
        DMA("gpsimd", sb_bc[ring][:, 0:TC].rearrange("p (b t) -> p b t", b=nblk_c),
            bass.AP(aga_out_t, row_b, [[0, 128], [160 * T, nblk_c], [1, T]]),
            s_bcs[ring], waits=w)
        DMA("gpsimd", sb_bc[ring][:, TC:2 * TC].rearrange("p (b t) -> p b t", b=nblk_c),
            bass.AP(aga_out_t, row_c, [[0, 128], [160 * T, nblk_c], [1, T]]),
            s_bcs[ring])
        bc_loaded[(dt, ch, n)] = s_bcs[ring].n

    def emit_scan_chunk_prep(dt, ch):
        DMA("gpsimd", sb_dtT[:, :], scr_dtT[dt * 128:(dt + 1) * 128, ch * TC:(ch + 1) * TC],
            s_dtl, waits=[(s_dts, dts_all), (sv, sv.n), (ss, ss.n)])
        DMA("gpsimd", sb_xdc[:, :], x_dT[dt * 128:(dt + 1) * 128, ch * TC:(ch + 1) * TC], s_dtl)
        blk0 = (ch * TC) // T
        for b in range(nblk_c):
            DMA("gpsimd", sb_mab[:, b * T:(b + 1) * T],
                bass.AP(agb_out_t, (2 * (blk0 + b)) * T, [[0, 128], [1, T]]), s_mrb,
                waits=[(scc_b, cc_agb), (sv, sv.n)] if b == 0 else [])
            DMA("gpsimd", sb_rab[:, b * T:(b + 1) * T],
                bass.AP(agb_out_t, (2 * (blk0 + b) + 1) * T, [[0, 128], [1, T]]), s_mrb)
        VOP(lambda e: e.tensor_sub(sb_xdc[:, :], sb_xdc[:, :], sb_mab[:, :]),
            waits=[(s_dtl, s_dtl.n), (s_mrb, s_mrb.n)])
        VOP(lambda e: e.tensor_mul(sb_xdc[:, :], sb_xdc[:, :], sb_rab[:, :]))
        VOP(lambda e: e.tensor_mul(sb_dtx[:, :], sb_dtT[:, :], sb_xdc[:, :]))
        scan_chunk_ready[(dt, ch)] = sv.n
        emit_scan_bc(dt, ch, 0)
        emit_scan_bc(dt, ch, 1)
        emit_scan_exp(dt, ch, 0)
        emit_scan_exp(dt, ch, 1)

    scan_exp_ms = {}
    scan_done_ms = {}

    def emit_scan_exp(dt, ch, n):
        slot = n % 3
        w = [(sv, scan_chunk_ready[(dt, ch)])]
        prev = (dt, ch, n - 3)
        if prev in scan_done_ms:
            w.append((sv, scan_done_ms[prev]))
        SOP(lambda e, dt=dt, n=n, slot=slot: e.activation(
            sb_ab3[slot][:, :], sb_dtT[:, :], AF.Exp, bias=0.0,
            scale=sb_acol[:, dt * N + n:dt * N + n + 1]), waits=w, fifo=True)
        scan_exp_ms[(dt, ch, n)] = ss.n

    def emit_scan_group(dt, ch, n):
        ring = n % 3
        slot = n % 3
        ahead = n + 2
        if ahead < N:
            if (dt, ch, ahead) not in bc_loaded:
                emit_scan_bc(dt, ch, ahead)
            if (dt, ch, ahead) not in scan_exp_ms:
                emit_scan_exp(dt, ch, ahead)
        VOP(lambda e, ring=ring: e.tensor_mul(sb_bx[:, :], sb_dtx[:, :], sb_bc[ring][:, 0:TC]),
            waits=[(s_bcs[ring], bc_loaded[(dt, ch, n)])], fifo=True)
        VOP(lambda e, dt=dt, n=n, slot=slot: e.tensor_tensor_scan(
            sb_h[:, :], sb_ab3[slot][:, :], sb_bx[:, :],
            sb_hst[:, dt * N + n:dt * N + n + 1], ALU.mult, ALU.add),
            waits=[(ss, scan_exp_ms[(dt, ch, n)]), (sgp, gp_hst0)], fifo=True)
        scan_done_ms[(dt, ch, n)] = sv.n
        VOP(lambda e, dt=dt, n=n: e.tensor_copy(sb_hst[:, dt * N + n:dt * N + n + 1],
                                                sb_h[:, TC - 1:TC]), fifo=True)
        VOP(lambda e, ring=ring: e.tensor_mul(sb_ytmp[:, :], sb_h[:, :], sb_bc[ring][:, TC:2 * TC]),
            fifo=True)
        if n == 0:
            VOP(lambda e: e.tensor_copy(sb_yacc[:, :], sb_ytmp[:, :]),
                waits=[(s_y, s_y.n)], fifo=True)
        else:
            VOP(lambda e: e.tensor_add(sb_yacc[:, :], sb_yacc[:, :], sb_ytmp[:, :]), fifo=True)
        bc_last_use[ring] = sv.n

    def emit_scan_chunk_finish(dt, ch):
        blk0 = (ch * TC) // T
        for b in range(nblk_c):
            DMA("gpsimd", y_in[(blk0 + b) * ds + dt * 128:(blk0 + b) * ds + (dt + 1) * 128, :],
                sb_yacc[:, b * T:(b + 1) * T], s_y, waits=[(sv, sv.n)] if b == 0 else [])

    scan_steps = []
    for dt in range(nds):
        for ch in range(nch):
            scan_steps.append(("prep", dt, ch, 0))
            for n in range(N):
                scan_steps.append(("group", dt, ch, n))
            scan_steps.append(("finish", dt, ch, 0))
    scan_pos = [0]

    def emit_some_scan(k):
        if stop_after == "upnoscan":
            return
        done = 0
        while done < k and scan_pos[0] < len(scan_steps):
            kind, dt, ch, n = scan_steps[scan_pos[0]]
            if kind == "prep":
                emit_scan_chunk_prep(dt, ch)
            elif kind == "group":
                emit_scan_group(dt, ch, n)
            else:
                emit_scan_chunk_finish(dt, ch)
            scan_pos[0] += 1
            done += 1

    total_up_slots = nhr * nht_r * ntb
    scan_per_slot = max(1, -(-len(scan_steps) // total_up_slots))

    if stop_after == "p0":
        _finish_early(prog, ctrs, out_tok, sb_sq, S, op, T, D, ndc)
        return _instantiate(nc, prog)

    up_psum_free = {0: 0, 1: 0}
    up_hr_ms = []
    q_dma_ms = [0, 0]
    ev_user_ms = [0, 0]
    up_sgn_v = [0, 0]
    for hr in range(nhr):
        spar = hr % 2
        for gm, (wsrc_t, sgl) in enumerate([(wgT_t, sb_sgn_g), (wuT_t, sb_sgn_u)]):
            sgt = sgl[spar]
            w = [(sv, up_sgn_v[spar])]
            if hr >= 2 and gm == 0:
                w.append((stt, up_hr_ms[hr - 2]))
            src_ap = bass.AP(wsrc_t, hr * HRS, [[H, 128], [128 * H, ndc], [1, HRS]])
            dst_ap = sgt[:, :].rearrange("p (a f) -> p a f", a=ndc)
            DMA("gpsimd", dst_ap, src_ap, s_wf[gm], waits=w)
            VOP(lambda e, sgt=sgt: e.tensor_scalar(sgt[:, :], sgt[:, :], 0.0, 0.5,
                                                   ALU.is_ge, ALU.subtract),
                waits=[(s_wf[gm], s_wf[gm].n)])
        up_sgn_v[spar] = sv.n
        sgn_done_v = sv.n
        for ht in range(nht_r):
            for tb in range(ntb):
                hg = hr * nht_r + ht
                slot = (ht * ntb + tb) % 2
                gb, ub = 2 * slot, 2 * slot + 1
                tw = [(sv, sgn_done_v), (sv, v_kT), (ss, up_psum_free[slot])]
                for dc in range(ndc):
                    op("tensor",
                       lambda e, dc=dc, ht=ht, tb=tb, gb=gb, spar=spar: e.matmul(
                           ps[gb][:, 0:NB], sb_sgn_g[spar][:, dc * HRS + ht * 128:dc * HRS + ht * 128 + 128],
                           sb_kT[:, dc * T + tb * NB:dc * T + (tb + 1) * NB],
                           start=(dc == 0), stop=(dc == ndc - 1)),
                       waits=tw if dc == 0 else ())
                for dc in range(ndc):
                    last = dc == ndc - 1
                    op("tensor",
                       lambda e, dc=dc, ht=ht, tb=tb, ub=ub, last=last, spar=spar: e.matmul(
                           ps[ub][:, 0:NB], sb_sgn_u[spar][:, dc * HRS + ht * 128:dc * HRS + ht * 128 + 128],
                           sb_kT[:, dc * T + tb * NB:dc * T + (tb + 1) * NB],
                           start=(dc == 0), stop=last),
                       inc=(stt if last else None))
                mm_up = stt.n
                SOP(lambda e, gb=gb, slot=slot: e.activation(sb_evg[slot][:, :], ps[gb][:, 0:NB],
                                                             AF.Identity, bias=0.0, scale=1.0),
                    waits=[(stt, mm_up), (sv, ev_user_ms[slot])])
                SOP(lambda e, ub=ub, slot=slot: e.activation(sb_evu[slot][:, :], ps[ub][:, 0:NB],
                                                             AF.Identity, bias=0.0, scale=1.0))
                up_psum_free[slot] = ss.n
                ev_ss = ss.n
                VOP(lambda e, tb=tb, slot=slot: e.tensor_tensor(sb_ev1[:, :], sb_evg[slot][:, :],
                                                                sb_iv2[:, tb * NB:(tb + 1) * NB], ALU.mult),
                    waits=[(ss, ev_ss), (s_bco, bco)])
                SOP(lambda e: e.activation(sb_ev2[:, :], sb_ev1[:, :], AF.Tanh, bias=0.0, scale=0.5),
                    waits=[(sv, sv.n)])
                VOP(lambda e: e.tensor_scalar(sb_ev3[:, :], sb_ev2[:, :], 0.5, 0.5, ALU.mult, ALU.add),
                    waits=[(ss, ss.n)])
                VOP(lambda e: e.tensor_mul(sb_ev4[:, :], sb_ev1[:, :], sb_ev3[:, :]))
                qp = slot
                VOP(lambda e, qp=qp, slot=slot: e.tensor_mul(sb_q[qp][:, :], sb_ev4[:, :], sb_evu[slot][:, :]),
                    waits=[(s_p[qp], q_dma_ms[qp])])
                ev_user_ms[slot] = sv.n
                VOP(lambda e, tb=tb, qp=qp: e.tensor_mul(sb_q[qp][:, :], sb_q[qp][:, :],
                                                         sb_iv2[:, tb * NB:(tb + 1) * NB]))
                DMA("sync", p_scr[hg * 128:(hg + 1) * 128, tb * NB:(tb + 1) * NB], sb_q[qp][:, :],
                    s_p[qp], waits=[(sv, sv.n)])
                q_dma_ms[qp] = s_p[qp].n
                emit_some_scan(scan_per_slot)
        up_hr_ms.append(stt.n)
    emit_some_scan(len(scan_steps))
    p_all = (s_p[0].n, s_p[1].n)
    y_all = s_y.n

    if stop_after == "up":
        _finish_early(prog, ctrs, out_tok, sb_sq, S, op, T, D, ndc)
        return _instantiate(nc, prog)
    if stop_after == "upnoscan":
        _finish_early(prog, ctrs, out_tok, sb_sq, S, op, T, D, ndc)
        return _instantiate(nc, prog)

    def cc_y_fn(e):
        with allow_ici_all_to_all():
            return e.collective_compute("AllToAll", ALU.bypass,
                                        replica_groups=[list(range(C))],
                                        ins=[y_in.opt()], outs=[y_out.opt()])
    op("gpsimd", cc_y_fn, waits=[(s_y, y_all)], inc=scc_y, k=1)
    cc_y = scc_y.n

    dn_ms = []
    dn_sgn_ms = [0, 0]
    for oq in range(noq):
        for h4 in range(nh // HH4):
            par4 = h4 % 2
            g4 = oq * (nh // HH4) + h4
            w4 = [(stt, dn_ms[2 * g4 - 3])] if g4 >= 2 else []
            DMA("gpsimd",
                sb_wds[par4][:, :].rearrange("p (a f) -> p a f", a=HH4),
                bass.AP(wdT_t, (h4 * HH4 * 128) * D + oq * OQ, [[D, 128], [128 * D, HH4], [1, OQ]]),
                s_wd[par4], waits=[(sv, dn_sgn_ms[par4])] + w4)
            VOP(lambda e, par4=par4: e.tensor_scalar(sb_wds[par4][:, :], sb_wds[par4][:, :],
                                                     0.0, 0.5, ALU.is_ge, ALU.subtract),
                waits=[(s_wd[par4], s_wd[par4].n)])
            dn_sgn_ms[par4] = sv.n
            sgn_d = sv.n
            for hh2 in range(HH4 // 2):
                hh = h4 * (HH4 // 2) + hh2
                par = hh % 2
                gi = oq * (nh // 2) + hh
                w_slot = [(stt, dn_ms[gi - 2])] if gi >= 2 else []
                DMA("sync",
                    sb_pt[par][:, :].rearrange("p (a f) -> p a f", a=2),
                    p_scr[hh * 256:(hh + 1) * 256, :].rearrange("(a p) f -> p a f", p=128),
                    s_pl[par],
                    waits=([(s_p[0], p_all[0]), (s_p[1], p_all[1])] if gi == 0 else []) + w_slot)
                for hi in range(2):
                    h = hh * 2 + hi
                    hq = hh2 * 2 + hi
                    for ot in range(no_t):
                        for tb in range(ntb):
                            bank = ot * ntb + tb
                            w = []
                            if hi == 0 and ot == 0 and tb == 0:
                                w = [(sv, sgn_d), (s_pl[par], s_pl[par].n)]
                                if h == 0 and oq > 0:
                                    w.append((ss, ss.n))
                            op("tensor",
                               lambda e, par=par, par4=par4, hq=hq, hi=hi, ot=ot, tb=tb, h=h, bank=bank: e.matmul(
                                   ps[bank][:, 0:NB],
                                   sb_wds[par4][:, hq * OQ + ot * 128:hq * OQ + (ot + 1) * 128],
                                   sb_pt[par][:, hi * T + tb * NB:hi * T + (tb + 1) * NB],
                                   start=(h == 0), stop=(h == nh - 1)),
                               waits=w,
                               inc=(stt if (hi == 1 and ot == no_t - 1 and tb == ntb - 1) else None))
                dn_ms.append(stt.n)
        mm_dn = stt.n
        for ot in range(no_t):
            for tb in range(ntb):
                bank = ot * ntb + tb
                oc = oq * no_t + ot
                fpar = bank % 2
                SOP(lambda e, bank=bank, fpar=fpar: e.activation(
                    sb_fs[fpar][:, :], ps[bank][:, 0:NB],
                    AF.Identity, bias=0.0, scale=2.0),
                    waits=[(stt, mm_dn), (s_fsp[fpar], fs_slot_ms[fpar])])
                DMA("gpsimd", ffn_scr[oc * 128:(oc + 1) * 128, tb * NB:(tb + 1) * NB],
                    sb_fs[fpar][:, :], s_fsp[fpar], waits=[(ss, ss.n)])
                fs_slot_ms[fpar] = s_fsp[fpar].n
    ss_dn = ss.n

    if stop_after == "down":
        _finish_early(prog, ctrs, out_tok, sb_sq, S, op, T, D, ndc)
        return _instantiate(nc, prog)

    GOP(lambda e: e.memset(sb_ident[:, :], 1.0), waits=[(sv, sv.n), (ss, ss.n)])
    GOP(lambda e: e.affine_select(sb_ident[:, :], sb_ident[:, :], [[1, 128]],
                                  ALU.is_equal, 0.0, base=0, channel_multiplier=-1),
        waits=[(sgp, sgp.n)])
    gp_ident = sgp.n
    out_slot_ms = [0, 0]
    s_p3 = [S("s_p30"), S("s_p31")]
    p3_user_ms = [0, 0]
    acc_tr_ms = [0, 0]
    for dc in range(ndc):
        par = dc % 2
        w0 = [(sv, p3_user_ms[par])]
        if dc == 0:
            w0 += [(sv, sv.n), (ss, ss.n)]
        DMA("sync", sb_xod[:, :], x_ownT[dc * 128:(dc + 1) * 128, :], s_x3,
            waits=([(sv, sv.n), (ss, ss.n)] if dc == 0 else [(sv, sv.n)]))
        DMA("sync", sb_ycol[par][:, :], y_out[dc * 128:(dc + 1) * 128, :], s_p3[par],
            waits=([(scc_y, cc_y), (ss, ss.n)] if dc == 0 else []) + [(sv, p3_user_ms[par])])
        DMA("sync", sb_fload[par][:, :], ffn_scr[dc * 128:(dc + 1) * 128, :], s_p3[par],
            waits=([(s_fsp[0], s_fsp[0].n), (s_fsp[1], s_fsp[1].n)] if dc == 0 else []))
        x3n = s_x3.n
        p3n = s_p3[par].n
        VOP(lambda e: e.tensor_sub(sb_xnTf[:, :], sb_xod[:, :], sb_mb[:, :]),
            waits=[(s_x3, x3n), (stt, stt.n)] if dc == 0 else [(s_x3, x3n)])
        VOP(lambda e: e.tensor_mul(sb_xnTf[:, :], sb_xnTf[:, :], sb_rb[:, :]))
        VOP(lambda e, dc=dc, par=par: e.scalar_tensor_tensor(sb_acc[par][:, :], sb_xnTf[:, :],
                                                             sb_dpb[:, dc:dc + 1], sb_xod[:, :],
                                                             ALU.mult, ALU.add),
            waits=[(stt, acc_tr_ms[par])])
        VOP(lambda e, par=par: e.tensor_add(sb_acc[par][:, :], sb_acc[par][:, :], sb_ycol[par][:, :]),
            waits=[(s_p3[par], p3n)])
        VOP(lambda e, par=par: e.tensor_add(sb_acc[par][:, :], sb_acc[par][:, :], sb_fload[par][:, :]),
            waits=([(ss, ss_dn)] if dc == 0 else []))
        acc_done = sv.n
        p3_user_ms[par] = sv.n
        for tt in range(ntt):
            pslot = tt % 2
            op("tensor",
               lambda e, tt=tt, pslot=pslot, par=par: e.transpose(ps[pslot][:, 0:128],
                                                                  sb_acc[par][:, tt * 128:(tt + 1) * 128],
                                                                  sb_ident[:, :]),
               waits=[(sv, acc_done), (sgp, gp_ident)] +
                     ([(ss, ss.n)] if (dc * ntt + tt) >= 2 else []),
               inc=stt)
            tr_done = stt.n
            SOP(lambda e, pslot=pslot: e.copy(sb_ot[pslot][:, :], ps[pslot][:, 0:128]),
                waits=[(stt, tr_done), (s_out, out_slot_ms[pslot])])
            DMA("sync", out_tok[tt * 128:(tt + 1) * 128, dc * 128:(dc + 1) * 128],
                sb_ot[pslot][:, :], s_out, waits=[(ss, ss.n)])
            out_slot_ms[pslot] = s_out.n
        acc_tr_ms[par] = stt.n
    prog["sync"].append(lambda eng, h=s_out.h, v=s_out.n: eng.wait_ge(h, v))
    prog["gpsimd"].append(lambda eng, h=s_out.h, v=s_out.n: eng.wait_ge(h, v))

    return _instantiate(nc, prog)


def _finish_early(prog, ctrs, out_tok, sb_src, S, op, T, D, ndc):
    s_out = ctrs["s_out"]
    sv = ctrs["sv"]
    ss = ctrs["ss"]
    W = min(D, sb_src.shape[1])
    for tt in range(T // 128):
        for c0 in range(0, D, W):
            op("sync", lambda eng, tt=tt, c0=c0: eng.dma_start(
                out=out_tok[tt * 128:(tt + 1) * 128, c0:c0 + W],
                in_=sb_src[:, 0:W]), waits=[(sv, sv.n), (ss, ss.n)] if (tt == 0 and c0 == 0) else [],
                inc=s_out, k=16)
    prog["sync"].append(lambda eng, h=s_out.h, v=s_out.n: eng.wait_ge(h, v))
    prog["gpsimd"].append(lambda eng, h=s_out.h, v=s_out.n: eng.wait_ge(h, v))


def _instantiate(nc, prog):
    with nc.Block() as block:
        @block.sync
        def _(eng):
            for fn in prog["sync"]:
                fn(eng)

        @block.gpsimd
        def _(eng):
            for fn in prog["gpsimd"]:
                fn(eng)

        @block.tensor
        def _(eng):
            for fn in prog["tensor"]:
                fn(eng)

        @block.vector
        def _(eng):
            for fn in prog["vector"]:
                fn(eng)

        @block.scalar
        def _(eng):
            for fn in prog["scalar"]:
                fn(eng)

    return nc


def make_in_maps(cfg, inputs):
    D, H, L, N, R, C = cfg["D"], cfg["H"], cfg["L"], cfg["N"], cfg["R"], cfg["C"]
    T = L // C
    ds = D // C
    x = np.ascontiguousarray(np.asarray(inputs["L0_attn_x"], dtype=np.float32)[0])
    xpw = np.ascontiguousarray(np.asarray(inputs["L0_attn_x_proj_w"], np.float32).T)
    dtpw = np.ascontiguousarray(np.asarray(inputs["L0_attn_dt_proj_w"], np.float32).T)
    dtbv = np.asarray(inputs["L0_attn_dt_proj_b"], np.float32)
    al = np.asarray(inputs["L0_attn_A_log"], np.float32)
    dpv = np.ascontiguousarray(np.asarray(inputs["L0_attn_D_param"], np.float32).reshape(D // 128, 128).T)
    wg = np.ascontiguousarray(np.asarray(inputs["L0_ffn_W_g"], np.float32).T)
    wu = np.ascontiguousarray(np.asarray(inputs["L0_ffn_W_u"], np.float32).T)
    wd = np.ascontiguousarray(np.asarray(inputs["L0_ffn_W_d"], np.float32).T)
    xT = np.ascontiguousarray(x.T)
    in_maps = []
    for c in range(C):
        in_maps.append({
            "x_tok": np.ascontiguousarray(x[c * T:(c + 1) * T, :]),
            "x_ownT": np.ascontiguousarray(xT[:, c * T:(c + 1) * T]),
            "x_dT": np.ascontiguousarray(xT[c * ds:(c + 1) * ds, :]),
            "xpwT": xpw,
            "dtpwT": np.ascontiguousarray(dtpw[:, c * ds:(c + 1) * ds]),
            "dtb": np.ascontiguousarray(dtbv[c * ds:(c + 1) * ds].reshape(ds // 128, 128).T),
            "alog": np.ascontiguousarray(
                al[c * ds:(c + 1) * ds, :].reshape(ds // 128, 128, N).transpose(1, 0, 2).reshape(128, -1)),
            "dp": dpv,
            "wgT": wg,
            "wuT": wu,
            "wdT": wd,
        })
    return in_maps


def kernel(**inputs):
    cfg = FULL_CFG
    nc = build_nc(cfg)
    in_maps = make_in_maps(cfg, inputs)
    res = run_bass_kernel_spmd(nc, in_maps, core_ids=list(range(cfg["C"])))
    out = np.concatenate([np.asarray(r["out_tok"]) for r in res.results], axis=0)
    return out[None].astype(np.float32)


# version 16
# speedup vs baseline: 1.4390x; 1.0318x over previous
import numpy as np
import concourse.bass as bass
import concourse.mybir as mybir
from concourse.bass_utils import run_bass_kernel_spmd
from concourse.aot_env import allow_ici_all_to_all

F32 = mybir.dt.float32
BF16 = mybir.dt.bfloat16
F16 = mybir.dt.float16
AF = mybir.ActivationFunctionType
ALU = mybir.AluOpType
AX = mybir.AxisListType

FULL_CFG = dict(D=2048, H=8192, L=8192, N=16, R=128, C=8)


class Ctr:
    def __init__(self, handle):
        self.h = handle
        self.n = 0


def build_nc(cfg):
    stop_after = cfg.get("stop_after", "full")
    D, H, L, N, R, C = cfg["D"], cfg["H"], cfg["L"], cfg["N"], cfg["R"], cfg["C"]
    T = L // C
    ds = D // C
    nds = ds // 128
    ndc = D // 128
    NB = min(512, T)
    ntb = T // NB
    HRS = min(256, H)
    nhr = H // HRS
    nht_r = HRS // 128
    nh = H // 128
    OQ = min(512, D)
    noq = D // OQ
    no_t = OQ // 128
    TC = min(2048, L)
    nch = L // TC
    nblk_c = max(1, TC // T)
    ntt = T // 128
    assert TC % T == 0 and R == 128 and ds % 128 == 0

    nc = bass.Bass("TRN2", target_bir_lowering=False, detect_race_conditions=False)

    x_tok = nc.dram_tensor("x_tok", [T, D], F32, kind="ExternalInput").ap()
    x_ownT = nc.dram_tensor("x_ownT", [D, T], F32, kind="ExternalInput").ap()
    x_dT = nc.dram_tensor("x_dT", [ds, L], F32, kind="ExternalInput").ap()
    xpwT = nc.dram_tensor("xpwT", [D, 160], F32, kind="ExternalInput").ap()
    dtpwT = nc.dram_tensor("dtpwT", [R, ds], F32, kind="ExternalInput").ap()
    dtb = nc.dram_tensor("dtb", [128, nds], F32, kind="ExternalInput").ap()
    alog = nc.dram_tensor("alog", [128, nds * N], F32, kind="ExternalInput").ap()
    dp = nc.dram_tensor("dp", [128, D // 128], F32, kind="ExternalInput").ap()
    wgT_t = nc.dram_tensor("wgT", [D, H], F32, kind="ExternalInput")
    wuT_t = nc.dram_tensor("wuT", [D, H], F32, kind="ExternalInput")
    wdT_t = nc.dram_tensor("wdT", [H, D], F32, kind="ExternalInput")
    wgT = wgT_t.ap()
    wuT = wuT_t.ap()
    wdT = wdT_t.ap()
    out_tok = nc.dram_tensor("out_tok", [T, D], F32, kind="ExternalOutput").ap()

    scr_tv_t = nc.dram_tensor("scr_tv", [8, T], F32, kind="Internal")
    aga_in_t = nc.dram_tensor("aga_in", [160, T], BF16, kind="Internal")
    aga_out_t = nc.dram_tensor("aga_out", [160 * C, T], BF16, kind="Internal")
    agb_out_t = nc.dram_tensor("agb_out", [2 * C, T], F32, kind="Internal")
    scr_dtT_t = nc.dram_tensor("scr_dtT", [ds, L], F32, kind="Internal")
    p_scr_t = nc.dram_tensor("p_scr", [H, T], BF16, kind="Internal")
    ffn_scr_t = nc.dram_tensor("ffn_scr", [D, T], BF16, kind="Internal")
    y_in_t = nc.dram_tensor("y_a2a_in", [C * ds, T], BF16, kind="Internal")
    y_out_t = nc.dram_tensor("y_a2a_out", [C * ds, T], BF16, kind="Internal")
    scr_tv = scr_tv_t.ap()
    aga_in = aga_in_t.ap()
    aga_out = aga_out_t.ap()
    agb_out = agb_out_t.ap()
    scr_dtT = scr_dtT_t.ap()
    p_scr = p_scr_t.ap()
    ffn_scr = ffn_scr_t.ap()
    y_in = y_in_t.ap()
    y_out = y_out_t.ap()

    A = nc.alloc_sbuf_tensor
    AAT = nc.alloc_sbuf_tensor_at

    def addr_of(h):
        return nc.lookup_mloc(h).addr

    sb_xtok = A("sb_xtok", [128, D], F32)
    sb_sq = A("sb_sq", [128, D], F32)
    sb_sq2 = A("sb_sq2", [128, D], F32)
    sb_p0a = A("sb_p0a", [128, 3072], F32)
    sb_tv = A("sb_tv", [128, 8], F32)
    sb_red = A("sb_red", [128, 12], F32)
    sb_xod = A("sb_xod", [128, T], F32)
    sb_xnTf = A("sb_xnTf", [128, T], F32)
    sb_xnb = [A(f"sb_xnb{i}", [128, T], BF16) for i in range(2)]
    sb_kT = A("sb_kT", [128, ndc * T], BF16)
    sb_mb = A("sb_mb", [128, T], F32)
    sb_rb = A("sb_rb", [128, T], F32)
    sb_Ab = A("sb_Ab", [128, T], F32)
    sb_Bb = A("sb_Bb", [128, T], F32)
    sb_iv2 = A("sb_iv2", [128, T], BF16)
    sb_xpw = A("sb_xpw", [128, ndc * 160], BF16)
    sb_dtw = A("sb_dtw", [128, ds], BF16)
    sb_ag2 = A("sb_ag2", [32, T], BF16)
    sb_dtbc = A("sb_dtbc", [128, nds], F32)
    sb_acol = A("sb_acol", [128, nds * N], F32)
    sb_sgn_g = [A(f"sb_sgn_g{i}", [128, ndc * HRS], BF16) for i in range(2)]
    sb_sgn_u = [A(f"sb_sgn_u{i}", [128, ndc * HRS], BF16) for i in range(2)]
    sb_q = [A(f"sb_q{i}", [128, NB], BF16) for i in range(2)]
    sb_evg = [A(f"sb_evg{i}", [128, NB], BF16) for i in range(2)]
    sb_evu = [A(f"sb_evu{i}", [128, NB], BF16) for i in range(2)]
    sb_ev1 = A("sb_ev1", [128, NB], BF16)
    sb_ev2 = A("sb_ev2", [128, NB], BF16)
    sb_ev3 = A("sb_ev3", [128, NB], BF16)
    sb_ev4 = A("sb_ev4", [128, NB], BF16)
    sb_dtT = A("sb_dtT", [128, TC], F32)
    sb_xdc = A("sb_xdc", [128, TC], F32)
    sb_h = A("sb_h", [128, TC], BF16)
    sb_bc = [A(f"sb_bc{i}", [128, 2 * TC], BF16) for i in range(3)]
    sb_hst = A("sb_hst", [128, max(1, nds * N)], F32)
    sb_ytmp = A("sb_ytmp", [128, TC], BF16)
    sb_yacc = A("sb_yacc", [128, TC], BF16)

    sb_dpb = A("sb_dpb", [128, ndc], F32)
    sb_cst = A("sb_cst", [128, 4], F32)
    sb_idp = A("sb_idp", [128, 128], F32)
    sb_tvT = A("sb_tvT", [8, 128], F32)
    a_xtok = addr_of(sb_xtok)
    sb_pt = [AAT(f"sb_pt{i}", [128, 2 * T], BF16, offset=a_xtok + i * (4 * T)) for i in range(2)]
    a_sq = addr_of(sb_sq)
    sb_wds = None
    a_sq2 = addr_of(sb_sq2)
    sb_mab = AAT("sb_mab", [128, TC], BF16, offset=a_sq2)
    sb_rab = AAT("sb_rab", [128, TC], BF16, offset=a_sq2 + 2 * TC)
    a_p0a = addr_of(sb_p0a)
    sb_sp1 = AAT("sb_sp1", [128, NB], F32, offset=a_p0a)
    sb_sp2 = AAT("sb_sp2", [128, NB], F32, offset=a_p0a + 4 * NB)
    sb_sp3 = AAT("sb_sp3", [128, NB], F32, offset=a_p0a + 8 * NB)
    sb_f16 = AAT("sb_f16", [128, T], F16, offset=a_p0a + 12 * NB)
    sb_dlow = AAT("sb_dlow", [128, T], BF16, offset=a_p0a + 12 * NB + 2 * T)
    sb_ag1 = AAT("sb_ag1", [128, T], BF16, offset=a_p0a + 12 * NB + 4 * T)
    sb_dtx = AAT("sb_dtx", [128, TC], BF16, offset=a_p0a)
    sb_ab3 = None
    sb_bx = AAT("sb_bx", [128, TC], BF16, offset=a_p0a + 4 * TC)
    a_dtT = addr_of(sb_dtT)
    sb_acc = [AAT(f"sb_acc{i}", [128, T], F32, offset=a_dtT + i * 4 * T) for i in range(2)]
    a_h = addr_of(sb_h)
    sb_ycol = [AAT(f"sb_ycol{i}", [128, T], BF16, offset=a_h + i * 2 * T) for i in range(2)]
    a_yt = addr_of(sb_ytmp)
    sb_fload = [AAT(f"sb_fload{i}", [128, T], BF16, offset=a_yt + i * 2 * T) for i in range(2)]
    a_xdc = addr_of(sb_xdc)
    sb_ot = [AAT(f"sb_ot{i}", [128, 128], F32, offset=a_xdc + i * 512) for i in range(2)]
    sb_ident = AAT("sb_ident", [128, 128], F32, offset=a_xdc + 1024)
    def _fit(name, shape, dtype, host, host_bytes):
        nbytes = shape[1] * (2 if dtype in (BF16, F16) else 4)
        if nbytes <= host_bytes:
            return AAT(name, shape, dtype, offset=addr_of(host))
        return A(name, shape, dtype)
    sb_ab3 = [AAT("sb_ab0", [128, TC], BF16, offset=a_p0a + 2 * TC),
              _fit("sb_ab1", [128, TC], BF16, sb_xod, 4 * T),
              _fit("sb_ab2", [128, TC], BF16, sb_xnTf, 4 * T)]
    HH4 = min(4, nh)
    sb_wds = [AAT(f"sb_wds{i}", [128, HH4 * OQ], BF16, offset=a_sq + i * (2 * HH4 * OQ)) for i in range(2)]
    sb_fs = [AAT(f"sb_fs{i}", [128, NB], BF16, offset=addr_of(sb_q[i])) for i in range(2)]
    ps = [nc.alloc_psum_tensor(f"ps{i}", [128, 512], F32) for i in range(8)]

    prog = {e: [] for e in ["sync", "gpsimd", "tensor", "vector", "scalar"]}
    ctrs = {}
    sem_keep = []

    def S(name):
        c = ctrs.get(name)
        if c is None:
            g = nc.semaphore(name)
            h = g.__enter__()
            sem_keep.append(g)
            c = Ctr(h)
            ctrs[name] = c
        return c

    self_ctr = {}

    def op(e, fn, waits=(), inc=None, k=1):
        own = self_ctr.get(e)
        wl = [(c.h, v) for (c, v) in waits if v > 0 and c is not own]
        dd = {}
        for h, v in wl:
            dd[h] = max(dd.get(h, 0), v)
        wl = list(dd.items())
        if inc is not None:
            inc.n += k

        def run(eng, fn=fn, wl=wl, inc=inc, k=k):
            for h, v in wl:
                eng.wait_ge(h, v)
            ins = fn(eng)
            if inc is not None:
                ins.then_inc(inc.h, k)
        prog[e].append(run)

    sv = S("sv")
    ss = S("ss")
    stt = S("stt")
    sgp = S("sgp")
    scc_b = S("scc_b")
    scc_a = S("scc_a")
    scc_y = S("scc_y")
    s_in = S("s_in")
    s_x3 = S("s_x3")
    s_tv = S("s_tv")
    s_scr = S("s_scr")
    s_bco = S("s_bco")
    s_ag = S("s_ag")
    s_dts = S("s_dts")
    s_dtl = S("s_dtl")
    s_bcs = [S("s_bc0"), S("s_bc1"), S("s_bc2")]
    s_mrb = S("s_mrb")
    s_wf = [S("s_wf0"), S("s_wf1")]
    s_p = [S("s_p0"), S("s_p1")]
    s_pl = [S("s_pl0"), S("s_pl1")]
    s_wd = [S("s_wd0"), S("s_wd1")]
    s_y = S("s_y")
    s_fsp = [S("s_fsp0"), S("s_fsp1")]
    fs_slot_ms = [0, 0]
    s_out = S("s_out")

    self_ctr.update({"tensor": stt})

    def VOP(fn, waits=(), fifo=False):
        op("vector", fn, waits=list(waits) + ([] if fifo else [(sv, sv.n)]), inc=sv)

    def SOP(fn, waits=(), fifo=False):
        op("scalar", fn, waits=list(waits) + ([] if fifo else [(ss, ss.n)]), inc=ss)

    def GOP(fn, waits=()):
        op("gpsimd", fn, waits=list(waits) + [(sgp, sgp.n)], inc=sgp)

    def DMA(e, out_ap, in_ap, ctr, waits=()):
        op(e, lambda eng, o=out_ap, i=in_ap: eng.dma_start(out=o, in_=i),
           waits=waits, inc=ctr, k=16)

    DMA("sync", sb_dtbc[:, :], dtb, s_in)
    DMA("sync", sb_acol[:, :], alog, s_in)
    DMA("sync", sb_dpb[:, :], dp, s_in)
    DMA("sync", sb_sq[0:R, 0:ds], dtpwT, s_in)
    in0 = s_in.n
    VOP(lambda e: e.tensor_copy(sb_dtw[0:R, :], sb_sq[0:R, 0:ds]), waits=[(s_in, in0)])
    v_dtw = sv.n
    SOP(lambda e: e.activation(sb_acol[:, :], sb_acol[:, :], AF.Exp, bias=0.0, scale=1.0),
        waits=[(s_in, in0)])
    VOP(lambda e: e.tensor_scalar_mul(sb_acol[:, :], sb_acol[:, :], -1.0), waits=[(ss, ss.n)])
    v_acol = sv.n
    GOP(lambda e: e.memset(sb_cst[:, 0:1], 1e-5))
    GOP(lambda e: e.memset(sb_cst[:, 1:2], 1e-8))
    GOP(lambda e: e.memset(sb_cst[:, 2:3], 1.0))
    gp_cst = sgp.n
    GOP(lambda e: e.memset(sb_hst[:, :], 0.0))
    gp_hst0 = sgp.n
    GOP(lambda e: e.memset(sb_tv[:, :], 0.0))
    GOP(lambda e: e.memset(sb_idp[:, :], 1.0))
    GOP(lambda e: e.affine_select(sb_idp[:, :], sb_idp[:, :], [[1, 128]],
                                  ALU.is_equal, 0.0, base=0, channel_multiplier=-1),
        waits=[(sgp, sgp.n)])
    gp_idp = sgp.n

    if stop_after == "params":
        _finish_early(prog, ctrs, out_tok, sb_sq, S, op, T, D, ndc)
        return _instantiate(nc, prog)

    for tt in range(ntt):
        DMA("sync", sb_xtok[:, :], x_tok[tt * 128:(tt + 1) * 128, :], s_tv,
            waits=[(sv, sv.n), (ss, ss.n)] if tt > 0 else [])
        SOP(lambda e: e.activation(sb_sq[:, 0:D], sb_xtok[:, :], AF.Identity, bias=0.0,
                                   scale=1.0, accum_out=sb_red[:, 0:1]),
            waits=[(s_tv, s_tv.n), (s_scr, s_scr.n)])
        SOP(lambda e: e.activation(sb_sq[:, 0:D], sb_xtok[:, :], AF.Square, bias=0.0,
                                   scale=1.0, accum_out=sb_red[:, 1:2]))
        ss_sums = ss.n
        VOP(lambda e: e.tensor_scalar_mul(sb_tv[:, 0:1], sb_red[:, 0:1], 1.0 / D),
            waits=[(ss, ss_sums)])
        VOP(lambda e: e.tensor_mul(sb_red[:, 2:3], sb_tv[:, 0:1], sb_tv[:, 0:1]))
        VOP(lambda e: e.tensor_scalar_mul(sb_red[:, 3:4], sb_red[:, 1:2], 1.0 / D))
        VOP(lambda e: e.tensor_sub(sb_red[:, 4:5], sb_red[:, 3:4], sb_red[:, 2:3]))
        SOP(lambda e: e.activation(sb_red[:, 5:6], sb_red[:, 4:5], AF.Ln, bias=sb_cst[:, 0:1], scale=1.0),
            waits=[(sv, sv.n), (sgp, gp_cst)])
        SOP(lambda e: e.activation(sb_tv[:, 1:2], sb_red[:, 5:6], AF.Exp, bias=0.0, scale=-0.5))
        ss_rstd = ss.n
        VOP(lambda e: e.tensor_scalar(sb_sq[:, 0:D], sb_xtok[:, :], sb_tv[:, 0:1], sb_tv[:, 1:2],
                                      ALU.subtract, ALU.mult), waits=[(ss, ss_rstd)])
        v_xn = sv.n
        SOP(lambda e: e.activation(sb_sq2[:, 0:D], sb_sq[:, 0:D], AF.Identity, bias=0.0,
                                   scale=1.0, accum_out=sb_red[:, 6:7]), waits=[(sv, v_xn)])
        SOP(lambda e: e.activation(sb_sq2[:, 0:D], sb_sq[:, 0:D], AF.Square, bias=0.0,
                                   scale=1.0, accum_out=sb_red[:, 7:8]))
        ss_sums2 = ss.n
        VOP(lambda e: e.tensor_scalar_mul(sb_tv[:, 6:7], sb_red[:, 6:7], 1.0 / D),
            waits=[(ss, ss_sums2)])
        VOP(lambda e: e.tensor_mul(sb_red[:, 8:9], sb_tv[:, 6:7], sb_tv[:, 6:7]))
        VOP(lambda e: e.tensor_scalar_mul(sb_red[:, 9:10], sb_red[:, 7:8], 1.0 / D))
        VOP(lambda e: e.tensor_sub(sb_red[:, 10:11], sb_red[:, 9:10], sb_red[:, 8:9]))
        SOP(lambda e: e.activation(sb_red[:, 11:12], sb_red[:, 10:11], AF.Ln, bias=sb_cst[:, 1:2], scale=1.0),
            waits=[(sv, sv.n)])
        SOP(lambda e: e.activation(sb_tv[:, 7:8], sb_red[:, 11:12], AF.Exp, bias=0.0, scale=-0.5))
        ss_rstd2 = ss.n
        VOP(lambda e: e.tensor_scalar(sb_sq2[:, 0:D], sb_sq[:, 0:D], sb_tv[:, 6:7], sb_tv[:, 7:8],
                                      ALU.subtract, ALU.mult), waits=[(ss, ss_rstd2)])
        VOP(lambda e: e.tensor_reduce(sb_red[:, 8:9], sb_sq2[:, 0:D], AX.X, ALU.max,
                                      apply_absolute_value=True))
        VOP(lambda e: e.reciprocal(sb_red[:, 9:10], sb_red[:, 8:9]))
        VOP(lambda e: e.tensor_scalar_mul(sb_red[:, 10:11], sb_red[:, 9:10], 127.0))
        VOP(lambda e: e.tensor_scalar_mul(sb_tv[:, 4:5], sb_red[:, 8:9], 2.0 / 127.0))
        VOP(lambda e: e.tensor_mul(sb_red[:, 11:12], sb_tv[:, 7:8], sb_red[:, 10:11]))
        VOP(lambda e: e.tensor_mul(sb_tv[:, 2:3], sb_tv[:, 1:2], sb_red[:, 11:12]))
        VOP(lambda e: e.tensor_mul(sb_red[:, 9:10], sb_tv[:, 0:1], sb_tv[:, 1:2]))
        VOP(lambda e: e.tensor_add(sb_red[:, 9:10], sb_red[:, 9:10], sb_tv[:, 6:7]))
        VOP(lambda e: e.tensor_mul(sb_tv[:, 3:4], sb_red[:, 9:10], sb_red[:, 11:12]))
        op("tensor",
           lambda e: e.transpose(ps[6][0:8, 0:128], sb_tv[:, 0:8], sb_idp[:, :]),
           waits=[(sv, sv.n), (sgp, gp_idp), (ss, ss.n)], inc=stt)
        SOP(lambda e: e.copy(sb_tvT[0:8, :], ps[6][0:8, 0:128]), waits=[(stt, stt.n)])
        DMA("gpsimd", scr_tv[0:5, tt * 128:(tt + 1) * 128], sb_tvT[0:5, :],
            s_scr, waits=[(ss, ss.n)])
    v_stats = sv.n
    scr_stats = s_scr.n

    op("gpsimd",
       lambda e: e.collective_compute("AllGather", ALU.bypass,
                                      replica_groups=[list(range(C))],
                                      ins=[scr_tv[0:2, :].opt()], outs=[agb_out.opt()]),
       waits=[(s_scr, scr_stats)], inc=scc_b, k=1)
    cc_agb = scc_b.n

    for dst, r in [(sb_mb, 0), (sb_rb, 1), (sb_Ab, 2), (sb_Bb, 3), (sb_iv2, 4)]:
        DMA("gpsimd", dst[:, :], bass.AP(scr_tv_t, r * T, [[0, 128], [1, T]]),
            s_bco, waits=[(s_scr, scr_stats)])
    bco = s_bco.n

    if stop_after == "agb":
        _finish_early(prog, ctrs, out_tok, sb_sq, S, op, T, D, ndc)
        return _instantiate(nc, prog)

    for dc in range(ndc):
        DMA("sync", sb_sq[:, 0:160], xpwT[dc * 128:(dc + 1) * 128, :], s_x3,
            waits=[(sv, sv.n)])
        VOP(lambda e, dc=dc: e.tensor_copy(sb_xpw[:, dc * 160:(dc + 1) * 160], sb_sq[:, 0:160]),
            waits=[(s_x3, s_x3.n)])
    v_xpw = sv.n

    nxpj = T // NB
    xp_mm_dc = []
    for dc in range(ndc):
        DMA("sync", sb_xod[:, :], x_ownT[dc * 128:(dc + 1) * 128, :], s_x3,
            waits=[(sv, sv.n)] if dc > 0 else [(sv, v_xpw)])
        par = dc % 2
        VOP(lambda e: e.tensor_sub(sb_xnTf[:, :], sb_xod[:, :], sb_mb[:, :]),
            waits=[(s_x3, s_x3.n), (s_bco, bco)])
        VOP(lambda e, par=par: e.tensor_mul(sb_xnb[par][:, :], sb_xnTf[:, :], sb_rb[:, :]),
            waits=([(stt, xp_mm_dc[dc - 2])] if dc >= 2 else []))
        v_xnb = sv.n
        for j in range(nxpj):
            op("tensor",
               lambda e, dc=dc, j=j, par=par: e.matmul(
                   ps[j][:, 0:NB], sb_xpw[:, dc * 160:dc * 160 + 128],
                   sb_xnb[par][:, j * NB:(j + 1) * NB],
                   start=(dc == 0), stop=(dc == ndc - 1)),
               waits=[(sv, v_xnb), (sv, v_xpw)] if j == 0 else ())
        for j in range(nxpj):
            op("tensor",
               lambda e, dc=dc, j=j, par=par: e.matmul(
                   ps[2 + j][0:32, 0:NB], sb_xpw[:, dc * 160 + 128:dc * 160 + 160],
                   sb_xnb[par][:, j * NB:(j + 1) * NB],
                   start=(dc == 0), stop=(dc == ndc - 1)),
               inc=(stt if j == nxpj - 1 else None))
        xp_mm_dc.append(stt.n)
        VOP(lambda e: e.tensor_mul(sb_xnTf[:, :], sb_xod[:, :], sb_Ab[:, :]))
        VOP(lambda e: e.tensor_sub(sb_xnTf[:, :], sb_xnTf[:, :], sb_Bb[:, :]))
        VOP(lambda e: e.tensor_scalar(sb_f16[:, :], sb_xnTf[:, :], 1536.0, None, ALU.add))
        VOP(lambda e, dc=dc: e.tensor_scalar(sb_kT[:, dc * T:(dc + 1) * T], sb_f16[:, :],
                                             1536.0, None, ALU.subtract))
    v_kT = sv.n
    mm_xp = stt.n
    SOP(lambda e: e.activation(sb_ag1[:, 0:NB], ps[0][:, 0:NB],
                               AF.Identity, bias=0.0, scale=1.0), waits=[(stt, mm_xp), (sv, v_kT)])
    if nxpj > 1:
        SOP(lambda e: e.activation(sb_ag1[:, NB:2 * NB], ps[1][:, 0:NB],
                                   AF.Identity, bias=0.0, scale=1.0))
    SOP(lambda e: e.activation(sb_ag2[0:32, 0:NB], ps[2][0:32, 0:NB],
                               AF.Identity, bias=0.0, scale=1.0))
    if nxpj > 1:
        SOP(lambda e: e.activation(sb_ag2[0:32, NB:2 * NB], ps[3][0:32, 0:NB],
                                   AF.Identity, bias=0.0, scale=1.0))
    ss_aga = ss.n
    DMA("gpsimd", aga_in[0:128, :], sb_ag1[:, :], s_scr, waits=[(ss, ss_aga)])
    DMA("gpsimd", aga_in[128:160, :], sb_ag2[0:32, :], s_scr)
    op("gpsimd",
       lambda e: e.collective_compute("AllGather", ALU.bypass,
                                      replica_groups=[list(range(C))],
                                      ins=[aga_in.opt()], outs=[aga_out.opt()]),
       waits=[(s_scr, s_scr.n)], inc=scc_a, k=1)
    cc_aga = scc_a.n

    if stop_after == "kx":
        _finish_early(prog, ctrs, out_tok, sb_sq, S, op, T, D, ndc)
        return _instantiate(nc, prog)

    dt_relu_ms = [0, 0]
    for dt in range(nds):
        for blk in range(C):
            DMA("sync", sb_dlow[:, 0:T], aga_out[blk * 160:blk * 160 + 128, :], s_ag,
                waits=[(scc_a, cc_aga)] + ([(stt, stt.n)] if (dt or blk) else []))
            for j in range(T // NB):
                it = (dt * C + blk) * (T // NB) + j
                bank = 2 + (it % 2)
                op("tensor",
                   lambda e, dt=dt, j=j, bank=bank: e.matmul(ps[bank][:, 0:NB],
                                                             sb_dtw[0:R, dt * 128:(dt + 1) * 128],
                                                             sb_dlow[0:R, j * NB:(j + 1) * NB],
                                                             start=True, stop=True),
                   waits=[(s_ag, s_ag.n), (sv, v_dtw), (ss, dt_relu_ms[it % 2]), (sv, sv.n)],
                   inc=stt)
                mm_dt = stt.n
                SOP(lambda e, dt=dt, bank=bank: e.activation(sb_sp1[:, 0:NB], ps[bank][:, 0:NB], AF.Abs,
                                                             bias=sb_dtbc[:, dt:dt + 1], scale=1.0),
                    waits=[(stt, mm_dt)])
                SOP(lambda e: e.activation(sb_sp2[:, 0:NB], sb_sp1[:, 0:NB], AF.Exp,
                                           bias=0.0, scale=-1.0))
                SOP(lambda e: e.activation(sb_sp1[:, 0:NB], sb_sp2[:, 0:NB], AF.Ln,
                                           bias=sb_cst[:, 2:3], scale=1.0))
                SOP(lambda e, dt=dt, bank=bank: e.activation(sb_sp3[:, 0:NB], ps[bank][:, 0:NB], AF.Relu,
                                                             bias=sb_dtbc[:, dt:dt + 1], scale=1.0),
                    waits=[(s_dts, s_dts.n)])
                dt_relu_ms[it % 2] = ss.n
                VOP(lambda e: e.tensor_add(sb_sp3[:, 0:NB], sb_sp3[:, 0:NB], sb_sp1[:, 0:NB]),
                    waits=[(ss, ss.n), (s_dts, s_dts.n)])
                DMA("gpsimd", scr_dtT[dt * 128:(dt + 1) * 128, blk * T + j * NB:blk * T + (j + 1) * NB],
                    sb_sp3[:, 0:NB], s_dts, waits=[(sv, sv.n)])
    dts_all = s_dts.n

    scan_chunk_ready = {}
    bc_last_use = [0, 0, 0]
    bc_loaded = {}

    def emit_scan_bc(dt, ch, n):
        ring = n % 3
        blk0 = (ch * TC) // T
        w = [(scc_a, cc_aga)] if (dt, ch, n) == (0, 0, 0) else [(sv, bc_last_use[ring])]
        row_b = (blk0 * 160 + 128 + n) * T
        row_c = (blk0 * 160 + 144 + n) * T
        DMA("gpsimd", sb_bc[ring][:, 0:TC].rearrange("p (b t) -> p b t", b=nblk_c),
            bass.AP(aga_out_t, row_b, [[0, 128], [160 * T, nblk_c], [1, T]]),
            s_bcs[ring], waits=w)
        DMA("gpsimd", sb_bc[ring][:, TC:2 * TC].rearrange("p (b t) -> p b t", b=nblk_c),
            bass.AP(aga_out_t, row_c, [[0, 128], [160 * T, nblk_c], [1, T]]),
            s_bcs[ring])
        bc_loaded[(dt, ch, n)] = s_bcs[ring].n

    def emit_scan_chunk_prep(dt, ch):
        DMA("gpsimd", sb_dtT[:, :], scr_dtT[dt * 128:(dt + 1) * 128, ch * TC:(ch + 1) * TC],
            s_dtl, waits=[(s_dts, dts_all), (sv, sv.n), (ss, ss.n)])
        DMA("gpsimd", sb_xdc[:, :], x_dT[dt * 128:(dt + 1) * 128, ch * TC:(ch + 1) * TC], s_dtl)
        blk0 = (ch * TC) // T
        for b in range(nblk_c):
            DMA("gpsimd", sb_mab[:, b * T:(b + 1) * T],
                bass.AP(agb_out_t, (2 * (blk0 + b)) * T, [[0, 128], [1, T]]), s_mrb,
                waits=[(scc_b, cc_agb), (sv, sv.n)] if b == 0 else [])
            DMA("gpsimd", sb_rab[:, b * T:(b + 1) * T],
                bass.AP(agb_out_t, (2 * (blk0 + b) + 1) * T, [[0, 128], [1, T]]), s_mrb)
        VOP(lambda e: e.tensor_sub(sb_xdc[:, :], sb_xdc[:, :], sb_mab[:, :]),
            waits=[(s_dtl, s_dtl.n), (s_mrb, s_mrb.n)])
        VOP(lambda e: e.tensor_mul(sb_xdc[:, :], sb_xdc[:, :], sb_rab[:, :]))
        VOP(lambda e: e.tensor_mul(sb_dtx[:, :], sb_dtT[:, :], sb_xdc[:, :]))
        scan_chunk_ready[(dt, ch)] = sv.n
        emit_scan_bc(dt, ch, 0)
        emit_scan_bc(dt, ch, 1)
        emit_scan_exp(dt, ch, 0)
        emit_scan_exp(dt, ch, 1)

    scan_exp_ms = {}
    scan_done_ms = {}

    def emit_scan_exp(dt, ch, n):
        slot = n % 3
        w = [(sv, scan_chunk_ready[(dt, ch)])]
        prev = (dt, ch, n - 3)
        if prev in scan_done_ms:
            w.append((sv, scan_done_ms[prev]))
        SOP(lambda e, dt=dt, n=n, slot=slot: e.activation(
            sb_ab3[slot][:, :], sb_dtT[:, :], AF.Exp, bias=0.0,
            scale=sb_acol[:, dt * N + n:dt * N + n + 1]), waits=w, fifo=True)
        scan_exp_ms[(dt, ch, n)] = ss.n

    def emit_scan_group(dt, ch, n):
        ring = n % 3
        slot = n % 3
        ahead = n + 2
        if ahead < N:
            if (dt, ch, ahead) not in bc_loaded:
                emit_scan_bc(dt, ch, ahead)
            if (dt, ch, ahead) not in scan_exp_ms:
                emit_scan_exp(dt, ch, ahead)
        VOP(lambda e, ring=ring: e.tensor_mul(sb_bx[:, :], sb_dtx[:, :], sb_bc[ring][:, 0:TC]),
            waits=[(s_bcs[ring], bc_loaded[(dt, ch, n)])], fifo=True)
        VOP(lambda e, dt=dt, n=n, slot=slot: e.tensor_tensor_scan(
            sb_h[:, :], sb_ab3[slot][:, :], sb_bx[:, :],
            sb_hst[:, dt * N + n:dt * N + n + 1], ALU.mult, ALU.add),
            waits=[(ss, scan_exp_ms[(dt, ch, n)]), (sgp, gp_hst0)], fifo=True)
        scan_done_ms[(dt, ch, n)] = sv.n
        VOP(lambda e, dt=dt, n=n: e.tensor_copy(sb_hst[:, dt * N + n:dt * N + n + 1],
                                                sb_h[:, TC - 1:TC]), fifo=True)
        VOP(lambda e, ring=ring: e.tensor_mul(sb_ytmp[:, :], sb_h[:, :], sb_bc[ring][:, TC:2 * TC]),
            fifo=True)
        if n == 0:
            VOP(lambda e: e.tensor_copy(sb_yacc[:, :], sb_ytmp[:, :]),
                waits=[(s_y, s_y.n)], fifo=True)
        else:
            VOP(lambda e: e.tensor_add(sb_yacc[:, :], sb_yacc[:, :], sb_ytmp[:, :]), fifo=True)
        bc_last_use[ring] = sv.n

    def emit_scan_chunk_finish(dt, ch):
        blk0 = (ch * TC) // T
        for b in range(nblk_c):
            DMA("gpsimd", y_in[(blk0 + b) * ds + dt * 128:(blk0 + b) * ds + (dt + 1) * 128, :],
                sb_yacc[:, b * T:(b + 1) * T], s_y, waits=[(sv, sv.n)] if b == 0 else [])

    scan_steps = []
    for dt in range(nds):
        for ch in range(nch):
            scan_steps.append(("prep", dt, ch, 0))
            for n in range(N):
                scan_steps.append(("group", dt, ch, n))
            scan_steps.append(("finish", dt, ch, 0))
    scan_pos = [0]

    def emit_some_scan(k):
        if stop_after == "upnoscan":
            return
        done = 0
        while done < k and scan_pos[0] < len(scan_steps):
            kind, dt, ch, n = scan_steps[scan_pos[0]]
            if kind == "prep":
                emit_scan_chunk_prep(dt, ch)
            elif kind == "group":
                emit_scan_group(dt, ch, n)
            else:
                emit_scan_chunk_finish(dt, ch)
            scan_pos[0] += 1
            done += 1

    total_up_slots = nhr * nht_r * ntb
    scan_per_slot = max(1, -(-len(scan_steps) // total_up_slots))

    if stop_after == "p0":
        _finish_early(prog, ctrs, out_tok, sb_sq, S, op, T, D, ndc)
        return _instantiate(nc, prog)

    up_psum_free = {0: 0, 1: 0}
    up_hr_ms = []
    q_dma_ms = [0, 0]
    ev_user_ms = [0, 0]
    up_sgn_v = [0, 0]
    for hr in range(nhr):
        spar = hr % 2
        for gm, (wsrc_t, sgl) in enumerate([(wgT_t, sb_sgn_g), (wuT_t, sb_sgn_u)]):
            sgt = sgl[spar]
            w = [(sv, up_sgn_v[spar])]
            if hr >= 2 and gm == 0:
                w.append((stt, up_hr_ms[hr - 2]))
            src_ap = bass.AP(wsrc_t, hr * HRS, [[H, 128], [128 * H, ndc], [1, HRS]])
            dst_ap = sgt[:, :].rearrange("p (a f) -> p a f", a=ndc)
            DMA("gpsimd", dst_ap, src_ap, s_wf[gm], waits=w)
            VOP(lambda e, sgt=sgt: e.tensor_scalar(sgt[:, :], sgt[:, :], 0.0, 0.5,
                                                   ALU.is_ge, ALU.subtract),
                waits=[(s_wf[gm], s_wf[gm].n)])
        up_sgn_v[spar] = sv.n
        sgn_done_v = sv.n
        for ht in range(nht_r):
            for tb in range(ntb):
                hg = hr * nht_r + ht
                slot = (ht * ntb + tb) % 2
                gb, ub = 2 * slot, 2 * slot + 1
                tw = [(sv, sgn_done_v), (sv, v_kT), (ss, up_psum_free[slot])]
                for dc in range(ndc):
                    op("tensor",
                       lambda e, dc=dc, ht=ht, tb=tb, gb=gb, spar=spar: e.matmul(
                           ps[gb][:, 0:NB], sb_sgn_g[spar][:, dc * HRS + ht * 128:dc * HRS + ht * 128 + 128],
                           sb_kT[:, dc * T + tb * NB:dc * T + (tb + 1) * NB],
                           start=(dc == 0), stop=(dc == ndc - 1)),
                       waits=tw if dc == 0 else ())
                for dc in range(ndc):
                    last = dc == ndc - 1
                    op("tensor",
                       lambda e, dc=dc, ht=ht, tb=tb, ub=ub, last=last, spar=spar: e.matmul(
                           ps[ub][:, 0:NB], sb_sgn_u[spar][:, dc * HRS + ht * 128:dc * HRS + ht * 128 + 128],
                           sb_kT[:, dc * T + tb * NB:dc * T + (tb + 1) * NB],
                           start=(dc == 0), stop=last),
                       inc=(stt if last else None))
                mm_up = stt.n
                SOP(lambda e, gb=gb, slot=slot: e.activation(sb_evg[slot][:, :], ps[gb][:, 0:NB],
                                                             AF.Identity, bias=0.0, scale=1.0),
                    waits=[(stt, mm_up), (sv, ev_user_ms[slot])])
                SOP(lambda e, ub=ub, slot=slot: e.activation(sb_evu[slot][:, :], ps[ub][:, 0:NB],
                                                             AF.Identity, bias=0.0, scale=1.0))
                up_psum_free[slot] = ss.n
                ev_ss = ss.n
                VOP(lambda e, tb=tb, slot=slot: e.tensor_tensor(sb_ev1[:, :], sb_evg[slot][:, :],
                                                                sb_iv2[:, tb * NB:(tb + 1) * NB], ALU.mult),
                    waits=[(ss, ev_ss), (s_bco, bco)])
                SOP(lambda e: e.activation(sb_ev2[:, :], sb_ev1[:, :], AF.Tanh, bias=0.0, scale=0.5),
                    waits=[(sv, sv.n)])
                VOP(lambda e: e.tensor_scalar(sb_ev3[:, :], sb_ev2[:, :], 0.5, 0.5, ALU.mult, ALU.add),
                    waits=[(ss, ss.n)])
                VOP(lambda e: e.tensor_mul(sb_ev4[:, :], sb_ev1[:, :], sb_ev3[:, :]))
                qp = slot
                VOP(lambda e, qp=qp, slot=slot: e.tensor_mul(sb_q[qp][:, :], sb_ev4[:, :], sb_evu[slot][:, :]),
                    waits=[(s_p[qp], q_dma_ms[qp])])
                ev_user_ms[slot] = sv.n
                VOP(lambda e, tb=tb, qp=qp: e.tensor_mul(sb_q[qp][:, :], sb_q[qp][:, :],
                                                         sb_iv2[:, tb * NB:(tb + 1) * NB]))
                DMA("sync", p_scr[hg * 128:(hg + 1) * 128, tb * NB:(tb + 1) * NB], sb_q[qp][:, :],
                    s_p[qp], waits=[(sv, sv.n)])
                q_dma_ms[qp] = s_p[qp].n
                emit_some_scan(scan_per_slot)
        up_hr_ms.append(stt.n)
    emit_some_scan(len(scan_steps))
    p_all = (s_p[0].n, s_p[1].n)
    y_all = s_y.n

    if stop_after == "up":
        _finish_early(prog, ctrs, out_tok, sb_sq, S, op, T, D, ndc)
        return _instantiate(nc, prog)
    if stop_after == "upnoscan":
        _finish_early(prog, ctrs, out_tok, sb_sq, S, op, T, D, ndc)
        return _instantiate(nc, prog)

    def cc_y_fn(e):
        with allow_ici_all_to_all():
            return e.collective_compute("AllToAll", ALU.bypass,
                                        replica_groups=[list(range(C))],
                                        ins=[y_in.opt()], outs=[y_out.opt()])
    op("gpsimd", cc_y_fn, waits=[(s_y, y_all)], inc=scc_y, k=1)
    cc_y = scc_y.n

    dn_ms = []
    dn_sgn_ms = [0, 0]
    for oq in range(noq):
        for h4 in range(nh // HH4):
            par4 = h4 % 2
            g4 = oq * (nh // HH4) + h4
            w4 = [(stt, dn_ms[2 * g4 - 3])] if g4 >= 2 else []
            DMA("gpsimd",
                sb_wds[par4][:, :].rearrange("p (a f) -> p a f", a=HH4),
                bass.AP(wdT_t, (h4 * HH4 * 128) * D + oq * OQ, [[D, 128], [128 * D, HH4], [1, OQ]]),
                s_wd[par4], waits=[(sv, dn_sgn_ms[par4])] + w4)
            VOP(lambda e, par4=par4: e.tensor_scalar(sb_wds[par4][:, :], sb_wds[par4][:, :],
                                                     0.0, 0.5, ALU.is_ge, ALU.subtract),
                waits=[(s_wd[par4], s_wd[par4].n)])
            dn_sgn_ms[par4] = sv.n
            sgn_d = sv.n
            for hh2 in range(HH4 // 2):
                hh = h4 * (HH4 // 2) + hh2
                par = hh % 2
                gi = oq * (nh // 2) + hh
                w_slot = [(stt, dn_ms[gi - 2])] if gi >= 2 else []
                DMA("sync",
                    sb_pt[par][:, :].rearrange("p (a f) -> p a f", a=2),
                    p_scr[hh * 256:(hh + 1) * 256, :].rearrange("(a p) f -> p a f", p=128),
                    s_pl[par],
                    waits=([(s_p[0], p_all[0]), (s_p[1], p_all[1])] if gi == 0 else []) + w_slot)
                for hi in range(2):
                    h = hh * 2 + hi
                    hq = hh2 * 2 + hi
                    for ot in range(no_t):
                        for tb in range(ntb):
                            bank = ot * ntb + tb
                            w = []
                            if hi == 0 and ot == 0 and tb == 0:
                                w = [(sv, sgn_d), (s_pl[par], s_pl[par].n)]
                                if h == 0 and oq > 0:
                                    w.append((ss, ss.n))
                            op("tensor",
                               lambda e, par=par, par4=par4, hq=hq, hi=hi, ot=ot, tb=tb, h=h, bank=bank: e.matmul(
                                   ps[bank][:, 0:NB],
                                   sb_wds[par4][:, hq * OQ + ot * 128:hq * OQ + (ot + 1) * 128],
                                   sb_pt[par][:, hi * T + tb * NB:hi * T + (tb + 1) * NB],
                                   start=(h == 0), stop=(h == nh - 1)),
                               waits=w,
                               inc=(stt if (hi == 1 and ot == no_t - 1 and tb == ntb - 1) else None))
                dn_ms.append(stt.n)
        mm_dn = stt.n
        for ot in range(no_t):
            for tb in range(ntb):
                bank = ot * ntb + tb
                oc = oq * no_t + ot
                fpar = bank % 2
                SOP(lambda e, bank=bank, fpar=fpar: e.activation(
                    sb_fs[fpar][:, :], ps[bank][:, 0:NB],
                    AF.Identity, bias=0.0, scale=2.0),
                    waits=[(stt, mm_dn), (s_fsp[fpar], fs_slot_ms[fpar])])
                DMA("gpsimd", ffn_scr[oc * 128:(oc + 1) * 128, tb * NB:(tb + 1) * NB],
                    sb_fs[fpar][:, :], s_fsp[fpar], waits=[(ss, ss.n)])
                fs_slot_ms[fpar] = s_fsp[fpar].n
    ss_dn = ss.n

    if stop_after == "down":
        _finish_early(prog, ctrs, out_tok, sb_sq, S, op, T, D, ndc)
        return _instantiate(nc, prog)

    GOP(lambda e: e.memset(sb_ident[:, :], 1.0), waits=[(sv, sv.n), (ss, ss.n)])
    GOP(lambda e: e.affine_select(sb_ident[:, :], sb_ident[:, :], [[1, 128]],
                                  ALU.is_equal, 0.0, base=0, channel_multiplier=-1),
        waits=[(sgp, sgp.n)])
    gp_ident = sgp.n
    out_slot_ms = [0, 0]
    s_p3 = [S("s_p30"), S("s_p31")]
    p3_user_ms = [0, 0]
    acc_tr_ms = [0, 0]
    for dc in range(ndc):
        par = dc % 2
        w0 = [(sv, p3_user_ms[par])]
        if dc == 0:
            w0 += [(sv, sv.n), (ss, ss.n)]
        DMA("sync", sb_xod[:, :], x_ownT[dc * 128:(dc + 1) * 128, :], s_x3,
            waits=([(sv, sv.n), (ss, ss.n)] if dc == 0 else [(sv, sv.n)]))
        DMA("sync", sb_ycol[par][:, :], y_out[dc * 128:(dc + 1) * 128, :], s_p3[par],
            waits=([(scc_y, cc_y), (ss, ss.n)] if dc == 0 else []) + [(sv, p3_user_ms[par])])
        DMA("sync", sb_fload[par][:, :], ffn_scr[dc * 128:(dc + 1) * 128, :], s_p3[par],
            waits=([(s_fsp[0], s_fsp[0].n), (s_fsp[1], s_fsp[1].n)] if dc == 0 else []))
        x3n = s_x3.n
        p3n = s_p3[par].n
        VOP(lambda e: e.tensor_sub(sb_xnTf[:, :], sb_xod[:, :], sb_mb[:, :]),
            waits=[(s_x3, x3n), (stt, stt.n)] if dc == 0 else [(s_x3, x3n)])
        VOP(lambda e: e.tensor_mul(sb_xnTf[:, :], sb_xnTf[:, :], sb_rb[:, :]))
        VOP(lambda e, dc=dc, par=par: e.scalar_tensor_tensor(sb_acc[par][:, :], sb_xnTf[:, :],
                                                             sb_dpb[:, dc:dc + 1], sb_xod[:, :],
                                                             ALU.mult, ALU.add),
            waits=[(stt, acc_tr_ms[par])])
        VOP(lambda e, par=par: e.tensor_add(sb_acc[par][:, :], sb_acc[par][:, :], sb_ycol[par][:, :]),
            waits=[(s_p3[par], p3n)])
        VOP(lambda e, par=par: e.tensor_add(sb_acc[par][:, :], sb_acc[par][:, :], sb_fload[par][:, :]),
            waits=([(ss, ss_dn)] if dc == 0 else []))
        acc_done = sv.n
        p3_user_ms[par] = sv.n
        for tt in range(ntt):
            pslot = tt % 2
            op("tensor",
               lambda e, tt=tt, pslot=pslot, par=par: e.transpose(ps[pslot][:, 0:128],
                                                                  sb_acc[par][:, tt * 128:(tt + 1) * 128],
                                                                  sb_ident[:, :]),
               waits=[(sv, acc_done), (sgp, gp_ident)] +
                     ([(ss, ss.n)] if (dc * ntt + tt) >= 2 else []),
               inc=stt)
            tr_done = stt.n
            SOP(lambda e, pslot=pslot: e.copy(sb_ot[pslot][:, :], ps[pslot][:, 0:128]),
                waits=[(stt, tr_done), (s_out, out_slot_ms[pslot])])
            DMA("sync", out_tok[tt * 128:(tt + 1) * 128, dc * 128:(dc + 1) * 128],
                sb_ot[pslot][:, :], s_out, waits=[(ss, ss.n)])
            out_slot_ms[pslot] = s_out.n
        acc_tr_ms[par] = stt.n
    prog["sync"].append(lambda eng, h=s_out.h, v=s_out.n: eng.wait_ge(h, v))
    prog["gpsimd"].append(lambda eng, h=s_out.h, v=s_out.n: eng.wait_ge(h, v))

    return _instantiate(nc, prog)


def _finish_early(prog, ctrs, out_tok, sb_src, S, op, T, D, ndc):
    s_out = ctrs["s_out"]
    sv = ctrs["sv"]
    ss = ctrs["ss"]
    W = min(D, sb_src.shape[1])
    for tt in range(T // 128):
        for c0 in range(0, D, W):
            op("sync", lambda eng, tt=tt, c0=c0: eng.dma_start(
                out=out_tok[tt * 128:(tt + 1) * 128, c0:c0 + W],
                in_=sb_src[:, 0:W]), waits=[(sv, sv.n), (ss, ss.n)] if (tt == 0 and c0 == 0) else [],
                inc=s_out, k=16)
    prog["sync"].append(lambda eng, h=s_out.h, v=s_out.n: eng.wait_ge(h, v))
    prog["gpsimd"].append(lambda eng, h=s_out.h, v=s_out.n: eng.wait_ge(h, v))


def _instantiate(nc, prog):
    with nc.Block() as block:
        @block.sync
        def _(eng):
            for fn in prog["sync"]:
                fn(eng)

        @block.gpsimd
        def _(eng):
            for fn in prog["gpsimd"]:
                fn(eng)

        @block.tensor
        def _(eng):
            for fn in prog["tensor"]:
                fn(eng)

        @block.vector
        def _(eng):
            for fn in prog["vector"]:
                fn(eng)

        @block.scalar
        def _(eng):
            for fn in prog["scalar"]:
                fn(eng)

    return nc


def make_in_maps(cfg, inputs):
    D, H, L, N, R, C = cfg["D"], cfg["H"], cfg["L"], cfg["N"], cfg["R"], cfg["C"]
    T = L // C
    ds = D // C
    x = np.ascontiguousarray(np.asarray(inputs["L0_attn_x"], dtype=np.float32)[0])
    xpw = np.ascontiguousarray(np.asarray(inputs["L0_attn_x_proj_w"], np.float32).T)
    dtpw = np.ascontiguousarray(np.asarray(inputs["L0_attn_dt_proj_w"], np.float32).T)
    dtbv = np.asarray(inputs["L0_attn_dt_proj_b"], np.float32)
    al = np.asarray(inputs["L0_attn_A_log"], np.float32)
    dpv = np.ascontiguousarray(np.asarray(inputs["L0_attn_D_param"], np.float32).reshape(D // 128, 128).T)
    wg = np.ascontiguousarray(np.asarray(inputs["L0_ffn_W_g"], np.float32).T)
    wu = np.ascontiguousarray(np.asarray(inputs["L0_ffn_W_u"], np.float32).T)
    wd = np.ascontiguousarray(np.asarray(inputs["L0_ffn_W_d"], np.float32).T)
    xT = np.ascontiguousarray(x.T)
    in_maps = []
    for c in range(C):
        in_maps.append({
            "x_tok": np.ascontiguousarray(x[c * T:(c + 1) * T, :]),
            "x_ownT": np.ascontiguousarray(xT[:, c * T:(c + 1) * T]),
            "x_dT": np.ascontiguousarray(xT[c * ds:(c + 1) * ds, :]),
            "xpwT": xpw,
            "dtpwT": np.ascontiguousarray(dtpw[:, c * ds:(c + 1) * ds]),
            "dtb": np.ascontiguousarray(dtbv[c * ds:(c + 1) * ds].reshape(ds // 128, 128).T),
            "alog": np.ascontiguousarray(
                al[c * ds:(c + 1) * ds, :].reshape(ds // 128, 128, N).transpose(1, 0, 2).reshape(128, -1)),
            "dp": dpv,
            "wgT": wg,
            "wuT": wu,
            "wdT": wd,
        })
    return in_maps


def kernel(**inputs):
    cfg = FULL_CFG
    nc = build_nc(cfg)
    in_maps = make_in_maps(cfg, inputs)
    res = run_bass_kernel_spmd(nc, in_maps, core_ids=list(range(cfg["C"])))
    out = np.concatenate([np.asarray(r["out_tok"]) for r in res.results], axis=0)
    return out[None].astype(np.float32)


# version 22
# speedup vs baseline: 1.6607x; 1.1540x over previous
import numpy as np
import concourse.bass as bass
import concourse.mybir as mybir
from concourse.bass_utils import run_bass_kernel_spmd
from concourse.aot_env import allow_ici_all_to_all

F32 = mybir.dt.float32
BF16 = mybir.dt.bfloat16
F16 = mybir.dt.float16
AF = mybir.ActivationFunctionType
ALU = mybir.AluOpType
AX = mybir.AxisListType

FULL_CFG = dict(D=2048, H=8192, L=8192, N=16, R=128, C=8)


class Ctr:
    def __init__(self, handle):
        self.h = handle
        self.n = 0


def build_nc(cfg):
    stop_after = cfg.get("stop_after", "full")
    D, H, L, N, R, C = cfg["D"], cfg["H"], cfg["L"], cfg["N"], cfg["R"], cfg["C"]
    T = L // C
    ds = D // C
    nds = ds // 128
    ndc = D // 128
    NB = min(512, T)
    ntb = T // NB
    HRS = min(256, H)
    nhr = H // HRS
    nht_r = HRS // 128
    nh = H // 128
    OQ = min(512, D)
    noq = D // OQ
    no_t = OQ // 128
    TC = min(2048, L)
    nch = L // TC
    nblk_c = max(1, TC // T)
    ntt = T // 128
    assert TC % T == 0 and R == 128 and ds % 128 == 0

    nc = bass.Bass("TRN2", target_bir_lowering=False, detect_race_conditions=False)

    x_tok = nc.dram_tensor("x_tok", [T, D], F32, kind="ExternalInput").ap()
    x_ownT = nc.dram_tensor("x_ownT", [D, T], F32, kind="ExternalInput").ap()
    x_dT = nc.dram_tensor("x_dT", [ds, L], F32, kind="ExternalInput").ap()
    xpwT = nc.dram_tensor("xpwT", [D, 160], F32, kind="ExternalInput").ap()
    dtpwT = nc.dram_tensor("dtpwT", [R, ds], F32, kind="ExternalInput").ap()
    dtb = nc.dram_tensor("dtb", [128, nds], F32, kind="ExternalInput").ap()
    alog = nc.dram_tensor("alog", [128, nds * N], F32, kind="ExternalInput").ap()
    dp = nc.dram_tensor("dp", [128, D // 128], F32, kind="ExternalInput").ap()
    wgT_t = nc.dram_tensor("wgT", [D, H], F32, kind="ExternalInput")
    wuT_t = nc.dram_tensor("wuT", [D, H], F32, kind="ExternalInput")
    wdT_t = nc.dram_tensor("wdT", [H, D], F32, kind="ExternalInput")
    wgT = wgT_t.ap()
    wuT = wuT_t.ap()
    wdT = wdT_t.ap()
    out_tok = nc.dram_tensor("out_tok", [T, D], F32, kind="ExternalOutput").ap()

    scr_tv_t = nc.dram_tensor("scr_tv", [8, T], F32, kind="Internal")
    aga_in_t = nc.dram_tensor("aga_in", [160, T], BF16, kind="Internal")
    aga_out_t = nc.dram_tensor("aga_out", [160 * C, T], BF16, kind="Internal")
    agb_out_t = nc.dram_tensor("agb_out", [2 * C, T], F32, kind="Internal")
    scr_dtT_t = nc.dram_tensor("scr_dtT", [ds, L], F32, kind="Internal")
    p_scr_t = nc.dram_tensor("p_scr", [H, T], BF16, kind="Internal")
    ffn_scr_t = nc.dram_tensor("ffn_scr", [D, T], BF16, kind="Internal")
    y_in_t = nc.dram_tensor("y_a2a_in", [C * ds, T], BF16, kind="Internal")
    y_out_t = nc.dram_tensor("y_a2a_out", [C * ds, T], BF16, kind="Internal")
    scr_tv = scr_tv_t.ap()
    aga_in = aga_in_t.ap()
    aga_out = aga_out_t.ap()
    agb_out = agb_out_t.ap()
    scr_dtT = scr_dtT_t.ap()
    p_scr = p_scr_t.ap()
    ffn_scr = ffn_scr_t.ap()
    y_in = y_in_t.ap()
    y_out = y_out_t.ap()

    A = nc.alloc_sbuf_tensor
    AAT = nc.alloc_sbuf_tensor_at

    def addr_of(h):
        return nc.lookup_mloc(h).addr

    sb_xtok = A("sb_xtok", [128, D], F32)
    sb_sq = A("sb_sq", [128, D], F32)
    sb_sq2 = A("sb_sq2", [128, D], F32)
    sb_p0a = A("sb_p0a", [128, 3072], F32)
    sb_tv = A("sb_tv", [128, 8], F32)
    sb_red = A("sb_red", [128, 12], F32)
    sb_xod = A("sb_xod", [128, T], F32)
    sb_xnTf = A("sb_xnTf", [128, T], F32)
    sb_xnb = [A(f"sb_xnb{i}", [128, T], BF16) for i in range(2)]
    sb_kT = A("sb_kT", [128, ndc * T], BF16)
    sb_mb = A("sb_mb", [128, T], F32)
    sb_rb = A("sb_rb", [128, T], F32)
    sb_Ab = A("sb_Ab", [128, T], F32)
    sb_Bb = A("sb_Bb", [128, T], F32)
    sb_iv2 = A("sb_iv2", [128, T], BF16)
    sb_xpw = A("sb_xpw", [128, ndc * 160], BF16)
    sb_dtw = A("sb_dtw", [128, ds], BF16)
    sb_ag2 = A("sb_ag2", [32, T], BF16)
    sb_dtbc = A("sb_dtbc", [128, nds], F32)
    sb_acol = A("sb_acol", [128, nds * N], F32)
    sb_sgn_g = [A(f"sb_sgn_g{i}", [128, ndc * HRS], BF16) for i in range(2)]
    sb_sgn_u = [A(f"sb_sgn_u{i}", [128, ndc * HRS], BF16) for i in range(2)]
    sb_q = [A(f"sb_q{i}", [128, NB], BF16) for i in range(2)]
    sb_evg = None
    sb_evu = None
    sb_ev1 = A("sb_ev1", [128, NB], BF16)
    sb_ev2 = A("sb_ev2", [128, NB], BF16)
    sb_ev3 = A("sb_ev3", [128, NB], BF16)
    sb_ev4 = A("sb_ev4", [128, NB], BF16)
    sb_dtT = A("sb_dtT", [128, TC], F32)
    sb_xdc = A("sb_xdc", [128, TC], F32)
    sb_h = A("sb_h", [128, TC], BF16)
    sb_bc = [A(f"sb_bc{i}", [128, 2 * TC], BF16) for i in range(3)]
    sb_hst = A("sb_hst", [128, max(1, nds * N)], F32)
    sb_ytmp = A("sb_ytmp", [128, TC], BF16)
    sb_yacc = A("sb_yacc", [128, TC], BF16)

    sb_dpb = A("sb_dpb", [128, ndc], F32)
    sb_cst = A("sb_cst", [128, 4], F32)
    sb_idp = A("sb_idp", [128, 128], F32)
    sb_tvT = A("sb_tvT", [8, 128], F32)
    a_xtok = addr_of(sb_xtok)
    sb_pt = [AAT(f"sb_pt{i}", [128, 2 * T], BF16, offset=a_xtok + i * (4 * T)) for i in range(2)]
    a_sq = addr_of(sb_sq)
    sb_wds = None
    a_sq2 = addr_of(sb_sq2)
    sb_mab = AAT("sb_mab", [128, TC], BF16, offset=a_sq2)
    sb_rab = AAT("sb_rab", [128, TC], BF16, offset=a_sq2 + 2 * TC)
    a_p0a = addr_of(sb_p0a)
    sb_sp1 = AAT("sb_sp1", [128, 2 * NB], F32, offset=a_p0a)
    sb_sp2 = AAT("sb_sp2", [128, 2 * NB], F32, offset=a_p0a + 8 * NB)
    sb_sp3 = AAT("sb_sp3", [128, 2 * NB], F32, offset=a_p0a + 16 * NB)
    sb_f16 = AAT("sb_f16", [128, T], F16, offset=addr_of(sb_dtT))
    sb_dlow = AAT("sb_dlow", [128, T], BF16, offset=addr_of(sb_dtT) + 2 * T)
    sb_ag1 = AAT("sb_ag1", [128, T], BF16, offset=addr_of(sb_xdc))
    sb_dtx = AAT("sb_dtx", [128, TC], BF16, offset=a_p0a)
    sb_ab3 = None
    sb_bx = AAT("sb_bx", [128, TC], BF16, offset=a_p0a + 4 * TC)
    a_dtT = addr_of(sb_dtT)
    sb_acc = [AAT(f"sb_acc{i}", [128, T], F32, offset=a_dtT + i * 4 * T) for i in range(2)]
    a_bc0 = addr_of(sb_bc[0])
    sb_ycol = [AAT(f"sb_ycol{i}", [128, T], BF16, offset=a_bc0 + i * 2 * T) for i in range(2)]
    a_yt = addr_of(sb_ytmp)
    sb_fload = [AAT(f"sb_fload{i}", [128, T], BF16, offset=a_yt + i * 2 * T) for i in range(2)]
    a_xdc = addr_of(sb_xdc)
    sb_ot = [AAT(f"sb_ot{i}", [128, 128], F32, offset=a_xdc + i * 512) for i in range(2)]
    if 4 * T <= 2 * TC:
        sb_xod2 = [sb_xod, AAT("sb_xod1", [128, T], F32, offset=addr_of(sb_h))]
        sb_xnTf2 = [sb_xnTf, AAT("sb_xnTf1", [128, T], F32, offset=addr_of(sb_dtx))]
    else:
        sb_xod2 = [sb_xod, A("sb_xod1", [128, T], F32)]
        sb_xnTf2 = [sb_xnTf, A("sb_xnTf1", [128, T], F32)]
    sb_ident = AAT("sb_ident", [128, 128], F32, offset=a_xdc + 1024)
    def _fit(name, shape, dtype, host, host_bytes):
        nbytes = shape[1] * (2 if dtype in (BF16, F16) else 4)
        if nbytes <= host_bytes:
            return AAT(name, shape, dtype, offset=addr_of(host))
        return A(name, shape, dtype)
    sb_ab3 = [AAT("sb_ab0", [128, TC], BF16, offset=a_p0a + 2 * TC),
              _fit("sb_ab1", [128, TC], BF16, sb_xod, 4 * T),
              _fit("sb_ab2", [128, TC], BF16, sb_xnTf, 4 * T)]
    HH4 = min(4, nh)
    sb_wds = [AAT(f"sb_wds{i}", [128, HH4 * OQ], BF16, offset=a_sq + i * (2 * HH4 * OQ)) for i in range(2)]
    sb_fs = [AAT(f"sb_fs{i}", [128, NB], BF16, offset=addr_of(sb_q[i])) for i in range(2)]
    sb_xtok2 = [sb_xtok, AAT("sb_xtok1", [128, D], F32, offset=addr_of(sb_kT))]
    a_xpw = addr_of(sb_xpw)
    if ndc * 160 * 2 >= 8 * 2 * NB:
        sb_evg = [AAT(f"sb_evg{i}", [128, NB], BF16, offset=a_xpw + i * 2 * NB) for i in range(4)]
        sb_evu = [AAT(f"sb_evu{i}", [128, NB], BF16, offset=a_xpw + (4 + i) * 2 * NB) for i in range(4)]
    else:
        sb_evg = [A(f"sb_evg{i}", [128, NB], BF16) for i in range(4)]
        sb_evu = [A(f"sb_evu{i}", [128, NB], BF16) for i in range(4)]
    ps2b = [nc.alloc_psum_tensor(f"ps2b{i}", [128, 1024], F32) for i in range(4)]
    ps = [ps2b[i // 2][:, (i % 2) * 512:(i % 2) * 512 + 512] for i in range(8)]

    prog = {e: [] for e in ["sync", "gpsimd", "tensor", "vector", "scalar"]}
    ctrs = {}
    sem_keep = []

    def S(name):
        c = ctrs.get(name)
        if c is None:
            g = nc.semaphore(name)
            h = g.__enter__()
            sem_keep.append(g)
            c = Ctr(h)
            ctrs[name] = c
        return c

    self_ctr = {}

    def op(e, fn, waits=(), inc=None, k=1):
        own = self_ctr.get(e)
        wl = [(c.h, v) for (c, v) in waits if v > 0 and c is not own]
        dd = {}
        for h, v in wl:
            dd[h] = max(dd.get(h, 0), v)
        wl = list(dd.items())
        if inc is not None:
            inc.n += k

        def run(eng, fn=fn, wl=wl, inc=inc, k=k):
            for h, v in wl:
                eng.wait_ge(h, v)
            ins = fn(eng)
            if inc is not None:
                ins.then_inc(inc.h, k)
        prog[e].append(run)

    sv = S("sv")
    ss = S("ss")
    stt = S("stt")
    sgp = S("sgp")
    scc_b = S("scc_b")
    scc_a = S("scc_a")
    scc_y = S("scc_y")
    s_in = S("s_in")
    s_x3 = S("s_x3")
    s_tv = S("s_tv")
    s_scr = S("s_scr")
    s_bco = S("s_bco")
    s_ag = S("s_ag")
    s_dts = S("s_dts")
    s_dtl = S("s_dtl")
    s_bcs = [S("s_bc0"), S("s_bc1"), S("s_bc2")]
    s_mrb = S("s_mrb")
    s_wf = [S("s_wf0"), S("s_wf1")]
    s_p = [S("s_p0"), S("s_p1")]
    s_pl = [S("s_pl0"), S("s_pl1")]
    s_wd = [S("s_wd0"), S("s_wd1")]
    s_y = S("s_y")
    s_fsp = [S("s_fsp0"), S("s_fsp1")]
    fs_slot_ms = [0, 0]
    s_p3 = [S("s_p30"), S("s_p31")]
    p3_user_ms = [0, 0]
    s_out = S("s_out")

    self_ctr.update({"tensor": stt})

    def VOP(fn, waits=(), fifo=False):
        op("vector", fn, waits=list(waits) + ([] if fifo else [(sv, sv.n)]), inc=sv)

    def SOP(fn, waits=(), fifo=False):
        op("scalar", fn, waits=list(waits) + ([] if fifo else [(ss, ss.n)]), inc=ss)

    def GOP(fn, waits=()):
        op("gpsimd", fn, waits=list(waits) + [(sgp, sgp.n)], inc=sgp)

    def DMA(e, out_ap, in_ap, ctr, waits=()):
        op(e, lambda eng, o=out_ap, i=in_ap: eng.dma_start(out=o, in_=i),
           waits=waits, inc=ctr, k=16)

    DMA("sync", sb_dtbc[:, :], dtb, s_in)
    DMA("sync", sb_acol[:, :], alog, s_in)
    DMA("sync", sb_dpb[:, :], dp, s_in)
    DMA("sync", sb_sq[0:R, 0:ds], dtpwT, s_in)
    in0 = s_in.n
    VOP(lambda e: e.tensor_copy(sb_dtw[0:R, :], sb_sq[0:R, 0:ds]), waits=[(s_in, in0)])
    v_dtw = sv.n
    SOP(lambda e: e.activation(sb_acol[:, :], sb_acol[:, :], AF.Exp, bias=0.0, scale=1.0),
        waits=[(s_in, in0)])
    VOP(lambda e: e.tensor_scalar_mul(sb_acol[:, :], sb_acol[:, :], -1.0), waits=[(ss, ss.n)])
    v_acol = sv.n
    GOP(lambda e: e.memset(sb_cst[:, 0:1], 1e-5))
    GOP(lambda e: e.memset(sb_cst[:, 1:2], 1e-8))
    GOP(lambda e: e.memset(sb_cst[:, 2:3], 1.0))
    gp_cst = sgp.n
    GOP(lambda e: e.memset(sb_hst[:, :], 0.0))
    gp_hst0 = sgp.n
    GOP(lambda e: e.memset(sb_tv[:, :], 0.0))
    GOP(lambda e: e.memset(sb_idp[:, :], 1.0))
    GOP(lambda e: e.affine_select(sb_idp[:, :], sb_idp[:, :], [[1, 128]],
                                  ALU.is_equal, 0.0, base=0, channel_multiplier=-1),
        waits=[(sgp, sgp.n)])
    gp_idp = sgp.n

    if stop_after == "params":
        _finish_early(prog, ctrs, out_tok, sb_sq, S, op, T, D, ndc)
        return _instantiate(nc, prog)

    up_sgn_v = [0, 0]
    s_wf_pre = []
    for hr0 in range(2):
        for gm, (wsrc_t, sgl) in enumerate([(wgT_t, sb_sgn_g), (wuT_t, sb_sgn_u)]):
            sgt = sgl[hr0]
            src_ap = bass.AP(wsrc_t, hr0 * HRS, [[H, 128], [128 * H, ndc], [1, HRS]])
            dst_ap = sgt[:, :].rearrange("p (a f) -> p a f", a=ndc)
            DMA("gpsimd", dst_ap, src_ap, s_wf[gm])
            VOP(lambda e, sgt=sgt: e.tensor_scalar(sgt[:, :], sgt[:, :], 0.0, 0.5,
                                                   ALU.is_ge, ALU.subtract),
                waits=[(s_wf[gm], s_wf[gm].n)])
        up_sgn_v[hr0] = sv.n

    xtok_user = [0, 0]
    for tt in range(ntt):
        xp_ = tt % 2
        DMA("sync", sb_xtok2[xp_][:, :], x_tok[tt * 128:(tt + 1) * 128, :], s_tv,
            waits=[(sv, xtok_user[xp_]), (ss, ss.n)] if tt > 1 else [])
        SOP(lambda e, xp_=xp_: e.activation(sb_sq[:, 0:D], sb_xtok2[xp_][:, :], AF.Identity, bias=0.0,
                                            scale=1.0, accum_out=sb_red[:, 0:1]),
            waits=[(s_tv, s_tv.n), (s_scr, s_scr.n)])
        SOP(lambda e, xp_=xp_: e.activation(sb_sq[:, 0:D], sb_xtok2[xp_][:, :], AF.Square, bias=0.0,
                                            scale=1.0, accum_out=sb_red[:, 1:2]))
        ss_sums = ss.n
        VOP(lambda e: e.tensor_scalar_mul(sb_tv[:, 0:1], sb_red[:, 0:1], 1.0 / D),
            waits=[(ss, ss_sums)])
        VOP(lambda e: e.tensor_mul(sb_red[:, 2:3], sb_tv[:, 0:1], sb_tv[:, 0:1]))
        VOP(lambda e: e.tensor_scalar_mul(sb_red[:, 3:4], sb_red[:, 1:2], 1.0 / D))
        VOP(lambda e: e.tensor_sub(sb_red[:, 4:5], sb_red[:, 3:4], sb_red[:, 2:3]))
        SOP(lambda e: e.activation(sb_red[:, 5:6], sb_red[:, 4:5], AF.Ln, bias=sb_cst[:, 0:1], scale=1.0),
            waits=[(sv, sv.n), (sgp, gp_cst)])
        SOP(lambda e: e.activation(sb_tv[:, 1:2], sb_red[:, 5:6], AF.Exp, bias=0.0, scale=-0.5))
        ss_rstd = ss.n
        VOP(lambda e, xp_=xp_: e.tensor_scalar(sb_sq[:, 0:D], sb_xtok2[xp_][:, :], sb_tv[:, 0:1], sb_tv[:, 1:2],
                                               ALU.subtract, ALU.mult), waits=[(ss, ss_rstd)])
        v_xn = sv.n
        xtok_user[xp_] = sv.n
        SOP(lambda e: e.activation(sb_sq2[:, 0:D], sb_sq[:, 0:D], AF.Identity, bias=0.0,
                                   scale=1.0, accum_out=sb_red[:, 6:7]), waits=[(sv, v_xn)])
        SOP(lambda e: e.activation(sb_sq2[:, 0:D], sb_sq[:, 0:D], AF.Square, bias=0.0,
                                   scale=1.0, accum_out=sb_red[:, 7:8]))
        ss_sums2 = ss.n
        VOP(lambda e: e.tensor_scalar_mul(sb_tv[:, 6:7], sb_red[:, 6:7], 1.0 / D),
            waits=[(ss, ss_sums2)])
        VOP(lambda e: e.tensor_mul(sb_red[:, 8:9], sb_tv[:, 6:7], sb_tv[:, 6:7]))
        VOP(lambda e: e.tensor_scalar_mul(sb_red[:, 9:10], sb_red[:, 7:8], 1.0 / D))
        VOP(lambda e: e.tensor_sub(sb_red[:, 10:11], sb_red[:, 9:10], sb_red[:, 8:9]))
        SOP(lambda e: e.activation(sb_red[:, 11:12], sb_red[:, 10:11], AF.Ln, bias=sb_cst[:, 1:2], scale=1.0),
            waits=[(sv, sv.n)])
        SOP(lambda e: e.activation(sb_tv[:, 7:8], sb_red[:, 11:12], AF.Exp, bias=0.0, scale=-0.5))
        ss_rstd2 = ss.n
        VOP(lambda e: e.tensor_scalar(sb_sq2[:, 0:D], sb_sq[:, 0:D], sb_tv[:, 6:7], sb_tv[:, 7:8],
                                      ALU.subtract, ALU.mult), waits=[(ss, ss_rstd2)])
        VOP(lambda e: e.tensor_reduce(sb_red[:, 8:9], sb_sq2[:, 0:D], AX.X, ALU.max,
                                      apply_absolute_value=True))
        VOP(lambda e: e.reciprocal(sb_red[:, 9:10], sb_red[:, 8:9]))
        VOP(lambda e: e.tensor_scalar_mul(sb_red[:, 10:11], sb_red[:, 9:10], 127.0))
        VOP(lambda e: e.tensor_scalar_mul(sb_tv[:, 4:5], sb_red[:, 8:9], 2.0 / 127.0))
        VOP(lambda e: e.tensor_mul(sb_red[:, 11:12], sb_tv[:, 7:8], sb_red[:, 10:11]))
        VOP(lambda e: e.tensor_mul(sb_tv[:, 2:3], sb_tv[:, 1:2], sb_red[:, 11:12]))
        VOP(lambda e: e.tensor_mul(sb_red[:, 9:10], sb_tv[:, 0:1], sb_tv[:, 1:2]))
        VOP(lambda e: e.tensor_add(sb_red[:, 9:10], sb_red[:, 9:10], sb_tv[:, 6:7]))
        VOP(lambda e: e.tensor_mul(sb_tv[:, 3:4], sb_red[:, 9:10], sb_red[:, 11:12]))
        op("tensor",
           lambda e: e.transpose(ps[6][0:8, 0:128], sb_tv[:, 0:8], sb_idp[:, :]),
           waits=[(sv, sv.n), (sgp, gp_idp), (ss, ss.n)], inc=stt)
        SOP(lambda e: e.copy(sb_tvT[0:8, :], ps[6][0:8, 0:128]), waits=[(stt, stt.n)])
        DMA("gpsimd", scr_tv[0:5, tt * 128:(tt + 1) * 128], sb_tvT[0:5, :],
            s_scr, waits=[(ss, ss.n)])
    v_stats = sv.n
    scr_stats = s_scr.n

    op("gpsimd",
       lambda e: e.collective_compute("AllGather", ALU.bypass,
                                      replica_groups=[list(range(C))],
                                      ins=[scr_tv[0:2, :].opt()], outs=[agb_out.opt()]),
       waits=[(s_scr, scr_stats)], inc=scc_b, k=1)
    cc_agb = scc_b.n

    for dst, r in [(sb_mb, 0), (sb_rb, 1), (sb_Ab, 2), (sb_Bb, 3), (sb_iv2, 4)]:
        DMA("gpsimd", dst[:, :], bass.AP(scr_tv_t, r * T, [[0, 128], [1, T]]),
            s_bco, waits=[(s_scr, scr_stats)])
    bco = s_bco.n

    if stop_after == "agb":
        _finish_early(prog, ctrs, out_tok, sb_sq, S, op, T, D, ndc)
        return _instantiate(nc, prog)

    for dc in range(ndc):
        DMA("sync", sb_sq[:, 0:160], xpwT[dc * 128:(dc + 1) * 128, :], s_x3,
            waits=[(sv, sv.n)])
        VOP(lambda e, dc=dc: e.tensor_copy(sb_xpw[:, dc * 160:(dc + 1) * 160], sb_sq[:, 0:160]),
            waits=[(s_x3, s_x3.n)])
    v_xpw = sv.n

    nxpj = T // NB
    xp_mm_dc = []
    p0c_user = [0, 0]
    for dc in range(ndc):
        par = dc % 2
        DMA("sync", sb_xod2[par][:, :], x_ownT[dc * 128:(dc + 1) * 128, :], s_p3[par],
            waits=[(sv, p0c_user[par])] if dc > 1 else [(sv, v_xpw)])
        VOP(lambda e, par=par: e.tensor_sub(sb_xnTf[:, :], sb_xod2[par][:, :], sb_mb[:, :]),
            waits=[(s_p3[par], s_p3[par].n), (s_bco, bco)])
        VOP(lambda e, par=par: e.tensor_mul(sb_xnb[par][:, :], sb_xnTf[:, :], sb_rb[:, :]),
            waits=([(stt, xp_mm_dc[dc - 2])] if dc >= 2 else []))
        v_xnb = sv.n
        for j in range(nxpj):
            op("tensor",
               lambda e, dc=dc, j=j, par=par: e.matmul(
                   ps[j][:, 0:NB], sb_xpw[:, dc * 160:dc * 160 + 128],
                   sb_xnb[par][:, j * NB:(j + 1) * NB],
                   start=(dc == 0), stop=(dc == ndc - 1)),
               waits=[(sv, v_xnb), (sv, v_xpw)] if j == 0 else ())
        for j in range(nxpj):
            op("tensor",
               lambda e, dc=dc, j=j, par=par: e.matmul(
                   ps[2 + j][0:32, 0:NB], sb_xpw[:, dc * 160 + 128:dc * 160 + 160],
                   sb_xnb[par][:, j * NB:(j + 1) * NB],
                   start=(dc == 0), stop=(dc == ndc - 1)),
               inc=(stt if j == nxpj - 1 else None))
        xp_mm_dc.append(stt.n)
        VOP(lambda e, par=par: e.tensor_mul(sb_xnTf[:, :], sb_xod2[par][:, :], sb_Ab[:, :]))
        VOP(lambda e: e.tensor_sub(sb_xnTf[:, :], sb_xnTf[:, :], sb_Bb[:, :]))
        VOP(lambda e: e.tensor_scalar(sb_f16[:, :], sb_xnTf[:, :], 1536.0, None, ALU.add))
        VOP(lambda e, dc=dc: e.tensor_scalar(sb_kT[:, dc * T:(dc + 1) * T], sb_f16[:, :],
                                             1536.0, None, ALU.subtract))
        p0c_user[par] = sv.n
    v_kT = sv.n
    mm_xp = stt.n
    SOP(lambda e: e.activation(sb_ag1[:, 0:NB], ps[0][:, 0:NB],
                               AF.Identity, bias=0.0, scale=1.0), waits=[(stt, mm_xp), (sv, v_kT)])
    if nxpj > 1:
        SOP(lambda e: e.activation(sb_ag1[:, NB:2 * NB], ps[1][:, 0:NB],
                                   AF.Identity, bias=0.0, scale=1.0))
    SOP(lambda e: e.activation(sb_ag2[0:32, 0:NB], ps[2][0:32, 0:NB],
                               AF.Identity, bias=0.0, scale=1.0))
    if nxpj > 1:
        SOP(lambda e: e.activation(sb_ag2[0:32, NB:2 * NB], ps[3][0:32, 0:NB],
                                   AF.Identity, bias=0.0, scale=1.0))
    ss_aga = ss.n
    DMA("gpsimd", aga_in[0:128, :], sb_ag1[:, :], s_scr, waits=[(ss, ss_aga)])
    DMA("gpsimd", aga_in[128:160, :], sb_ag2[0:32, :], s_scr)
    op("gpsimd",
       lambda e: e.collective_compute("AllGather", ALU.bypass,
                                      replica_groups=[list(range(C))],
                                      ins=[aga_in.opt()], outs=[aga_out.opt()]),
       waits=[(s_scr, s_scr.n)], inc=scc_a, k=1)
    cc_aga = scc_a.n

    if stop_after == "kx":
        _finish_early(prog, ctrs, out_tok, sb_sq, S, op, T, D, ndc)
        return _instantiate(nc, prog)

    dt_relu_ms = [0, 0]
    NJ = T // NB
    W2 = NJ * NB
    for dt in range(nds):
        for blk in range(C):
            it = dt * C + blk
            pb = ps2b[1]
            DMA("sync", sb_dlow[:, 0:T], aga_out[blk * 160:blk * 160 + 128, :], s_ag,
                waits=[(scc_a, cc_aga)] + ([(stt, stt.n)] if (dt or blk) else []))
            for j in range(NJ):
                op("tensor",
                   lambda e, dt=dt, j=j, pb=pb: e.matmul(pb[:, j * NB:(j + 1) * NB],
                                                         sb_dtw[0:R, dt * 128:(dt + 1) * 128],
                                                         sb_dlow[0:R, j * NB:(j + 1) * NB],
                                                         start=True, stop=True),
                   waits=([(s_ag, s_ag.n), (sv, v_dtw), (ss, dt_relu_ms[0]), (sv, sv.n)]
                          if j == 0 else []),
                   inc=(stt if j == NJ - 1 else None))
            mm_dt = stt.n
            for j in range(NJ):
                SOP(lambda e, j=j, pb=pb: e.activation(sb_sp2[:, j * NB:(j + 1) * NB],
                                                       pb[:, j * NB:(j + 1) * NB],
                                                       AF.Identity, bias=0.0, scale=1.0),
                    waits=[(stt, mm_dt)] if j == 0 else [])
            dt_relu_ms[0] = ss.n
            SOP(lambda e, dt=dt: e.activation(sb_sp1[:, 0:W2], sb_sp2[:, 0:W2], AF.Abs,
                                              bias=sb_dtbc[:, dt:dt + 1], scale=1.0))
            SOP(lambda e: e.activation(sb_sp1[:, 0:W2], sb_sp1[:, 0:W2], AF.Exp,
                                       bias=0.0, scale=-1.0))
            SOP(lambda e: e.activation(sb_sp1[:, 0:W2], sb_sp1[:, 0:W2], AF.Ln,
                                       bias=sb_cst[:, 2:3], scale=1.0))
            SOP(lambda e, dt=dt: e.activation(sb_sp3[:, 0:W2], sb_sp2[:, 0:W2], AF.Relu,
                                              bias=sb_dtbc[:, dt:dt + 1], scale=1.0),
                waits=[(s_dts, s_dts.n)])
            VOP(lambda e: e.tensor_add(sb_sp3[:, 0:W2], sb_sp3[:, 0:W2], sb_sp1[:, 0:W2]),
                waits=[(ss, ss.n), (s_dts, s_dts.n)])
            DMA("gpsimd", scr_dtT[dt * 128:(dt + 1) * 128, blk * T:(blk + 1) * T],
                sb_sp3[:, 0:W2], s_dts, waits=[(sv, sv.n)])
    dts_all = s_dts.n

    scan_chunk_ready = {}
    bc_last_use = [0, 0, 0]
    bc_loaded = {}

    def emit_scan_bc(dt, ch, n):
        ring = n % 3
        blk0 = (ch * TC) // T
        w = [(scc_a, cc_aga)] if (dt, ch, n) == (0, 0, 0) else [(sv, bc_last_use[ring])]
        row_b = (blk0 * 160 + 128 + n) * T
        row_c = (blk0 * 160 + 144 + n) * T
        DMA("sync", sb_bc[ring][:, 0:TC].rearrange("p (b t) -> p b t", b=nblk_c),
            bass.AP(aga_out_t, row_b, [[0, 128], [160 * T, nblk_c], [1, T]]),
            s_bcs[ring], waits=w)
        DMA("sync", sb_bc[ring][:, TC:2 * TC].rearrange("p (b t) -> p b t", b=nblk_c),
            bass.AP(aga_out_t, row_c, [[0, 128], [160 * T, nblk_c], [1, T]]),
            s_bcs[ring])
        bc_loaded[(dt, ch, n)] = s_bcs[ring].n

    def emit_scan_chunk_prep(dt, ch):
        DMA("gpsimd", sb_dtT[:, :], scr_dtT[dt * 128:(dt + 1) * 128, ch * TC:(ch + 1) * TC],
            s_dtl, waits=[(s_dts, dts_all), (sv, sv.n), (ss, ss.n)])
        DMA("gpsimd", sb_xdc[:, :], x_dT[dt * 128:(dt + 1) * 128, ch * TC:(ch + 1) * TC], s_dtl)
        blk0 = (ch * TC) // T
        for b in range(nblk_c):
            DMA("gpsimd", sb_mab[:, b * T:(b + 1) * T],
                bass.AP(agb_out_t, (2 * (blk0 + b)) * T, [[0, 128], [1, T]]), s_mrb,
                waits=[(scc_b, cc_agb), (sv, sv.n)] if b == 0 else [])
            DMA("gpsimd", sb_rab[:, b * T:(b + 1) * T],
                bass.AP(agb_out_t, (2 * (blk0 + b) + 1) * T, [[0, 128], [1, T]]), s_mrb)
        VOP(lambda e: e.tensor_sub(sb_xdc[:, :], sb_xdc[:, :], sb_mab[:, :]),
            waits=[(s_dtl, s_dtl.n), (s_mrb, s_mrb.n)])
        VOP(lambda e: e.tensor_mul(sb_xdc[:, :], sb_xdc[:, :], sb_rab[:, :]))
        VOP(lambda e: e.tensor_mul(sb_dtx[:, :], sb_dtT[:, :], sb_xdc[:, :]))
        scan_chunk_ready[(dt, ch)] = sv.n
        emit_scan_bc(dt, ch, 0)
        emit_scan_bc(dt, ch, 1)
        emit_scan_exp(dt, ch, 0)
        emit_scan_exp(dt, ch, 1)

    scan_exp_ms = {}
    scan_done_ms = {}

    def emit_scan_exp(dt, ch, n):
        slot = n % 3
        w = [(sv, scan_chunk_ready[(dt, ch)])]
        prev = (dt, ch, n - 3)
        if prev in scan_done_ms:
            w.append((sv, scan_done_ms[prev]))
        SOP(lambda e, dt=dt, n=n, slot=slot: e.activation(
            sb_ab3[slot][:, :], sb_dtT[:, :], AF.Exp, bias=0.0,
            scale=sb_acol[:, dt * N + n:dt * N + n + 1]), waits=w, fifo=True)
        scan_exp_ms[(dt, ch, n)] = ss.n

    def emit_scan_group(dt, ch, n):
        ring = n % 3
        slot = n % 3
        ahead = n + 2
        if ahead < N:
            if (dt, ch, ahead) not in bc_loaded:
                emit_scan_bc(dt, ch, ahead)
            if (dt, ch, ahead) not in scan_exp_ms:
                emit_scan_exp(dt, ch, ahead)
        VOP(lambda e, ring=ring: e.tensor_mul(sb_bx[:, :], sb_dtx[:, :], sb_bc[ring][:, 0:TC]),
            waits=[(s_bcs[ring], bc_loaded[(dt, ch, n)])], fifo=True)
        VOP(lambda e, dt=dt, n=n, slot=slot: e.tensor_tensor_scan(
            sb_h[:, :], sb_ab3[slot][:, :], sb_bx[:, :],
            sb_hst[:, dt * N + n:dt * N + n + 1], ALU.mult, ALU.add),
            waits=[(ss, scan_exp_ms[(dt, ch, n)]), (sgp, gp_hst0)], fifo=True)
        scan_done_ms[(dt, ch, n)] = sv.n
        VOP(lambda e, dt=dt, n=n: e.tensor_copy(sb_hst[:, dt * N + n:dt * N + n + 1],
                                                sb_h[:, TC - 1:TC]), fifo=True)
        VOP(lambda e, ring=ring: e.tensor_mul(sb_ytmp[:, :], sb_h[:, :], sb_bc[ring][:, TC:2 * TC]),
            fifo=True)
        if n == 0:
            VOP(lambda e: e.tensor_copy(sb_yacc[:, :], sb_ytmp[:, :]),
                waits=[(s_y, s_y.n)], fifo=True)
        else:
            VOP(lambda e: e.tensor_add(sb_yacc[:, :], sb_yacc[:, :], sb_ytmp[:, :]), fifo=True)
        bc_last_use[ring] = sv.n

    def emit_scan_chunk_finish(dt, ch):
        blk0 = (ch * TC) // T
        for b in range(nblk_c):
            DMA("sync", y_in[(blk0 + b) * ds + dt * 128:(blk0 + b) * ds + (dt + 1) * 128, :],
                sb_yacc[:, b * T:(b + 1) * T], s_y, waits=[(sv, sv.n)] if b == 0 else [])

    scan_steps = []
    for dt in range(nds):
        for ch in range(nch):
            scan_steps.append(("prep", dt, ch, 0))
            for n in range(N):
                scan_steps.append(("group", dt, ch, n))
            scan_steps.append(("finish", dt, ch, 0))
    scan_pos = [0]

    def emit_some_scan(k):
        if stop_after == "upnoscan":
            return
        done = 0
        while done < k and scan_pos[0] < len(scan_steps):
            kind, dt, ch, n = scan_steps[scan_pos[0]]
            if kind == "prep":
                emit_scan_chunk_prep(dt, ch)
            elif kind == "group":
                emit_scan_group(dt, ch, n)
            else:
                emit_scan_chunk_finish(dt, ch)
            scan_pos[0] += 1
            done += 1

    total_up_slots = nhr * nht_r * ntb
    scan_per_slot = max(1, -(-len(scan_steps) // total_up_slots))

    if stop_after == "p0":
        _finish_early(prog, ctrs, out_tok, sb_sq, S, op, T, D, ndc)
        return _instantiate(nc, prog)

    up_psum_free = {0: 0, 1: 0, 2: 0, 3: 0}
    up_hr_ms = []
    q_dma_ms = [0, 0]
    ev_user_ms = [0, 0, 0, 0]
    for hr in range(nhr):
        spar = hr % 2
        if hr < 2:
            sgn_done_v = up_sgn_v[hr]
            for ht in [None]:
                pass
        for gm, (wsrc_t, sgl) in enumerate([(wgT_t, sb_sgn_g), (wuT_t, sb_sgn_u)]) if hr >= 2 else []:
            sgt = sgl[spar]
            w = [(sv, up_sgn_v[spar])]
            if hr >= 2 and gm == 0:
                w.append((stt, up_hr_ms[hr - 2]))
            src_ap = bass.AP(wsrc_t, hr * HRS, [[H, 128], [128 * H, ndc], [1, HRS]])
            dst_ap = sgt[:, :].rearrange("p (a f) -> p a f", a=ndc)
            DMA("gpsimd", dst_ap, src_ap, s_wf[gm], waits=w)
            VOP(lambda e, sgt=sgt: e.tensor_scalar(sgt[:, :], sgt[:, :], 0.0, 0.5,
                                                   ALU.is_ge, ALU.subtract),
                waits=[(s_wf[gm], s_wf[gm].n)])
        if hr >= 2:
            up_sgn_v[spar] = sv.n
            sgn_done_v = sv.n
        else:
            sgn_done_v = up_sgn_v[hr]
        for ht in range(nht_r):
            for tb in range(ntb):
                hg = hr * nht_r + ht
                slot = (ht * ntb + tb) % 4
                gb, ub = slot, 4 + slot
                tw = [(sv, sgn_done_v), (sv, v_kT), (ss, up_psum_free[slot])]
                for dc in range(ndc):
                    op("tensor",
                       lambda e, dc=dc, ht=ht, tb=tb, gb=gb, spar=spar: e.matmul(
                           ps[gb][:, 0:NB], sb_sgn_g[spar][:, dc * HRS + ht * 128:dc * HRS + ht * 128 + 128],
                           sb_kT[:, dc * T + tb * NB:dc * T + (tb + 1) * NB],
                           start=(dc == 0), stop=(dc == ndc - 1)),
                       waits=tw if dc == 0 else ())
                for dc in range(ndc):
                    last = dc == ndc - 1
                    op("tensor",
                       lambda e, dc=dc, ht=ht, tb=tb, ub=ub, last=last, spar=spar: e.matmul(
                           ps[ub][:, 0:NB], sb_sgn_u[spar][:, dc * HRS + ht * 128:dc * HRS + ht * 128 + 128],
                           sb_kT[:, dc * T + tb * NB:dc * T + (tb + 1) * NB],
                           start=(dc == 0), stop=last),
                       inc=(stt if last else None))
                mm_up = stt.n
                emit_some_scan(scan_per_slot)
                SOP(lambda e, gb=gb, slot=slot: e.activation(sb_evg[slot][:, :], ps[gb][:, 0:NB],
                                                             AF.Identity, bias=0.0, scale=1.0),
                    waits=[(stt, mm_up), (sv, ev_user_ms[slot])])
                SOP(lambda e, ub=ub, slot=slot: e.activation(sb_evu[slot][:, :], ps[ub][:, 0:NB],
                                                             AF.Identity, bias=0.0, scale=1.0))
                up_psum_free[slot] = ss.n
                ev_ss = ss.n
                VOP(lambda e, tb=tb, slot=slot: e.tensor_tensor(sb_ev1[:, :], sb_evg[slot][:, :],
                                                                sb_iv2[:, tb * NB:(tb + 1) * NB], ALU.mult),
                    waits=[(ss, ev_ss), (s_bco, bco)])
                SOP(lambda e: e.activation(sb_ev2[:, :], sb_ev1[:, :], AF.Tanh, bias=0.0, scale=0.5),
                    waits=[(sv, sv.n)])
                VOP(lambda e: e.tensor_scalar(sb_ev3[:, :], sb_ev2[:, :], 0.5, 0.5, ALU.mult, ALU.add),
                    waits=[(ss, ss.n)])
                VOP(lambda e: e.tensor_mul(sb_ev4[:, :], sb_ev1[:, :], sb_ev3[:, :]))
                qp = slot % 2
                VOP(lambda e, qp=qp, slot=slot: e.tensor_mul(sb_q[qp][:, :], sb_ev4[:, :], sb_evu[slot][:, :]),
                    waits=[(s_p[qp], q_dma_ms[qp])])
                ev_user_ms[slot] = sv.n
                VOP(lambda e, tb=tb, qp=qp: e.tensor_mul(sb_q[qp][:, :], sb_q[qp][:, :],
                                                         sb_iv2[:, tb * NB:(tb + 1) * NB]))
                DMA("sync", p_scr[hg * 128:(hg + 1) * 128, tb * NB:(tb + 1) * NB], sb_q[qp][:, :],
                    s_p[qp], waits=[(sv, sv.n)])
                q_dma_ms[qp] = s_p[qp].n
        up_hr_ms.append(stt.n)
    emit_some_scan(len(scan_steps))
    p_all = (s_p[0].n, s_p[1].n)
    y_all = s_y.n

    if stop_after == "up":
        _finish_early(prog, ctrs, out_tok, sb_sq, S, op, T, D, ndc)
        return _instantiate(nc, prog)
    if stop_after == "upnoscan":
        _finish_early(prog, ctrs, out_tok, sb_sq, S, op, T, D, ndc)
        return _instantiate(nc, prog)

    def cc_y_fn(e):
        with allow_ici_all_to_all():
            return e.collective_compute("AllToAll", ALU.bypass,
                                        replica_groups=[list(range(C))],
                                        ins=[y_in.opt()], outs=[y_out.opt()])
    op("gpsimd", cc_y_fn, waits=[(s_y, y_all)], inc=scc_y, k=1)
    cc_y = scc_y.n

    dn_ms = []
    dn_sgn_ms = [0, 0]
    for oq in range(noq):
        for h4 in range(nh // HH4):
            par4 = h4 % 2
            g4 = oq * (nh // HH4) + h4
            w4 = [(stt, dn_ms[2 * g4 - 3])] if g4 >= 2 else []
            DMA("gpsimd",
                sb_wds[par4][:, :].rearrange("p (a f) -> p a f", a=HH4),
                bass.AP(wdT_t, (h4 * HH4 * 128) * D + oq * OQ, [[D, 128], [128 * D, HH4], [1, OQ]]),
                s_wd[par4], waits=[(sv, dn_sgn_ms[par4])] + w4)
            VOP(lambda e, par4=par4: e.tensor_scalar(sb_wds[par4][:, :], sb_wds[par4][:, :],
                                                     0.0, 0.5, ALU.is_ge, ALU.subtract),
                waits=[(s_wd[par4], s_wd[par4].n)])
            dn_sgn_ms[par4] = sv.n
            sgn_d = sv.n
            for hh2 in range(HH4 // 2):
                hh = h4 * (HH4 // 2) + hh2
                par = hh % 2
                gi = oq * (nh // 2) + hh
                w_slot = [(stt, dn_ms[gi - 2])] if gi >= 2 else []
                DMA("sync",
                    sb_pt[par][:, :].rearrange("p (a f) -> p a f", a=2),
                    p_scr[hh * 256:(hh + 1) * 256, :].rearrange("(a p) f -> p a f", p=128),
                    s_pl[par],
                    waits=([(s_p[0], p_all[0]), (s_p[1], p_all[1])] if gi == 0 else []) + w_slot)
                for hi in range(2):
                    h = hh * 2 + hi
                    hq = hh2 * 2 + hi
                    for ot in range(no_t):
                        for tb in range(ntb):
                            bank = ot * ntb + tb
                            w = []
                            if hi == 0 and ot == 0 and tb == 0:
                                w = [(sv, sgn_d), (s_pl[par], s_pl[par].n)]
                                if h == 0 and oq > 0:
                                    w.append((ss, ss.n))
                            op("tensor",
                               lambda e, par=par, par4=par4, hq=hq, hi=hi, ot=ot, tb=tb, h=h, bank=bank: e.matmul(
                                   ps[bank][:, 0:NB],
                                   sb_wds[par4][:, hq * OQ + ot * 128:hq * OQ + (ot + 1) * 128],
                                   sb_pt[par][:, hi * T + tb * NB:hi * T + (tb + 1) * NB],
                                   start=(h == 0), stop=(h == nh - 1)),
                               waits=w,
                               inc=(stt if (hi == 1 and ot == no_t - 1 and tb == ntb - 1) else None))
                dn_ms.append(stt.n)
        mm_dn = stt.n
        for ot in range(no_t):
            for tb in range(ntb):
                bank = ot * ntb + tb
                oc = oq * no_t + ot
                fpar = bank % 2
                SOP(lambda e, bank=bank, fpar=fpar: e.activation(
                    sb_fs[fpar][:, :], ps[bank][:, 0:NB],
                    AF.Identity, bias=0.0, scale=2.0),
                    waits=[(stt, mm_dn), (s_fsp[fpar], fs_slot_ms[fpar])])
                DMA("gpsimd", ffn_scr[oc * 128:(oc + 1) * 128, tb * NB:(tb + 1) * NB],
                    sb_fs[fpar][:, :], s_fsp[fpar], waits=[(ss, ss.n)])
                fs_slot_ms[fpar] = s_fsp[fpar].n
    ss_dn = ss.n

    if stop_after == "down":
        _finish_early(prog, ctrs, out_tok, sb_sq, S, op, T, D, ndc)
        return _instantiate(nc, prog)

    GOP(lambda e: e.memset(sb_ident[:, :], 1.0), waits=[(sv, sv.n), (ss, ss.n)])
    GOP(lambda e: e.affine_select(sb_ident[:, :], sb_ident[:, :], [[1, 128]],
                                  ALU.is_equal, 0.0, base=0, channel_multiplier=-1),
        waits=[(sgp, sgp.n)])
    gp_ident = sgp.n
    out_slot_ms = [0, 0]
    acc_tr_ms = [0, 0]
    for dc in range(ndc):
        par = dc % 2
        w0 = [(sv, p3_user_ms[par])]
        if dc == 0:
            w0 += [(sv, sv.n), (ss, ss.n)]
        DMA("sync", sb_xod2[par][:, :], x_ownT[dc * 128:(dc + 1) * 128, :], s_p3[par],
            waits=([(sv, sv.n), (ss, ss.n)] if dc == 0 else [(sv, p3_user_ms[par])]))
        DMA("sync", sb_ycol[par][:, :], y_out[dc * 128:(dc + 1) * 128, :], s_p3[par],
            waits=([(scc_y, cc_y), (ss, ss.n)] if dc == 0 else []))
        DMA("sync", sb_fload[par][:, :], ffn_scr[dc * 128:(dc + 1) * 128, :], s_p3[par],
            waits=([(s_fsp[0], s_fsp[0].n), (s_fsp[1], s_fsp[1].n)] if dc == 0 else []))
        p3n = s_p3[par].n
        VOP(lambda e, par=par: e.tensor_sub(sb_xnTf2[par][:, :], sb_xod2[par][:, :], sb_mb[:, :]),
            waits=([(stt, stt.n)] if dc == 0 else []) + [(s_p3[par], p3n)])
        VOP(lambda e, par=par: e.tensor_mul(sb_xnTf2[par][:, :], sb_xnTf2[par][:, :], sb_rb[:, :]))
        VOP(lambda e, dc=dc, par=par: e.scalar_tensor_tensor(sb_acc[par][:, :], sb_xnTf2[par][:, :],
                                                             sb_dpb[:, dc:dc + 1], sb_xod2[par][:, :],
                                                             ALU.mult, ALU.add),
            waits=[(stt, acc_tr_ms[par])])
        VOP(lambda e, par=par: e.tensor_add(sb_acc[par][:, :], sb_acc[par][:, :], sb_ycol[par][:, :]))
        VOP(lambda e, par=par: e.tensor_add(sb_acc[par][:, :], sb_acc[par][:, :], sb_fload[par][:, :]),
            waits=([(ss, ss_dn)] if dc == 0 else []))
        acc_done = sv.n
        p3_user_ms[par] = sv.n
        for tt in range(ntt):
            pslot = tt % 2
            op("tensor",
               lambda e, tt=tt, pslot=pslot, par=par: e.transpose(ps[pslot][:, 0:128],
                                                                  sb_acc[par][:, tt * 128:(tt + 1) * 128],
                                                                  sb_ident[:, :]),
               waits=[(sv, acc_done), (sgp, gp_ident)] +
                     ([(ss, ss.n)] if (dc * ntt + tt) >= 2 else []),
               inc=stt)
            tr_done = stt.n
            SOP(lambda e, pslot=pslot: e.copy(sb_ot[pslot][:, :], ps[pslot][:, 0:128]),
                waits=[(stt, tr_done), (s_out, out_slot_ms[pslot])])
            DMA("gpsimd", out_tok[tt * 128:(tt + 1) * 128, dc * 128:(dc + 1) * 128],
                sb_ot[pslot][:, :], s_out, waits=[(ss, ss.n)])
            out_slot_ms[pslot] = s_out.n
        acc_tr_ms[par] = stt.n
    prog["sync"].append(lambda eng, h=s_out.h, v=s_out.n: eng.wait_ge(h, v))
    prog["gpsimd"].append(lambda eng, h=s_out.h, v=s_out.n: eng.wait_ge(h, v))

    return _instantiate(nc, prog)


def _finish_early(prog, ctrs, out_tok, sb_src, S, op, T, D, ndc):
    s_out = ctrs["s_out"]
    sv = ctrs["sv"]
    ss = ctrs["ss"]
    W = min(D, sb_src.shape[1])
    for tt in range(T // 128):
        for c0 in range(0, D, W):
            op("sync", lambda eng, tt=tt, c0=c0: eng.dma_start(
                out=out_tok[tt * 128:(tt + 1) * 128, c0:c0 + W],
                in_=sb_src[:, 0:W]), waits=[(sv, sv.n), (ss, ss.n)] if (tt == 0 and c0 == 0) else [],
                inc=s_out, k=16)
    prog["sync"].append(lambda eng, h=s_out.h, v=s_out.n: eng.wait_ge(h, v))
    prog["gpsimd"].append(lambda eng, h=s_out.h, v=s_out.n: eng.wait_ge(h, v))


def _instantiate(nc, prog):
    with nc.Block() as block:
        @block.sync
        def _(eng):
            for fn in prog["sync"]:
                fn(eng)

        @block.gpsimd
        def _(eng):
            for fn in prog["gpsimd"]:
                fn(eng)

        @block.tensor
        def _(eng):
            for fn in prog["tensor"]:
                fn(eng)

        @block.vector
        def _(eng):
            for fn in prog["vector"]:
                fn(eng)

        @block.scalar
        def _(eng):
            for fn in prog["scalar"]:
                fn(eng)

    return nc


def make_in_maps(cfg, inputs):
    D, H, L, N, R, C = cfg["D"], cfg["H"], cfg["L"], cfg["N"], cfg["R"], cfg["C"]
    T = L // C
    ds = D // C
    x = np.ascontiguousarray(np.asarray(inputs["L0_attn_x"], dtype=np.float32)[0])
    xpw = np.ascontiguousarray(np.asarray(inputs["L0_attn_x_proj_w"], np.float32).T)
    dtpw = np.ascontiguousarray(np.asarray(inputs["L0_attn_dt_proj_w"], np.float32).T)
    dtbv = np.asarray(inputs["L0_attn_dt_proj_b"], np.float32)
    al = np.asarray(inputs["L0_attn_A_log"], np.float32)
    dpv = np.ascontiguousarray(np.asarray(inputs["L0_attn_D_param"], np.float32).reshape(D // 128, 128).T)
    wg = np.ascontiguousarray(np.asarray(inputs["L0_ffn_W_g"], np.float32).T)
    wu = np.ascontiguousarray(np.asarray(inputs["L0_ffn_W_u"], np.float32).T)
    wd = np.ascontiguousarray(np.asarray(inputs["L0_ffn_W_d"], np.float32).T)
    xT = np.ascontiguousarray(x.T)
    in_maps = []
    for c in range(C):
        in_maps.append({
            "x_tok": np.ascontiguousarray(x[c * T:(c + 1) * T, :]),
            "x_ownT": np.ascontiguousarray(xT[:, c * T:(c + 1) * T]),
            "x_dT": np.ascontiguousarray(xT[c * ds:(c + 1) * ds, :]),
            "xpwT": xpw,
            "dtpwT": np.ascontiguousarray(dtpw[:, c * ds:(c + 1) * ds]),
            "dtb": np.ascontiguousarray(dtbv[c * ds:(c + 1) * ds].reshape(ds // 128, 128).T),
            "alog": np.ascontiguousarray(
                al[c * ds:(c + 1) * ds, :].reshape(ds // 128, 128, N).transpose(1, 0, 2).reshape(128, -1)),
            "dp": dpv,
            "wgT": wg,
            "wuT": wu,
            "wdT": wd,
        })
    return in_maps


def kernel(**inputs):
    cfg = FULL_CFG
    nc = build_nc(cfg)
    in_maps = make_in_maps(cfg, inputs)
    res = run_bass_kernel_spmd(nc, in_maps, core_ids=list(range(cfg["C"])))
    out = np.concatenate([np.asarray(r["out_tok"]) for r in res.results], axis=0)
    return out[None].astype(np.float32)
